# version 08
# baseline (speedup 1.0000x reference)
import numpy as np
from contextlib import ExitStack
import concourse.bass as bass
import concourse.mybir as mybir
from concourse.bass_utils import run_bass_kernel_spmd

F32 = mybir.dt.float32
AF = mybir.ActivationFunctionType
ALU = mybir.AluOpType
AXX = mybir.AxisListType.X

D, N, Q, TOK = 1024, 64, 64, 2048
NT = TOK // 128
NCH = TOK // Q
DB = 32
NB = D // DB
NDP = DB // 2
EPS = 1e-5
NE, IC = 16, 512

TB_ARGE, TB_CTIL, TB_CTAU = 0, 1536, 2560
TB_GRD0, TB_BTIL, TB_GRD1 = 3584, 3648, 4672
TB_BTT, TB_MZ = 4736, 6912
R3_COLS = 8960
F_WE1, F_WE3 = 0, 4096
F_WE2 = 12288


class _RecInstr:
    __slots__ = ("post", "auto")

    def __init__(self, auto=None):
        self.post = []
        self.auto = auto

    def then_inc(self, sem, amt):
        self.post.append((sem, amt))
        return self


_OWN = {"v": "sV", "s": "sS", "g": "sP"}


class _RecEng:

    def __init__(self, prog, key):
        self._prog = prog
        self._key = key

    def __getattr__(self, attr):
        prog, key = self._prog, self._key
        own = _OWN.get(key)

        def call(*a, **kw):
            tape = prog.tape[key]
            if own is not None and attr not in ("wait_ge", "dma_start"):
                tape.append(("wait_ge", (prog.sem[own], prog.cnt[own]), {},
                             _RecInstr()))
                ri = _RecInstr(auto=own)
                ri.post.append((prog.sem[own], 1))
                prog.cnt[own] += 1
                tape.append((attr, a, kw, ri))
                return ri
            ri = _RecInstr()
            tape.append((attr, a, kw, ri))
            return ri
        return call


class P:
    def __init__(self, nc, block, stack):
        self.nc = nc
        self.block = block
        self.tape = {k: [] for k in ("t", "v", "s", "g", "y")}
        self.sem, self.cnt = {}, {}
        for name in ("dC", "dX0", "dX1", "dX2", "dX3", "dWo", "dw1", "dw3",
                     "dw2", "dM", "dK", "dO", "sT", "sV", "sS", "sP"):
            self.sem[name] = stack.enter_context(nc.semaphore(name))
            self.cnt[name] = 0
        self.t = _RecEng(self, "t")
        self.v = _RecEng(self, "v")
        self.s = _RecEng(self, "s")
        self.g = _RecEng(self, "g")
        self.y = _RecEng(self, "y")

    def inc(self, instr, name, amt=1):
        if instr.auto == name:
            return self.cnt[name]
        self.cnt[name] += amt
        instr.then_inc(self.sem[name], amt)
        return self.cnt[name]

    def w(self, eng, name, val=None):
        vv = self.cnt[name] if val is None else val
        if vv > 0:
            eng.wait_ge(self.sem[name], vv)

    def dma(self, q, o, i, name="dC", **kw):
        return self.inc(q.dma_start(out=o, in_=i, **kw), name, 16)

    def finalize(self, frac=1.0):
        secs = {"t": self.block.tensor, "v": self.block.vector,
                "s": self.block.scalar, "g": self.block.gpsimd,
                "y": self.block.sync}

        def replay(tape):
            tape = tape[:int(len(tape) * frac)]
            def body(e):
                for attr, a, kw, ri in tape:
                    ins = getattr(e, attr)(*a, **kw)
                    for sem, amt in ri.post:
                        ins.then_inc(sem, amt)
            return body

        for k, sec in secs.items():
            sec(replay(self.tape[k]))


def build_nc(dbg=False, frac=1.0, stop_after=None):
    nc = bass.Bass("TRN2", target_bir_lowering=False)
    ain = lambda n_, sh: nc.dram_tensor(n_, sh, F32, kind="ExternalInput").ap()
    xo = ain("xo", [TOK, D]); xg = ain("xg", [TOK, D])
    logA = ain("logA", [D, N]); Bp = ain("Bp", [D, N]); Cp = ain("Cp", [D, N])
    logdt = ain("logdt", [D])
    Wout = ain("Wout", [D, D]); bout = ain("bout", [D])
    gw = ain("gw", [NE, D])
    We1 = ain("We1", [NE, D, IC]); We2 = ain("We2", [NE, IC, D]); We3 = ain("We3", [NE, D, IC])
    be1 = ain("be1", [NE, IC]); be2 = ain("be2", [NE, D]); be3 = ain("be3", [NE, IC])
    ws1 = ain("ws1", [D, D]); ws2 = ain("ws2", [D, D]); ws3 = ain("ws3", [D, D])
    bs1 = ain("bs1", [D]); bs2 = ain("bs2", [D]); bs3 = ain("bs3", [D])
    eye = ain("eye", [128, 128]); LTZ = ain("LTZ", [128, 128])
    JT = ain("JT", [128, 192]); msk32 = ain("msk32", [128, 32])
    flagc = ain("flagc", [128, 1]); onesr = ain("onesr", [1, 128])
    kohc = ain("kohc", [16, 2048])
    gamb = ain("gamb", [128, D]); betb = ain("betb", [128, D])
    Dbc = ain("Dbc", [128, D]); gbb = ain("gbb", [128, NE])
    bcolh = ain("bcolh", [128, 160]); dtch = ain("dtch", [128, 8])
    out = nc.dram_tensor("out", [TOK, D], F32, kind="ExternalOutput").ap()
    x1sp = nc.dram_tensor("x1sp", [TOK, D], F32,
                          kind="ExternalOutput" if dbg else "Internal").ap()
    dbg_t = {}
    if dbg:
        for nm, sh in (("dbg_u", [128, 32768]), ("dbg_y", [128, 16384]),
                       ("dbg_cm", [128, NT * NE]), ("dbg_xf", [128, 16384])):
            dbg_t[nm] = nc.dram_tensor(nm, sh, F32, kind="ExternalOutput").ap()

    stack = ExitStack()
    with stack:
        sb = lambda nm, cols, rows=128: stack.enter_context(
            nc.sbuf_tensor(nm, [rows, cols], F32))
        R1 = sb("R1", 16384)
        R2 = sb("R2", 16384)
        R3 = sb("R3", R3_COLS)
        SCA = sb("SCA", 2048)
        SCB = sb("SCB", 2048)
        KE = sb("KE", 128); KL = sb("KL", 128); KJ = sb("KJ", 192)
        KM = sb("KM", 32); KF = sb("KF", 1)
        KGB = sb("KGB", NE)
        mdA = sb("mdA", 512); Cpk = sb("Cpk", 512); dBpk = sb("dBpk", 512)
        dAQ = sb("dAQ", 512)
        gwT = sb("gwT", 8 * NE)
        STA = sb("STA", 448)
        bcol = sb("bcol", 160)
        ctm = sb("ctm", NT * NE)
        gsc = sb("gsc", 1024)
        hsb = sb("hsb", 1024)
        CFM_C, BE2_C = 9216, 11264
        PS = [stack.enter_context(nc.psum_tensor("ps%d" % i, [128, 512], F32))
              for i in range(8)]
        block = stack.enter_context(nc.Block())
        p = P(nc, block, stack)
        t, v, s, g, y = p.t, p.v, p.s, p.g, p.y

        bankfree = [("sT", 0)] * 8

        def bank_wait(i):
            sem, val = bankfree[i]
            p.w(t, sem, val)

        S_SUM, S_SQ, S_MU, S_V1, S_R1, S_S, S_Q1, S_MR = 0, 32, 64, 96, 128, 160, 192, 224
        S_SUM3, S_MU3, S_R33 = 256, 272, 288
        S_T0, S_T1, S_T2, S_DT = 320, 352, 384, 416

        p.dma(y, KE[:, :], eye); p.dma(y, KL[:, :], LTZ); p.dma(y, KJ[:, :], JT)
        p.dma(y, KM[:, :], msk32); p.dma(y, KF[:, :], flagc)
        p.dma(y, hsb[:, :], betb)
        p.dma(y, KGB[:, :], gbb)
        p.dma(y, bcol[:, :], bcolh)
        PRE_G, PRE_A, PRE_C, PRE_B = 0, 64, 576, 1088
        for c in range(8):
            p.dma(y, SCB[:, PRE_A + c * 64:PRE_A + c * 64 + 64], logA[c * 128:(c + 1) * 128, :])
            p.dma(y, SCB[:, PRE_C + c * 64:PRE_C + c * 64 + 64], Cp[c * 128:(c + 1) * 128, :])
            p.dma(y, SCB[:, PRE_B + c * 64:PRE_B + c * 64 + 64], Bp[c * 128:(c + 1) * 128, :])
        p.dma(y, STA[:, S_DT:S_DT + 8], dtch)
        p.dma(y, gsc[0:16, :], gw)
        cW_pre = p.cnt["dC"]

        p.w(v, "dC", cW_pre)
        v.memset(SCB[:, PRE_G:PRE_G + 64], 0.0)
        p.w(s, "dC", cW_pre)
        s.activation(STA[:, S_DT:S_DT + 8], STA[:, S_DT:S_DT + 8], AF.Exp)
        sx = p.inc(s.activation(SCB[:, PRE_A:PRE_A + 512], SCB[:, PRE_A:PRE_A + 512],
                                AF.Exp), "sS")
        p.w(v, "sS", sx)
        for c in range(8):
            v.tensor_scalar(SCB[:, PRE_A + c * 64:PRE_A + (c + 1) * 64],
                            SCB[:, PRE_A + c * 64:PRE_A + (c + 1) * 64],
                            STA[:, S_DT + c:S_DT + c + 1], -1.0, ALU.mult, ALU.mult)
            lastv = v.tensor_scalar_mul(SCB[:, PRE_B + c * 64:PRE_B + (c + 1) * 64],
                                        SCB[:, PRE_B + c * 64:PRE_B + (c + 1) * 64],
                                        STA[:, S_DT + c:S_DT + c + 1])
        vPre = p.inc(lastv, "sV")

        p.w(t, "sV", vPre)
        jobs = [(base, dst, c) for c in range(8)
                for base, dst in ((PRE_A, mdA), (PRE_C, Cpk), (PRE_B, dBpk))]
        ev_hist = []
        for ji, (base, dst, c) in enumerate(jobs):
            bank = PS[ji % 2]
            if ji >= 2:
                p.w(t, "sV", ev_hist[ji - 2])
            off = base + c * 64
            t.matmul(bank[0:64, 0:128], SCB[:, off:off + 64], KE[:, :],
                     is_transpose=True, start=True, stop=False, skip_group_check=True)
            ins = t.matmul(bank[:, 128:256],
                           bass.AP(SCB, off - 64, [[2048, 128], [1, 128]]), KE[:, :],
                           is_transpose=True, start=True, stop=True, skip_group_check=True)
            tv = p.inc(ins, "sT")
            p.w(v, "sT", tv)
            v.tensor_copy(dst[0:64, c * 64:(c + 1) * 64],
                          bass.AP(bank, 0, [[512, 64], [2, 64]]))
            ev = p.inc(v.tensor_copy(dst[64:128, c * 64:(c + 1) * 64],
                                     bass.AP(bank, 64 * 512 + 128 + 1, [[512, 64], [2, 64]])),
                       "sV")
            ev_hist.append(ev)
            bankfree[ji % 2] = ("sV", ev)
        vPk = p.cnt["sV"]
        p.w(v, "sV", vPk)
        vq = p.inc(v.tensor_scalar_mul(dAQ[:, :], mdA[:, :], 64.0), "sV")
        p.w(s, "sV", vq)
        sdq = p.inc(s.activation(dAQ[:, :], dAQ[:, :], AF.Exp), "sS")

        p.w(t, "dC", cW_pre)
        bank_wait(2)
        for c in range(8):
            lastt = t.matmul(PS[2][:, c * 16:(c + 1) * 16], gsc[0:16, c * 128:(c + 1) * 128],
                             KE[0:16, 0:16], is_transpose=True, start=(c == 0), stop=(c == 7),
                             skip_group_check=True)
        tgw = p.inc(lastt, "sT")
        p.w(v, "sT", tgw)
        vgw = p.inc(v.tensor_copy(gwT[:, :], PS[2][:, 0:128]), "sV")
        bankfree[2] = ("sV", vgw)
        p.w(y, "sV", vgw)
        dGam = p.dma(y, gsc[:, 0:1024], gamb, "dM")
        tpk_done = p.cnt["sT"]

        def x_src(j):
            src = xo if j < NT else xg
            jj = j % NT
            return src[jj * 128:(jj + 1) * 128, :]

        sS0, sV0 = p.cnt["sS"], p.cnt["sV"]
        ubufs = [(SCA, 0), (SCA, 1024), (SCB, 0), (SCB, 1024)]
        xd = {}
        vcons = {}
        for g0 in range(0, 2 * NT, 2):
            for k in range(2):
                j = g0 + k
                ten, off = ubufs[j % 4]
                q = y if j % 2 == 0 else s
                if j < 4 and j % 4 >= 2:
                    p.w(q, "sT", tpk_done)
                if j >= 4:
                    p.w(q, "sV", vcons[j - 4])
                xd[j] = p.dma(q, ten[:, off:off + 1024], x_src(j),
                              "dX%d" % (j % 4))
            for k in range(2):
                j = g0 + k
                ten, off = ubufs[j % 4]
                buf = ten[:, off:off + 1024]
                p.w(s, "dX%d" % (j % 4), xd[j])
                p.inc(s.activation(R3[:, 0:1024], buf, AF.Square,
                                   accum_out=STA[:, S_SQ + j:S_SQ + j + 1]), "sS")
                p.inc(s.activation(R3[:, 0:1024], buf, AF.Copy,
                                   accum_out=STA[:, S_SUM + j:S_SUM + j + 1]), "sS")
            st4 = lambda c0_: STA[:, c0_ + g0:c0_ + g0 + 2]
            p.w(v, "sS", p.cnt["sS"])
            v.tensor_scalar_mul(st4(S_MU), st4(S_SUM), 1.0 / D)
            v.tensor_tensor(st4(S_V1), st4(S_MU), st4(S_MU), ALU.mult)
            v.tensor_scalar_mul(st4(S_T0), st4(S_SQ), 1.0 / D)
            v.tensor_tensor(st4(S_V1), st4(S_T0), st4(S_V1), ALU.subtract)
            v.tensor_scalar_add(st4(S_T0), st4(S_V1), EPS)
            vx = p.inc(v.reciprocal(st4(S_T1), st4(S_T0)), "sV")
            p.w(s, "sV", vx)
            sx = p.inc(s.activation(st4(S_R1), st4(S_T1), AF.Sqrt), "sS")
            p.w(v, "sS", sx)
            v.tensor_tensor(st4(S_T2), st4(S_R1), st4(S_R1), ALU.mult)
            v.tensor_tensor(st4(S_T2), st4(S_T0), st4(S_T2), ALU.mult)
            v.tensor_scalar(st4(S_T2), st4(S_T2), -0.5, 1.5, ALU.mult, ALU.add)
            v.tensor_tensor(st4(S_R1), st4(S_R1), st4(S_T2), ALU.mult)
            v.tensor_tensor(st4(S_T2), st4(S_V1), st4(S_T1), ALU.mult)
            v.tensor_scalar_add(st4(S_T2), st4(S_T2), EPS)
            vx = p.inc(v.reciprocal(st4(S_T1), st4(S_T2)), "sV")
            p.w(s, "sV", vx)
            sx = p.inc(s.activation(st4(S_T1), st4(S_T1), AF.Sqrt), "sS")
            p.w(v, "sS", sx)
            v.tensor_tensor(st4(S_T0), st4(S_T1), st4(S_T1), ALU.mult)
            v.tensor_tensor(st4(S_T0), st4(S_T2), st4(S_T0), ALU.mult)
            v.tensor_scalar(st4(S_T0), st4(S_T0), -0.5, 1.5, ALU.mult, ALU.add)
            v.tensor_tensor(st4(S_T1), st4(S_T1), st4(S_T0), ALU.mult)
            v.tensor_tensor(st4(S_S), st4(S_R1), st4(S_T1), ALU.mult)
            v.tensor_tensor(st4(S_MR), st4(S_MU), st4(S_R1), ALU.mult)
            p.inc(v.tensor_scalar_add(st4(S_Q1), st4(S_R1), 1.0), "sV")
            for k in range(2):
                j = g0 + k
                ten, off = ubufs[j % 4]
                buf = ten[:, off:off + 1024]
                dstR = R1 if j < NT else R2
                dc = (j % NT) * 1024
                p.w(v, "dM", dGam)
                v.tensor_scalar(dstR[:, dc:dc + 1024], buf,
                                STA[:, S_MU + j:S_MU + j + 1],
                                STA[:, S_S + j:S_S + j + 1],
                                ALU.subtract, ALU.mult)
                vv = p.inc(v.tensor_tensor(dstR[:, dc:dc + 1024],
                                           dstR[:, dc:dc + 1024],
                                           gsc[:, 0:1024], ALU.mult), "sV")
                vcons[j] = vv
                p.w(g, "sV", vv)
                p.w(g, "dC", cW_pre)
                p.inc(g.tensor_tensor(dstR[:, dc:dc + 1024], dstR[:, dc:dc + 1024],
                                      hsb[:, :], ALU.add), "sP")
        sP_u = p.cnt["sP"]

        if dbg:
            p.w(y, "sP", sP_u)
            p.dma(y, dbg_t["dbg_u"][:, 0:16384], R1[:, :], "dO")
            p.dma(y, dbg_t["dbg_u"][:, 16384:32768], R2[:, :], "dO")
        dbg_dO_u = p.cnt["dO"]

        if stop_after == "U":
            p.w(y, "sP", sP_u)
            p.dma(y, out[0:128, :], R1[:, 0:1024], "dO")
            p.w(y, "dO", p.cnt["dO"])
            p.finalize(frac)
            return nc, dbg_t
        p.w(y, "sP", sP_u)
        dDb = p.dma(y, hsb[:, :], Dbc, "dM")

        nbl = NB
        if stop_after and stop_after.startswith("T") and stop_after[1:].isdigit():
            nbl = int(stop_after[1:])
        tT_free = 0
        intra_done = [0, 0]
        vsc_prev = 0
        for b in range(nbl):
            c0, dp0 = b * DB, b * NDP
            eng, sem_e = (v, "sV") if b % 2 == 0 else (g, "sP")
            for hh in range(2):
                dph = dp0 + hh * 8
                p.w(eng, "sT", tT_free)
                if b % 2 == 1:
                    p.w(eng, "sV", vPk)
                p.w(eng, "dC", cW_pre)
                if hh == 1 and b % 2 == 1:
                    p.w(eng, "sV", p.cnt["sV"])
                in0 = bass.AP(mdA, dph, [[512, 128], [1, 8], [0, 192]])
                in1 = bass.AP(KJ, 0, [[192, 128], [0, 8], [1, 192]])
                ae = p.inc(eng.tensor_tensor(R3[:, TB_ARGE:TB_ARGE + 8 * 192], in0, in1,
                                             ALU.mult), sem_e)
                p.w(s, sem_e, ae)
                se = p.inc(s.activation(R3[:, TB_ARGE:TB_ARGE + 8 * 192],
                                        R3[:, TB_ARGE:TB_ARGE + 8 * 192], AF.Exp), "sS")
                p.w(v, "sS", max(se, sdq))
                p.w(v, "sT", tT_free)
                eap = lambda o: bass.AP(R3, TB_ARGE + o, [[R3_COLS, 128], [192, 8], [1, 64]])
                cb = bass.AP(Cpk, dph, [[512, 128], [1, 8], [0, 64]])
                db_ = bass.AP(dBpk, dph, [[512, 128], [1, 8], [0, 64]])
                ho = hh * 512
                v.tensor_tensor(R3[:, TB_CTIL + ho:TB_CTIL + ho + 512], eap(0), cb, ALU.mult)
                v.tensor_tensor(R3[:, TB_CTAU + ho:TB_CTAU + ho + 512], eap(64), cb, ALU.mult)
                vtb = p.inc(v.tensor_tensor(R3[:, TB_BTIL + ho:TB_BTIL + ho + 512],
                                            eap(128), db_, ALU.mult), "sV")
            if b == 0:
                v.memset(R3[:, TB_GRD0:TB_GRD0 + 64], 0.0)
                vtb = p.inc(v.memset(R3[:, TB_GRD1:TB_GRD1 + 64], 0.0), "sV")
            d0a = bass.AP(dAQ, dp0, [[512, 128], [1, NDP], [0, 32]])
            d0b = bass.AP(KM, 0, [[32, 128], [0, NDP], [1, 32]])
            p.w(g, "sV", vsc_prev)
            p.w(g, "sS", sdq)
            vd0 = p.inc(g.tensor_tensor(SCA[:, 1024:1536], d0a, d0b, ALU.mult), "sP")
            if b == 0:
                p.w(t, "sP", sP_u)
                p.w(t, "dM", dDb)

            if stop_after == "TA":
                p.w(y, "sV", p.cnt["sV"]); p.w(y, "sS", p.cnt["sS"])
                p.dma(y, out[0:128, :], R3[:, 0:1024], "dO")
                p.w(y, "dO", p.cnt["dO"])
                p.finalize(frac)
                return nc, dbg_t
            p.w(t, "sV", vtb)
            for bank_i, w0 in ((0, 0), (1, 4), (2, 8), (3, 12)):
                bank_wait(bank_i)
                for k in range(4):
                    wdx = w0 + k
                    src = bass.AP(R3, TB_BTIL + wdx * 64 - 64, [[R3_COLS, 128], [1, 128]])
                    ins = t.matmul(PS[bank_i][:, k * 128:(k + 1) * 128], src, KE[:, :],
                                   is_transpose=True, start=(k == 0), stop=(k == 3),
                                   skip_group_check=True)
                tw = p.inc(ins, "sT")
                p.w(s, "sT", tw)
                ev = p.inc(s.copy(R3[:, TB_BTT + w0 * 128:TB_BTT + (w0 + 4) * 128],
                                  PS[bank_i][:, :]), "sS")
                bankfree[bank_i] = ("sS", ev)
            bank_wait(0)
            src = bass.AP(R3, TB_BTIL + 16 * 64 - 64, [[R3_COLS, 128], [1, 128]])
            tw = p.inc(t.matmul(PS[0][:, 0:128], src, KE[:, :], is_transpose=True,
                                start=True, stop=True, skip_group_check=True), "sT")
            p.w(s, "sT", tw)
            ev = p.inc(s.copy(R3[:, TB_BTT + 16 * 128:TB_BTT + 17 * 128], PS[0][:, 0:128]),
                       "sS")
            bankfree[0] = ("sS", ev)
            sS_win = p.cnt["sS"]
            if stop_after == "TB":
                p.w(y, "sS", p.cnt["sS"])
                p.dma(y, out[0:128, :], R3[:, 0:1024], "dO")
                p.w(y, "dO", p.cnt["dO"])
                p.finalize(frac)
                return nc, dbg_t

            def mz_build(chh):
                for par in (0, 1):
                    for sub in range(2):
                        bank_i = par * 2 + sub
                        bank_wait(bank_i)
                        first = {0: True, 64: True}
                        for k in range(4):
                            kk = sub * 8 + k * 2 + par
                            ch = chh * 16 + kk
                            dpi = ch // 2
                            bt = R3[par * 64:par * 64 + 64,
                                    TB_BTIL + dpi * 64:TB_BTIL + dpi * 64 + 64]
                            ct = R3[par * 64:par * 64 + 64,
                                    TB_CTAU + dpi * 64:TB_CTAU + dpi * 64 + 64]
                            zz = KL[par * 64:par * 64 + 64,
                                    (1 - par) * 64:(1 - par) * 64 + 64]
                            for colg, half in ((0, 0), (64, 1)):
                                stf = first[colg]; first[colg] = False
                                dcol = k * 128 + half * 64
                                zcol = k * 128 + (1 - half) * 64
                                t.matmul(PS[bank_i][half * 64:half * 64 + 64, dcol:dcol + 64],
                                         bt, ct, start=stf, stop=False,
                                         tile_position=(par * 64, colg),
                                         skip_group_check=True)
                                lmz = t.matmul(
                                    PS[bank_i][half * 64:half * 64 + 64, zcol:zcol + 64],
                                    bt, zz, start=False, stop=(k == 3),
                                    tile_position=(par * 64, colg), skip_group_check=True)
                        tmz = p.inc(lmz, "sT")
                        p.w(v, "sT", tmz)
                        p.w(v, "sT", intra_done[chh])
                        ltzb = bass.AP(KL, 0, [[128, 128], [0, 4], [1, 128]])
                        if chh == 0:
                            dst = bass.AP(R3, TB_MZ + (sub * 8 + par) * 128,
                                          [[R3_COLS, 128], [256, 4], [1, 128]])
                        else:
                            dst = bass.AP(SCB, (sub * 8 + par) * 128,
                                          [[2048, 128], [256, 4], [1, 128]])
                        ev_ = p.inc(v.tensor_tensor(dst, PS[bank_i][:, :], ltzb, ALU.mult),
                                    "sV")
                        bankfree[bank_i] = ("sV", ev_)
                return p.cnt["sV"]

            sV_mz = mz_build(0)
            if stop_after == "TC":
                p.w(y, "sV", p.cnt["sV"])
                p.dma(y, out[0:128, :], R3[:, 0:1024], "dO")
                p.w(y, "dO", p.cnt["dO"])
                p.finalize(frac)
                return nc, dbg_t

            p.w(t, "sS", sS_win)
            bank_wait(6); bank_wait(7)
            started = {}
            last_ins = None
            for ch in range(DB):
                dpi, par = ch // 2, ch % 2
                for cpar in (0, 1):
                    bank_i = 6 + cpar
                    if cpar == 0:
                        stat = R3[0:64, TB_BTT + (dpi + 1) * 128 + par * 64:
                                  TB_BTT + (dpi + 1) * 128 + par * 64 + 64]
                    else:
                        stat = R3[64:128, TB_BTT + dpi * 128 + par * 64:
                                  TB_BTT + dpi * 128 + par * 64 + 64]
                    for half, uten in ((0, R2), (1, R1)):
                        ntile = NT - 1 if (half == 1 and cpar == 1) else NT
                        mov = bass.AP(uten, cpar * 64 * 16384 + c0 + ch,
                                      [[16384, 64], [1024, ntile]])
                        ocol = dpi * 32 + (0 if half == 0 else 16)
                        dst = bass.AP(PS[bank_i], par * 64 * 512 + ocol,
                                      [[512, 64], [1, ntile]])
                        key = (bank_i, par)
                        stf = key not in started
                        started[key] = True
                        last_ins = t.matmul(dst, stat, mov, start=stf, stop=False,
                                            tile_position=(cpar * 64, par * 64),
                                            skip_group_check=True)
            th = p.inc(last_ins, "sT")
            if stop_after == "TD0":
                p.w(y, "sT", th)
                p.dma(y, out[0:128, :], R3[:, 0:1024], "dO")
                p.w(y, "dO", p.cnt["dO"])
                p.finalize(frac)
                return nc, dbg_t
            p.w(s, "sT", th); p.w(v, "sT", th)
            s.copy(bass.AP(SCA, 0, [[2048, 128], [32, NDP], [2, 16]]),
                   bass.AP(PS[6], 0, [[512, 128], [32, NDP], [1, 16]]))
            ev6 = p.inc(s.copy(bass.AP(SCA, 512 + 1, [[2048, 128], [32, NDP], [2, 16]]),
                               bass.AP(PS[6], 16, [[512, 128], [32, NDP], [1, 16]])), "sS")
            v.tensor_copy(bass.AP(SCA, 1, [[2048, 128], [32, NDP], [2, 16]]),
                          bass.AP(PS[7], 0, [[512, 128], [32, NDP], [1, 16]]))
            ev = p.inc(v.tensor_copy(
                bass.AP(SCA, 512 + 2, [[2048, 128], [32, NDP], [2, 15]]),
                bass.AP(PS[7], 16, [[512, 128], [32, NDP], [1, 15]])), "sV")
            bankfree[6] = ("sS", ev6); bankfree[7] = ("sV", ev)
            if stop_after == "TDa":
                p.w(y, "sV", p.cnt["sV"]); p.w(y, "sS", p.cnt["sS"])
                p.dma(y, out[0:128, :], SCA[:, 0:1024], "dO")
                p.w(y, "dO", p.cnt["dO"])
                p.finalize(frac)
                return nc, dbg_t
            p.w(v, "sS", p.cnt["sS"])
            p.w(v, "sP", vd0)
            v.tensor_tensor_scan(SCA[:, 0:512], SCA[:, 1024:1536], SCA[:, 0:512],
                                 0.0, ALU.mult, ALU.add)
            v.tensor_scalar_mul(bass.AP(SCA, 512, [[2048, 128], [32, NDP]]),
                                bass.AP(SCA, 31, [[2048, 128], [32, NDP]]), KF[:, 0:1])
            vsc = p.inc(v.tensor_tensor_scan(SCA[:, 512:1024], SCA[:, 1024:1536],
                                             SCA[:, 512:1024], 0.0, ALU.mult, ALU.add),
                        "sV")
            vsc_prev = vsc
            if stop_after == "TDb":
                p.w(y, "sV", p.cnt["sV"])
                p.dma(y, out[0:128, :], SCA[:, 0:1024], "dO")
                p.w(y, "dO", p.cnt["dO"])
                p.finalize(frac)
                return nc, dbg_t

            if stop_after == "TD":
                p.w(y, "sV", p.cnt["sV"]); p.w(y, "sS", p.cnt["sS"])
                p.dma(y, out[0:128, :], SCA[:, 0:1024], "dO")
                p.w(y, "dO", p.cnt["dO"])
                p.finalize(frac)
                return nc, dbg_t
            started5 = {}

            def intra_half(chh, mzv):
                p.w(t, "sV", max(vsc, mzv))
                if chh == 0:
                    bank_wait(4); bank_wait(5)
                last = None
                for kk in range(16):
                    ch = chh * 16 + kk
                    dpi, par = ch // 2, ch % 2
                    mov_e = bass.AP(R1, c0 + ch, [[16384, 64], [1024, NT]])
                    mov_o = bass.AP(R1, 64 * 16384 + c0 + ch, [[16384, 64], [1024, NT]])
                    if chh == 0:
                        mzA = R3[0:64, TB_MZ + kk * 128:TB_MZ + kk * 128 + 64]
                        mzB = R3[64:128, TB_MZ + kk * 128 + 64:TB_MZ + kk * 128 + 128]
                    else:
                        mzA = SCB[0:64, kk * 128:kk * 128 + 64]
                        mzB = SCB[64:128, kk * 128 + 64:kk * 128 + 128]
                    stf = (4, 0) not in started5; started5[(4, 0)] = True
                    t.matmul(bass.AP(PS[4], ch * 16, [[512, 64], [1, NT]]),
                             mzA, mov_e, start=stf, stop=False,
                             tile_position=(0, 0), skip_group_check=True)
                    stf = (5, 1) not in started5; started5[(5, 1)] = True
                    t.matmul(bass.AP(PS[5], 64 * 512 + ch * 16, [[512, 64], [1, NT]]),
                             mzB, mov_o, start=stf, stop=False,
                             tile_position=(64, 64), skip_group_check=True)
                    if par == 0:
                        stf = (5, 0) not in started5; started5[(5, 0)] = True
                        t.matmul(bass.AP(PS[5], ch * 16, [[512, 64], [1, NT]]),
                                 KL[64:128, 64:128], KL[64:128, 0:NT],
                                 start=stf, stop=False, tile_position=(64, 0),
                                 skip_group_check=True)
                    else:
                        stf = (4, 1) not in started5; started5[(4, 1)] = True
                        t.matmul(bass.AP(PS[4], 64 * 512 + ch * 16, [[512, 64], [1, NT]]),
                                 KL[0:64, 0:64], KL[0:64, 64:64 + NT],
                                 start=stf, stop=False, tile_position=(0, 64),
                                 skip_group_check=True)
                    for cpar in (0, 1):
                        bank_i = 4 + par
                        ct = R3[par * 64:par * 64 + 64,
                                TB_CTIL + dpi * 64:TB_CTIL + dpi * 64 + 64]
                        mov_h = bass.AP(SCA, par * 64 * 2048 + 512 + dpi * 32 + cpar,
                                        [[2048, 64], [2, NT]])
                        key = (bank_i, cpar)
                        stf = key not in started5
                        started5[key] = True
                        dst = bass.AP(PS[bank_i], cpar * 64 * 512 + ch * 16,
                                      [[512, 64], [1, NT]])
                        last = t.matmul(dst, ct, mov_h, start=stf, stop=False,
                                        tile_position=(par * 64, cpar * 64),
                                        skip_group_check=True)
                return p.inc(last, "sT")

            sV_mz2 = mz_build(1)
            ti0 = intra_half(0, sV_mz)
            intra_done[0] = ti0
            ti = intra_half(1, sV_mz2)
            intra_done[1] = ti
            tT_free = ti
            p.w(g, "sT", ti)
            p.w(g, "dM", dDb)
            if b == 0:
                p.w(g, "dO", dbg_dO_u)
            ucols = bass.AP(R1, c0, [[16384, 128], [1024, NT], [1, DB]])
            dbb = bass.AP(hsb, c0, [[1024, 128], [0, NT], [1, DB]])
            g.tensor_tensor(ucols, ucols, dbb, ALU.mult)
            ytmp = SCA[:, 1536:2048]
            src4 = bass.AP(PS[4], 0, [[512, 128], [1, 16], [16, DB]])
            src5 = bass.AP(PS[5], 0, [[512, 128], [1, 16], [16, DB]])
            tmpd = bass.AP(SCA, 1536, [[2048, 128], [1, 16], [16, DB]])
            p.w(v, "sT", ti)
            p.w(v, "sP", p.cnt["sP"])
            v.scalar_tensor_tensor(tmpd, src4, 1.0, ucols, ALU.mult, ALU.add)
            ev = p.inc(v.scalar_tensor_tensor(ucols, src5, 1.0, tmpd, ALU.mult, ALU.add),
                       "sV")
            bankfree[4] = ("sV", ev)
            bankfree[5] = ("sV", ev)
            started5.clear()
        sV_y = p.cnt["sV"]
        sP_y = p.cnt["sP"]

        if dbg:
            p.w(y, "sV", sV_y)
            p.w(y, "sP", sP_y)
            p.dma(y, dbg_t["dbg_y"][:, :], R1[:, :], "dO")
        dbg_dO_y = p.cnt["dO"]
        if stop_after and stop_after.startswith("T") and stop_after[1:].isdigit():
            p.w(y, "sV", p.cnt["sV"])
            p.dma(y, out[0:128, :], R1[:, 0:1024], "dO")
            p.w(y, "dO", p.cnt["dO"])
            p.finalize(frac)
            return nc, dbg_t

        p.w(t, "sV", sV_y)
        p.w(t, "sP", sP_y)
        blocks = [(ti_, dcc) for dcc in range(8) for ti_ in range(NT)]
        for bi in range(0, len(blocks), 4):
            bank_i = (bi // 4) % 2
            bank_wait(bank_i)
            for k in range(4):
                ti_, dcc = blocks[bi + k]
                src = bass.AP(R1, ti_ * 1024 + dcc * 128, [[16384, 128], [1, 128]])
                ins = t.matmul(PS[bank_i][:, k * 128:(k + 1) * 128], src, KE[:, :],
                               is_transpose=True, start=(k == 0), stop=(k == 3),
                               skip_group_check=True)
            tw = p.inc(ins, "sT")
            p.w(v, "sT", tw)
            ti0, dc0 = blocks[bi]
            ev = p.inc(v.tensor_copy(
                R2[:, dc0 * 2048 + ti0 * 128:dc0 * 2048 + ti0 * 128 + 512],
                PS[bank_i][:, :]), "sV")
            bankfree[bank_i] = ("sV", ev)
        sV_yfm = p.cnt["sV"]

        last_attn_mm = 0
        for ec in range(8):
            if ec == 0:
                p.w(y, "sT", tT_free)
            else:
                p.w(y, "sT", wt_cons)
            if ec == 0:
                p.w(y, "sV", sV_y)
                p.w(y, "sP", sP_y)
            dwr = p.dma(y, SCB[:, 0:1024], Wout[ec * 128:(ec + 1) * 128, :], "dWo")
            p.w(t, "dWo", dwr)
            for hb in range(2):
                bank_i = 6 + hb
                bank_wait(bank_i)
                for k in range(4):
                    ins = t.matmul(PS[bank_i][:, k * 128:(k + 1) * 128],
                                   SCB[:, hb * 512 + k * 128:hb * 512 + (k + 1) * 128],
                                   KE[:, :], is_transpose=True, start=(k == 0),
                                   stop=(k == 3), skip_group_check=True)
                tw = p.inc(ins, "sT")
                wt_cons = tw
                p.w(v, "sT", tw)
                if ec > 0:
                    p.w(v, "sT", last_attn_mm)
                ev = p.inc(v.tensor_copy(hsb[:, hb * 512:hb * 512 + 512], PS[bank_i][:, :]),
                           "sV")
                bankfree[bank_i] = ("sV", ev)
            vwt = p.cnt["sV"]
            p.w(t, "sV", vwt)
            p.w(t, "sV", sV_yfm)
            for tt in range(4):
                bank_i = 2 + (tt % 2)
                bank_wait(bank_i)
                for dcc in range(8):
                    stat = hsb[:, dcc * 128:(dcc + 1) * 128]
                    mov = R2[:, dcc * 2048 + tt * 512:dcc * 2048 + tt * 512 + 512]
                    ins = t.matmul(PS[bank_i][:, :], stat, mov, start=(dcc == 0),
                                   stop=(dcc == 7), skip_group_check=True)
                ta = p.inc(ins, "sT")
                last_attn_mm = ta
                p.w(s, "sT", ta)
                p.w(s, "dO", dbg_dO_y)
                ev = p.inc(s.activation(
                    R1[:, ec * 2048 + tt * 512:ec * 2048 + tt * 512 + 512],
                    PS[bank_i][:, :], AF.Identity, bias=bcol[:, ec:ec + 1], scale=1.0),
                    "sS")
                bankfree[bank_i] = ("sS", ev)
        sS_attn = p.cnt["sS"]

        p.w(t, "sS", sS_attn)
        v_ev2 = []
        spill_h = []
        scb_ok = False
        sV_usw = sV_y
        for ti_ in range(NT):
            buf = SCA[:, (ti_ % 2) * 1024:(ti_ % 2) * 1024 + 1024]
            qx = y if ti_ % 2 == 0 else s
            if ti_ < 2:
                p.w(qx, "sV", sV_usw)
                p.w(qx, "sP", sP_y)
            if ti_ >= 2:
                p.w(qx, "sV", v_ev2[ti_ - 2])
            dj = p.dma(qx, buf, x_src(ti_), "dX%d" % (ti_ % 2))
            p.w(v, "dX%d" % (ti_ % 2), dj)
            vr = p.inc(v.tensor_scalar(buf, buf, STA[:, S_Q1 + ti_:S_Q1 + ti_ + 1],
                                       STA[:, S_MR + ti_:S_MR + ti_ + 1],
                                       ALU.mult, ALU.subtract), "sV")
            for hb in range(2):
                bank_i = 6 + hb
                bank_wait(bank_i)
                for k in range(4):
                    ec = hb * 4 + k
                    src = bass.AP(R1, ec * 2048 + ti_ * 128, [[16384, 128], [1, 128]])
                    ins = t.matmul(PS[bank_i][:, k * 128:(k + 1) * 128], src, KE[:, :],
                                   is_transpose=True, start=(k == 0), stop=(k == 3),
                                   skip_group_check=True)
            ta = p.inc(ins, "sT")
            p.w(v, "sT", ta)
            p.w(v, "sV", vr)
            if not scb_ok:
                p.w(v, "sT", wt_cons)
                scb_ok = True
            if ti_ >= 2:
                p.w(v, "dO", spill_h[ti_ - 1])
            v.tensor_tensor(bass.AP(SCB, (ti_ % 2) * 1024, [[2048, 128], [1, 512]]),
                            PS[6][:, :],
                            bass.AP(SCA, (ti_ % 2) * 1024, [[2048, 128], [1, 512]]),
                            ALU.add)
            ev = p.inc(v.tensor_tensor(
                bass.AP(SCB, (ti_ % 2) * 1024 + 512, [[2048, 128], [1, 512]]),
                PS[7][:, :],
                bass.AP(SCA, (ti_ % 2) * 1024 + 512, [[2048, 128], [1, 512]]),
                ALU.add), "sV")
            bankfree[6] = ("sV", ev); bankfree[7] = ("sV", ev)
            v_ev2.append(ev)
            p.w(y, "sV", ev)
            spill_h.append(p.dma(y, x1sp[ti_ * 128:(ti_ + 1) * 128, :],
                                 bass.AP(SCB, (ti_ % 2) * 1024, [[2048, 128], [1, 1024]]),
                                 "dO"))
        dO_x1 = p.cnt["dO"]

        if stop_after == "W":
            p.w(y, "dO", p.cnt["dO"])
            p.finalize(frac)
            return nc, dbg_t

        sS3_0, sV3_0 = p.cnt["sS"], p.cnt["sV"]
        x1d = []
        for j in range(NT):
            buf = SCA[:, (j % 2) * 1024:(j % 2) * 1024 + 1024]
            if j == 0:
                p.w(y, "dO", dO_x1)
            if j < 2:
                p.w(y, "sV", v_ev2[NT - 1])
            if j >= 2:
                p.w(y, "sS", sS3_0 + (j - 1))
                p.w(y, "sV", sV3_0 + (j - 1))
            x1d.append(p.dma(y, buf, x1sp[j * 128:(j + 1) * 128, :], "dX%d" % (j % 2)))
        p.w(s, "dO", dO_x1)
        for j in range(NT):
            buf = SCA[:, (j % 2) * 1024:(j % 2) * 1024 + 1024]
            p.w(s, "dX%d" % (j % 2), x1d[j])
            p.inc(s.activation(SCB[:, 0:1024], buf, AF.Square,
                               accum_out=STA[:, S_SUM3 + j:S_SUM3 + j + 1]), "sS")
            p.w(v, "dX%d" % (j % 2), x1d[j])
            p.inc(v.reduce_sum(STA[:, S_MU3 + j:S_MU3 + j + 1], buf, axis=AXX), "sV")
        st16 = lambda c0_: STA[:, c0_:c0_ + 16]
        p.w(v, "sS", p.cnt["sS"]); p.w(v, "sV", p.cnt["sV"])
        v.tensor_scalar_mul(st16(S_MU3), st16(S_MU3), 1.0 / D)
        v.tensor_tensor(st16(S_R33), st16(S_MU3), st16(S_MU3), ALU.mult)
        v.tensor_scalar_mul(st16(S_T0), st16(S_SUM3), 1.0 / D)
        v.tensor_tensor(st16(S_R33), st16(S_T0), st16(S_R33), ALU.subtract)
        v.tensor_scalar_add(st16(S_T0), st16(S_R33), EPS)
        vx = p.inc(v.reciprocal(st16(S_T1), st16(S_T0)), "sV")
        p.w(s, "sV", vx)
        sx = p.inc(s.activation(st16(S_R33), st16(S_T1), AF.Sqrt), "sS")
        p.w(v, "sS", sx)
        v.tensor_tensor(st16(S_T2), st16(S_R33), st16(S_R33), ALU.mult)
        v.tensor_tensor(st16(S_T2), st16(S_T0), st16(S_T2), ALU.mult)
        v.tensor_scalar(st16(S_T2), st16(S_T2), -0.5, 1.5, ALU.mult, ALU.add)
        vf3 = p.inc(v.tensor_tensor(st16(S_R33), st16(S_R33), st16(S_T2), ALU.mult), "sV")

        vts_h = []
        tx_h = []
        sV_f3, sS_f3 = p.cnt["sV"], p.cnt["sS"]
        for j in range(NT):
            bufA = SCA[:, (j % 2) * 1024:(j % 2) * 1024 + 1024]
            if j < 2:
                p.w(y, "sV", sV_f3)
                p.w(y, "sS", sS_f3)
            if j >= 2:
                p.w(y, "sV", vts_h[j - 2])
            dj = p.dma(y, bufA, x1sp[j * 128:(j + 1) * 128, :], "dX%d" % (j % 2))
            p.w(v, "dX%d" % (j % 2), dj)
            p.w(v, "sV", vf3)
            if j >= 2:
                p.w(v, "sT", tx_h[j - 2])
            bufB = SCB[:, (j % 2) * 1024:(j % 2) * 1024 + 1024]
            vts = p.inc(v.tensor_scalar(bufB, bufA,
                                        STA[:, S_MU3 + j:S_MU3 + j + 1],
                                        STA[:, S_R33 + j:S_R33 + j + 1],
                                        ALU.subtract, ALU.mult), "sV")
            vts_h.append(vts)
            p.w(t, "sV", vts)
            for hb in range(2):
                bank_i = 6 + hb
                bank_wait(bank_i)
                for k in range(4):
                    dcc = hb * 4 + k
                    src = bass.AP(SCB, (j % 2) * 1024 + dcc * 128, [[2048, 128], [1, 128]])
                    ins = t.matmul(PS[bank_i][:, k * 128:(k + 1) * 128], src, KE[:, :],
                                   is_transpose=True, start=(k == 0), stop=(k == 3),
                                   skip_group_check=True)
            tx = p.inc(ins, "sT")
            tx_h.append(tx)
            p.w(s, "sT", tx)
            for hb in range(2):
                dc0 = hb * 4
                ev = p.inc(s.copy(bass.AP(R1, dc0 * 2048 + j * 128,
                                          [[16384, 128], [2048, 4], [1, 128]]),
                                  PS[6 + hb][:, :]), "sS")
                bankfree[6 + hb] = ("sS", ev)
        sS_xf = p.cnt["sS"]
        if dbg:
            p.w(y, "sS", sS_xf)
            p.dma(y, dbg_t["dbg_xf"][:, :], R1[:, :], "dO")

        p.w(t, "sS", sS_xf)
        p.w(t, "sV", vgw)
        for tt in range(4):
            bank_i = tt % 2
            bank_wait(bank_i)
            for dcc in range(8):
                ins = t.matmul(PS[bank_i][0:16, 0:512], gwT[:, dcc * 16:(dcc + 1) * 16],
                               R1[:, dcc * 2048 + tt * 512:dcc * 2048 + tt * 512 + 512],
                               start=(dcc == 0), stop=(dcc == 7), skip_group_check=True)
            tg = p.inc(ins, "sT")
            p.w(v, "sT", tg)
            ev = p.inc(v.tensor_copy(
                bass.AP(R2, CFM_C + tt * 512, [[16384, 16], [1, 512]]),
                PS[bank_i][0:16, 0:512]), "sV")
            bankfree[bank_i] = ("sV", ev)
        sV_gs = p.cnt["sV"]
        p.w(t, "sV", sV_gs)
        bank_wait(2)
        for j in range(NT):
            ins = t.matmul(PS[2][:, j * 16:(j + 1) * 16],
                           bass.AP(R2, CFM_C + j * 128, [[16384, 16], [1, 128]]),
                           KE[0:16, 0:16], is_transpose=True, start=(j == 0),
                           stop=(j == NT - 1), skip_group_check=True)
        tg = p.inc(ins, "sT")
        p.w(v, "sT", tg)
        vtm = p.inc(v.tensor_copy(ctm[:, :], PS[2][:, 0:256]), "sV")
        bankfree[2] = ("sV", vtm)

        GS_MX, GS_SM, GS_OR, GS_SC, GS_GS, GS_PEN, GS_T = 0, 16, 32, 288, 544, 608, 672
        p.w(v, "sV", vtm)
        gA = lambda o, dims: bass.AP(gsc, o, [[1024, 128]] + dims)
        v.tensor_reduce(gA(GS_MX, [[1, 16]]), bass.AP(ctm, 0, [[256, 128], [16, 16], [1, 16]]),
                        axis=AXX, op=ALU.max)
        for j in range(NT):
            lastv = v.tensor_scalar(gsc[:, GS_OR + j * 16:GS_OR + (j + 1) * 16],
                                    ctm[:, j * 16:(j + 1) * 16],
                                    gsc[:, GS_MX + j:GS_MX + j + 1], None, ALU.subtract)
        vx = p.inc(lastv, "sV")
        p.w(s, "sV", vx)
        sx = p.inc(s.activation(gsc[:, GS_OR:GS_OR + 256], gsc[:, GS_OR:GS_OR + 256],
                                AF.Exp), "sS")
        p.w(v, "sS", sx)
        v.tensor_reduce(gA(GS_SM, [[1, 16]]), gA(GS_OR, [[16, 16], [1, 16]]),
                        axis=AXX, op=ALU.add)
        v.reciprocal(gA(GS_SM, [[1, 16]]), gA(GS_SM, [[1, 16]]))
        for j in range(NT):
            v.tensor_scalar_mul(gsc[:, GS_OR + j * 16:GS_OR + (j + 1) * 16],
                                gsc[:, GS_OR + j * 16:GS_OR + (j + 1) * 16],
                                gsc[:, GS_SM + j:GS_SM + j + 1])
        gbbc = bass.AP(KGB, 0, [[16, 128], [0, 16], [1, 16]])
        v.tensor_tensor(gsc[:, GS_SC:GS_SC + 256], gA(GS_OR, [[1, 256]]), gbbc, ALU.add)
        gsl = lambda i: bass.AP(gsc, GS_SC + i, [[1024, 128], [16, 16], [4, 4]])
        first = True
        for i in range(4):
            for jj in range(i + 1, 4):
                v.tensor_tensor(gsc[:, GS_T:GS_T + 64], gsl(i), gsl(jj), ALU.add)
                if first:
                    v.tensor_copy(gsc[:, GS_GS:GS_GS + 64], gsc[:, GS_T:GS_T + 64])
                    first = False
                else:
                    v.tensor_tensor(gsc[:, GS_GS:GS_GS + 64], gsc[:, GS_GS:GS_GS + 64],
                                    gsc[:, GS_T:GS_T + 64], ALU.max)
        gg = lambda k: bass.AP(gsc, GS_GS + k, [[1024, 128], [4, 16]])
        gp = lambda k: bass.AP(gsc, GS_PEN + k, [[1024, 128], [4, 16]])
        v.memset(gsc[:, GS_PEN:GS_PEN + 64], 0.0)
        for g1 in range(4):
            for g2 in range(4):
                if g1 == g2:
                    continue
                v.tensor_tensor(gsc[:, GS_T:GS_T + 16], gg(g2), gg(g1), ALU.is_gt)
                v.tensor_tensor(gp(g1), gp(g1), gA(GS_T, [[1, 16]]), ALU.add)
        v.tensor_scalar(gsc[:, GS_PEN:GS_PEN + 64], gsc[:, GS_PEN:GS_PEN + 64],
                        1.5, 1e30, ALU.is_ge, ALU.mult)
        pen_e = bass.AP(gsc, GS_PEN, [[1024, 128], [4, 16], [1, 4], [0, 4]])
        v.tensor_tensor(gsc[:, GS_SC:GS_SC + 256], gA(GS_SC, [[1, 256]]), pen_e, ALU.subtract)
        v.tensor_reduce(gA(GS_MX, [[1, 16]]), gA(GS_SC, [[16, 16], [1, 16]]),
                        axis=AXX, op=ALU.max)
        for j in range(NT):
            v.tensor_scalar(gsc[:, GS_T + j * 16:GS_T + (j + 1) * 16],
                            gsc[:, GS_SC + j * 16:GS_SC + (j + 1) * 16],
                            gsc[:, GS_MX + j:GS_MX + j + 1], None, ALU.is_ge)
        v.tensor_scalar_mul(gsc[:, GS_T:GS_T + 256], gsc[:, GS_T:GS_T + 256], 1e30)
        v.tensor_tensor(gsc[:, GS_T:GS_T + 256], gA(GS_SC, [[1, 256]]),
                        gA(GS_T, [[1, 256]]), ALU.subtract)
        v.tensor_reduce(gA(GS_SM, [[1, 16]]), gA(GS_T, [[16, 16], [1, 16]]),
                        axis=AXX, op=ALU.max)
        for j in range(NT):
            v.tensor_scalar(ctm[:, j * 16:(j + 1) * 16],
                            gsc[:, GS_SC + j * 16:GS_SC + (j + 1) * 16],
                            gsc[:, GS_SM + j:GS_SM + j + 1], None, ALU.is_ge)
        vcm = p.inc(v.tensor_tensor(ctm[:, :], ctm[:, :], gA(GS_OR, [[1, 256]]), ALU.mult),
                    "sV")
        if dbg:
            p.w(y, "sV", vcm)
            p.dma(y, dbg_t["dbg_cm"][:, :], ctm[:, :], "dO")
        if stop_after == "G":
            p.w(y, "sV", vcm)
            p.dma(y, out[0:128, 0:256], ctm[:, :], "dO")
            p.w(y, "dO", p.cnt["dO"])
            p.finalize(frac)
            return nc, dbg_t
        cfm_ap = lambda o, n_: bass.AP(R2, CFM_C + o, [[16384, 16], [1, n_]])
        p.w(t, "sV", vcm)
        for wv in range(4):
            bank_wait(2)
            for k in range(4):
                j = wv * 4 + k
                ins = t.matmul(PS[2][0:16, k * 128:(k + 1) * 128],
                               ctm[:, j * 16:(j + 1) * 16],
                               KE[:, :], is_transpose=True, start=(k == 0), stop=(k == 3),
                               skip_group_check=True)
            tg = p.inc(ins, "sT")
            p.w(v, "sT", tg)
            vcf = p.inc(v.tensor_copy(cfm_ap(wv * 512, 512),
                                      bass.AP(PS[2], 0, [[512, 16], [1, 512]])), "sV")
            bankfree[2] = ("sV", vcf)
        dbe2 = p.dma(y, bass.AP(R2, BE2_C, [[16384, 16], [1, 1024]]), be2, "dM")

        YA, CBC = 0, 8192
        KOH2 = 8192
        tcons_h, tcons_w2, prev_vh = 0, 0, 0
        we2_done = [0, 0]
        ostore_h = []
        dko_vals = {}
        for g_pre in range(2):
            dko_vals[g_pre] = p.dma(
                y, R3[0:16, KOH2 + g_pre * 128:KOH2 + (g_pre + 1) * 128],
                kohc[:, g_pre * 128:(g_pre + 1) * 128], "dK")
        pend = None

        def hmov(b, off, n_):
            if b == 0:
                return hsb[:, off:off + n_]
            return SCA[:, 1024 + off:1024 + off + n_]

        def emit_we2(pd):
            nonlocal tcons_w2
            p.w(t, "sV", pd["vh"])
            p.w(t, "dw2", pd["d2"])
            if pd["ei"] == 17:
                p.w(t, "dM", dbe2)
            bank_wait(4); bank_wait(5); bank_wait(6); bank_wait(7)
            for dcc in range(8):
                bk = PS[4 + dcc // 2]
                ocol = (dcc % 2) * 256
                for ic_ in range(4):
                    stat = R2[:, F_WE2 + ic_ * 1024 + dcc * 128:
                              F_WE2 + ic_ * 1024 + dcc * 128 + 128]
                    ins = t.matmul(bk[:, ocol:ocol + 256], stat,
                                   hmov(pd["buf"], ic_ * 256, 256),
                                   start=(ic_ == 0), stop=False,
                                   skip_group_check=True)
                if pd["ei"] == 17:
                    ins = t.matmul(bk[:, ocol:ocol + 256],
                                   bass.AP(R2, BE2_C + dcc * 128,
                                           [[16384, 16], [1, 128]]),
                                   cfm_ap(pd["tof"], 256),
                                   start=False, stop=True, skip_group_check=True)
            tcons_w2 = p.inc(ins, "sT")
            we2_done[pd["buf"]] = tcons_w2
            pd["done"] = tcons_w2

        def emit_evac(pd):
            p.w(v, "sT", pd["done"])
            for dcc in range(8):
                bk = PS[4 + dcc // 2]
                ocol = (dcc % 2) * 256
                dsty = bass.AP(R2, YA + dcc * 1024 + pd["ts"] * 256,
                               [[16384, 128], [1, 256]])
                srcp = bass.AP(bk, ocol, [[512, 128], [1, 256]])
                if pd["ei"] == 0:
                    ins2 = v.tensor_scalar(dsty, srcp,
                                           bcol[:, 8 + dcc:8 + dcc + 1],
                                           None, ALU.add)
                else:
                    ins2 = v.tensor_tensor(dsty, dsty, srcp, ALU.add)
            vya = p.inc(ins2, "sV")
            for bb in (4, 5, 6, 7):
                bankfree[bb] = ("sV", vya)

        for half in range(2):
            th0 = half * 1024
            for ei in range(18):
                is_sh = ei < 2
                e = ei - 2
                if is_sh:
                    shi = ei
                    w1s = lambda dcc: bass.AP(ws1.tensor, dcc * 128 * D + shi * IC,
                                              [[D, 128], [1, IC]])
                    w3s = lambda dcc: bass.AP(ws3.tensor, dcc * 128 * D + shi * IC,
                                              [[D, 128], [1, IC]])
                    w2s = lambda ic_: ws2[shi * IC + ic_ * 128:shi * IC + (ic_ + 1) * 128, :]
                else:
                    w1s = lambda dcc: We1[e][dcc * 128:(dcc + 1) * 128, :]
                    w3s = lambda dcc: We3[e][dcc * 128:(dcc + 1) * 128, :]
                    w2s = lambda ic_: We2[e][ic_ * 128:(ic_ + 1) * 128, :]
                if half == 0 and ei == 0:
                    p.w(y, "sT", tT_free)
                else:
                    p.w(y, "sT", tcons_h)
                d1 = [p.dma(y, R3[:, F_WE1 + dcc * 512:F_WE1 + (dcc + 1) * 512], w1s(dcc),
                            "dw1") for dcc in range(8)]
                d3 = [p.dma(y, R3[:, F_WE3 + dcc * 512:F_WE3 + (dcc + 1) * 512], w3s(dcc),
                            "dw3") for dcc in range(8)]
                if not is_sh:
                    g_idx = half * 16 + e
                    sl_ = g_idx % 2
                    p.w(t, "dK")
                    p.w(t, "sV", vcf)
                    bank_wait(4); bank_wait(5)
                    for k in range(2):
                        ins = t.matmul(PS[4 + k][:, :],
                                       R3[0:16, KOH2 + sl_ * 128:KOH2 + sl_ * 128 + 128],
                                       cfm_ap(th0 + k * 512, 512),
                                       start=True, stop=True, skip_group_check=True)
                    tcb = p.inc(ins, "sT")
                    p.w(s, "sT", tcb)
                    p.w(s, "sV", prev_vh)
                    s.copy(R2[:, CBC:CBC + 512], PS[4][:, :])
                    scb = p.inc(s.copy(R2[:, CBC + 512:CBC + 1024], PS[5][:, :]),
                                "sS")
                    bankfree[4] = ("sS", scb); bankfree[5] = ("sS", scb)
                    if g_idx + 2 < 32:
                        gn = g_idx + 2
                        p.w(y, "sT", tcb)
                        dko_vals[gn] = p.dma(
                            y, R3[0:16, KOH2 + (gn % 2) * 128:KOH2 + (gn % 2) * 128 + 128],
                            kohc[:, (gn % 16) * 128:((gn % 16) + 1) * 128], "dK")
                for ts_ in range(4):
                    tof = th0 + ts_ * 256
                    p.w(t, "dw1", d1[-1])
                    p.w(t, "dw3", d3[-1])
                    for ic2 in range(2):
                        b1, b3 = ic2, 2 + ic2
                        bank_wait(b1); bank_wait(b3)
                        for sub in range(2):
                            ich = ic2 * 2 + sub
                            for dcc in range(8):
                                st1 = R3[:, F_WE1 + dcc * 512 + ich * 128:
                                         F_WE1 + dcc * 512 + ich * 128 + 128]
                                st3 = R3[:, F_WE3 + dcc * 512 + ich * 128:
                                         F_WE3 + dcc * 512 + ich * 128 + 128]
                                mov = R1[:, dcc * 2048 + tof:dcc * 2048 + tof + 256]
                                t.matmul(PS[b1][:, sub * 256:(sub + 1) * 256], st1, mov,
                                         start=(dcc == 0), stop=(dcc == 7),
                                         skip_group_check=True)
                                ins = t.matmul(PS[b3][:, sub * 256:(sub + 1) * 256], st3,
                                               mov, start=(dcc == 0), stop=(dcc == 7),
                                               skip_group_check=True)
                    tcons_h = p.inc(ins, "sT")
                    if pend is not None:
                        emit_we2(pend)
                    if ts_ == 0:
                        if pend is None and half == 0 and ei == 0:
                            p.w(y, "sT", tT_free)
                        else:
                            p.w(y, "sT", tcons_w2)
                        d2 = [p.dma(y, R2[:, F_WE2 + ic_ * 1024:F_WE2 + (ic_ + 1) * 1024],
                                    w2s(ic_), "dw2") for ic_ in range(4)]
                    p.w(s, "sT", tcons_h)
                    if half == 0 and ei == 0 and ts_ == 0:
                        p.w(s, "sT", last_attn_mm)
                    p.w(s, "sV", prev_vh)
                    for ich in range(4):
                        col = (144 + shi * 4 + ich) if is_sh else (16 + e * 4 + ich)
                        p.inc(s.activation(SCA[:, ich * 256:(ich + 1) * 256],
                                           PS[ich // 2][:, (ich % 2) * 256:(ich % 2) * 256 + 256],
                                           AF.Silu, bias=bcol[:, col:col + 1], scale=1.0),
                              "sS")
                    sl = p.cnt["sS"]
                    buf = ts_ % 2
                    p.w(v, "sS", sl)
                    p.w(v, "sT", we2_done[buf])
                    for ich in range(4):
                        col3 = (152 + shi * 4 + ich) if is_sh else (80 + e * 4 + ich)
                        lastv = v.scalar_tensor_tensor(hmov(buf, ich * 256, 256),
                                                       PS[2 + ich // 2][:, (ich % 2) * 256:
                                                                        (ich % 2) * 256 + 256],
                                                       bcol[:, col3:col3 + 1],
                                                       SCA[:, ich * 256:(ich + 1) * 256],
                                                       ALU.add, ALU.mult)
                    if not is_sh:
                        p.w(v, "sS", scb)
                        cbap = bass.AP(R2, CBC + ts_ * 256, [[16384, 128], [0, 4], [1, 256]])
                        lastv = v.tensor_tensor(hmov(buf, 0, 1024), hmov(buf, 0, 1024),
                                                cbap, ALU.mult)
                    vh = p.inc(lastv, "sV")
                    prev_vh = vh
                    for bb in (0, 1, 2, 3):
                        bankfree[bb] = ("sV", vh)
                    if pend is not None:
                        emit_evac(pend)
                    pend = {"buf": buf, "vh": vh, "ts": ts_, "ei": ei,
                            "d2": d2[-1], "tof": tof}
            emit_we2(pend)
            emit_evac(pend)
            pend = None
            vout = []
            for tih in range(8):
                ti_ = half * 8 + tih
                bufA = SCA[:, (tih % 2) * 1024:(tih % 2) * 1024 + 1024]
                qx = y if tih % 2 == 0 else s
                if tih == 0:
                    p.w(qx, "sV", prev_vh)
                if tih == 1:
                    p.w(qx, "sT", tcons_w2)
                if tih >= 2:
                    p.w(qx, "sV", vout[tih - 2])
                dj = p.dma(qx, bufA, x1sp[ti_ * 128:(ti_ + 1) * 128, :],
                           "dX%d" % (tih % 2))
                p.w(t, "sV", p.cnt["sV"])
                for hb in range(2):
                    bank_i = 2 + hb
                    bank_wait(bank_i)
                    for k in range(4):
                        dcc = hb * 4 + k
                        src = bass.AP(R2, YA + dcc * 1024 + tih * 128,
                                      [[16384, 128], [1, 128]])
                        ins = t.matmul(PS[bank_i][:, k * 128:(k + 1) * 128], src, KE[:, :],
                                       is_transpose=True, start=(k == 0), stop=(k == 3),
                                       skip_group_check=True)
                to = p.inc(ins, "sT")
                p.w(v, "sT", to)
                p.w(v, "dX%d" % (tih % 2), dj)
                if ostore_h:
                    p.w(v, "dO", ostore_h[-1])
                v.tensor_tensor(bass.AP(SCB, (tih % 2) * 1024, [[2048, 128], [1, 512]]),
                                PS[2][:, :],
                                bass.AP(SCA, (tih % 2) * 1024, [[2048, 128], [1, 512]]),
                                ALU.add)
                ev = p.inc(v.tensor_tensor(
                    bass.AP(SCB, (tih % 2) * 1024 + 512, [[2048, 128], [1, 512]]),
                    PS[3][:, :],
                    bass.AP(SCA, (tih % 2) * 1024 + 512, [[2048, 128], [1, 512]]),
                    ALU.add), "sV")
                bankfree[2] = ("sV", ev); bankfree[3] = ("sV", ev)
                vout.append(ev)
                p.w(y, "sV", ev)
                ostore_h.append(p.dma(y, out[ti_ * 128:(ti_ + 1) * 128, :],
                                      bass.AP(SCB, (tih % 2) * 1024,
                                              [[2048, 128], [1, 1024]]), "dO"))
        p.w(y, "dO", p.cnt["dO"])
        p.finalize(frac)

    return nc, dbg_t


def _bcolh(i):
    o = np.zeros((128, 160), np.float32)
    o[:, 0:8] = i["L0_attn_b_out"].reshape(8, 128).T
    o[:, 8:16] = i["L0_ffn_bs2"].reshape(8, 128).T
    o[:, 16:80] = i["L0_ffn_be1"].reshape(16, 4, 128).transpose(2, 0, 1).reshape(128, 64)
    o[:, 80:144] = i["L0_ffn_be3"].reshape(16, 4, 128).transpose(2, 0, 1).reshape(128, 64)
    o[:, 144:152] = i["L0_ffn_bs1"].reshape(8, 128).T
    o[:, 152:160] = i["L0_ffn_bs3"].reshape(8, 128).T
    return o


def _kohc():
    a = np.zeros((16, 2048), np.float32)
    for e in range(16):
        a[e, e * 128:(e + 1) * 128] = 1.0
    return a


def _consts():
    tri = np.triu(np.ones((Q, Q), np.float32))
    LTZ = np.zeros((128, 128), np.float32)
    LTZ[0:64, 0:64] = tri
    LTZ[64:128, 64:128] = tri
    jt = np.concatenate([np.arange(1, 65), np.arange(-63, 1), np.arange(63, -1, -1)])
    JT = np.tile(jt.astype(np.float32), (128, 1))
    msk = np.ones((128, 32), np.float32)
    msk[:, 0] = 0.0
    return {"eye": np.eye(128, dtype=np.float32), "LTZ": LTZ, "JT": JT,
            "msk32": msk, "onesr": np.ones((1, 128), np.float32),
            "kohc": _kohc()}


_NC_CACHE = {}


def make_in_maps(i):
    cns = _consts()
    x = i["L0_attn_x"]
    base = {
        "logA": i["L0_attn_log_A"], "Bp": i["L0_attn_B_param"], "Cp": i["L0_attn_C_param"],
        "logdt": i["L0_attn_log_dt"],
        "Wout": i["L0_attn_W_out"], "bout": i["L0_attn_b_out"],
        "gw": i["L0_ffn_gate_w"],
        "We1": i["L0_ffn_We1"], "We2": i["L0_ffn_We2"], "We3": i["L0_ffn_We3"],
        "be1": i["L0_ffn_be1"], "be2": i["L0_ffn_be2"], "be3": i["L0_ffn_be3"],
        "ws1": i["L0_ffn_ws1"], "ws2": i["L0_ffn_ws2"], "ws3": i["L0_ffn_ws3"],
        "bs1": i["L0_ffn_bs1"], "bs2": i["L0_ffn_bs2"], "bs3": i["L0_ffn_bs3"],
        "gamb": np.tile(i["L0_attn_ln_gamma"][None, :], (128, 1)),
        "bcolh": _bcolh(i), "dtch": i["L0_attn_log_dt"].reshape(8, 128).T.copy(),
        "betb": np.tile(i["L0_attn_ln_beta"][None, :], (128, 1)),
        "Dbc": np.tile(i["L0_attn_D_param"][None, :], (128, 1)),
        "gbb": np.tile(i["L0_ffn_gate_b"][None, :], (128, 1)),
    }
    base.update(cns)
    in_maps = []
    for k in range(8):
        b, h = k // 2, k % 2
        m = dict(base)
        m["xo"] = np.ascontiguousarray(x[b, h * TOK:(h + 1) * TOK, :])
        m["xg"] = np.ascontiguousarray(x[b, (1 - h) * TOK:(2 - h) * TOK, :])
        m["flagc"] = np.full((128, 1), float(h), np.float32)
        in_maps.append(m)
    return in_maps


def kernel(**inputs):
    i = {k: np.ascontiguousarray(np.asarray(vv, np.float32)) for k, vv in inputs.items()}
    if "nc" not in _NC_CACHE:
        _NC_CACHE["nc"] = build_nc(dbg=False)
    nc, _ = _NC_CACHE["nc"]
    in_maps = make_in_maps(i)
    res = run_bass_kernel_spmd(nc, in_maps, core_ids=list(range(8)))
    outp = np.empty((4, 4096, D), np.float32)
    for k in range(8):
        b, h = k // 2, k % 2
        outp[b, h * TOK:(h + 1) * TOK, :] = res.results[k]["out"]
    return outp


# version 09
# speedup vs baseline: 1.0093x; 1.0093x over previous
import numpy as np
from contextlib import ExitStack
import concourse.bass as bass
import concourse.mybir as mybir
from concourse.bass_utils import run_bass_kernel_spmd

F32 = mybir.dt.float32
AF = mybir.ActivationFunctionType
ALU = mybir.AluOpType
AXX = mybir.AxisListType.X

D, N, Q, TOK = 1024, 64, 64, 2048
NT = TOK // 128
NCH = TOK // Q
DB = 32
NB = D // DB
NDP = DB // 2
EPS = 1e-5
NE, IC = 16, 512

TB_ARGE, TB_CTIL, TB_CTAU = 0, 1536, 2560
TB_GRD0, TB_BTIL, TB_GRD1 = 3584, 3648, 4672
TB_BTT, TB_MZ = 4736, 6912
R3_COLS = 8960
F_WE1, F_WE3 = 0, 4096
F_WE2 = 12288


class _RecInstr:
    __slots__ = ("post", "auto")

    def __init__(self, auto=None):
        self.post = []
        self.auto = auto

    def then_inc(self, sem, amt):
        self.post.append((sem, amt))
        return self


_OWN = {"v": "sV", "s": "sS", "g": "sP"}


class _RecEng:

    def __init__(self, prog, key):
        self._prog = prog
        self._key = key

    def __getattr__(self, attr):
        prog, key = self._prog, self._key
        own = _OWN.get(key)

        def call(*a, **kw):
            tape = prog.tape[key]
            if own is not None and attr not in ("wait_ge", "dma_start"):
                tape.append(("wait_ge", (prog.sem[own], prog.cnt[own]), {},
                             _RecInstr()))
                ri = _RecInstr(auto=own)
                ri.post.append((prog.sem[own], 1))
                prog.cnt[own] += 1
                tape.append((attr, a, kw, ri))
                return ri
            ri = _RecInstr()
            tape.append((attr, a, kw, ri))
            return ri
        return call


class P:
    def __init__(self, nc, block, stack):
        self.nc = nc
        self.block = block
        self.tape = {k: [] for k in ("t", "v", "s", "g", "y")}
        self.sem, self.cnt = {}, {}
        for name in ("dC", "dX0", "dX1", "dX2", "dX3", "dWo", "dw1", "dw3",
                     "dw2", "dM", "dK", "dO", "sT", "sV", "sS", "sP"):
            self.sem[name] = stack.enter_context(nc.semaphore(name))
            self.cnt[name] = 0
        self.t = _RecEng(self, "t")
        self.v = _RecEng(self, "v")
        self.s = _RecEng(self, "s")
        self.g = _RecEng(self, "g")
        self.y = _RecEng(self, "y")

    def inc(self, instr, name, amt=1):
        if instr.auto == name:
            return self.cnt[name]
        self.cnt[name] += amt
        instr.then_inc(self.sem[name], amt)
        return self.cnt[name]

    def w(self, eng, name, val=None):
        vv = self.cnt[name] if val is None else val
        if vv > 0:
            eng.wait_ge(self.sem[name], vv)

    def dma(self, q, o, i, name="dC", **kw):
        return self.inc(q.dma_start(out=o, in_=i, **kw), name, 16)

    def finalize(self, frac=1.0):
        secs = {"t": self.block.tensor, "v": self.block.vector,
                "s": self.block.scalar, "g": self.block.gpsimd,
                "y": self.block.sync}

        def replay(tape):
            tape = tape[:int(len(tape) * frac)]
            def body(e):
                for attr, a, kw, ri in tape:
                    ins = getattr(e, attr)(*a, **kw)
                    for sem, amt in ri.post:
                        ins.then_inc(sem, amt)
            return body

        for k, sec in secs.items():
            sec(replay(self.tape[k]))


def build_nc(dbg=False, frac=1.0, stop_after=None):
    nc = bass.Bass("TRN2", target_bir_lowering=False)
    ain = lambda n_, sh: nc.dram_tensor(n_, sh, F32, kind="ExternalInput").ap()
    xo = ain("xo", [TOK, D]); xg = ain("xg", [TOK, D])
    logA = ain("logA", [D, N]); Bp = ain("Bp", [D, N]); Cp = ain("Cp", [D, N])
    logdt = ain("logdt", [D])
    Wout = ain("Wout", [D, D]); bout = ain("bout", [D])
    gw = ain("gw", [NE, D])
    We1 = ain("We1", [NE, D, IC]); We2 = ain("We2", [NE, IC, D]); We3 = ain("We3", [NE, D, IC])
    be1 = ain("be1", [NE, IC]); be2 = ain("be2", [NE, D]); be3 = ain("be3", [NE, IC])
    ws1 = ain("ws1", [D, D]); ws2 = ain("ws2", [D, D]); ws3 = ain("ws3", [D, D])
    bs1 = ain("bs1", [D]); bs2 = ain("bs2", [D]); bs3 = ain("bs3", [D])
    eye = ain("eye", [128, 128]); LTZ = ain("LTZ", [128, 128])
    JT = ain("JT", [128, 192]); msk32 = ain("msk32", [128, 32])
    flagc = ain("flagc", [128, 1]); onesr = ain("onesr", [1, 128])
    kohc = ain("kohc", [16, 2048])
    gamb = ain("gamb", [128, D]); betb = ain("betb", [128, D])
    Dbc = ain("Dbc", [128, D]); gbb = ain("gbb", [128, NE])
    bcolh = ain("bcolh", [128, 160]); dtch = ain("dtch", [128, 8])
    out = nc.dram_tensor("out", [TOK, D], F32, kind="ExternalOutput").ap()
    x1sp = nc.dram_tensor("x1sp", [TOK, D], F32,
                          kind="ExternalOutput" if dbg else "Internal").ap()
    dbg_t = {}
    if dbg:
        for nm, sh in (("dbg_u", [128, 32768]), ("dbg_y", [128, 16384]),
                       ("dbg_cm", [128, NT * NE]), ("dbg_xf", [128, 16384])):
            dbg_t[nm] = nc.dram_tensor(nm, sh, F32, kind="ExternalOutput").ap()

    stack = ExitStack()
    with stack:
        sb = lambda nm, cols, rows=128: stack.enter_context(
            nc.sbuf_tensor(nm, [rows, cols], F32))
        R1 = sb("R1", 16384)
        R2 = sb("R2", 16384)
        R3 = sb("R3", R3_COLS)
        SCA = sb("SCA", 2048)
        SCB = sb("SCB", 2048)
        KE = sb("KE", 128); KL = sb("KL", 128); KJ = sb("KJ", 192)
        KM = sb("KM", 32); KF = sb("KF", 1)
        KGB = sb("KGB", NE)
        mdA = sb("mdA", 512); Cpk = sb("Cpk", 512); dBpk = sb("dBpk", 512)
        dAQ = sb("dAQ", 512)
        gwT = sb("gwT", 8 * NE)
        STA = sb("STA", 448)
        bcol = sb("bcol", 160)
        ctm = sb("ctm", NT * NE)
        gsc = sb("gsc", 1024)
        hsb = sb("hsb", 1024)
        CFM_C, BE2_C = 9216, 11264
        PS = [stack.enter_context(nc.psum_tensor("ps%d" % i, [128, 512], F32))
              for i in range(8)]
        block = stack.enter_context(nc.Block())
        p = P(nc, block, stack)
        t, v, s, g, y = p.t, p.v, p.s, p.g, p.y

        bankfree = [("sT", 0)] * 8

        def bank_wait(i):
            sem, val = bankfree[i]
            p.w(t, sem, val)

        S_SUM, S_SQ, S_MU, S_V1, S_R1, S_S, S_Q1, S_MR = 0, 32, 64, 96, 128, 160, 192, 224
        S_SUM3, S_MU3, S_R33 = 256, 272, 288
        S_T0, S_T1, S_T2, S_DT = 320, 352, 384, 416

        p.dma(y, KE[:, :], eye); p.dma(y, KL[:, :], LTZ); p.dma(y, KJ[:, :], JT)
        p.dma(y, KM[:, :], msk32); p.dma(y, KF[:, :], flagc)
        p.dma(y, hsb[:, :], betb)
        p.dma(y, KGB[:, :], gbb)
        p.dma(y, bcol[:, :], bcolh)
        PRE_G, PRE_A, PRE_C, PRE_B = 0, 64, 576, 1088
        for c in range(8):
            p.dma(y, SCB[:, PRE_A + c * 64:PRE_A + c * 64 + 64], logA[c * 128:(c + 1) * 128, :])
            p.dma(y, SCB[:, PRE_C + c * 64:PRE_C + c * 64 + 64], Cp[c * 128:(c + 1) * 128, :])
            p.dma(y, SCB[:, PRE_B + c * 64:PRE_B + c * 64 + 64], Bp[c * 128:(c + 1) * 128, :])
        p.dma(y, STA[:, S_DT:S_DT + 8], dtch)
        p.dma(y, gsc[0:16, :], gw)
        cW_pre = p.cnt["dC"]

        p.w(v, "dC", cW_pre)
        v.memset(SCB[:, PRE_G:PRE_G + 64], 0.0)
        p.w(s, "dC", cW_pre)
        s.activation(STA[:, S_DT:S_DT + 8], STA[:, S_DT:S_DT + 8], AF.Exp)
        sx = p.inc(s.activation(SCB[:, PRE_A:PRE_A + 512], SCB[:, PRE_A:PRE_A + 512],
                                AF.Exp), "sS")
        p.w(v, "sS", sx)
        for c in range(8):
            v.tensor_scalar(SCB[:, PRE_A + c * 64:PRE_A + (c + 1) * 64],
                            SCB[:, PRE_A + c * 64:PRE_A + (c + 1) * 64],
                            STA[:, S_DT + c:S_DT + c + 1], -1.0, ALU.mult, ALU.mult)
            lastv = v.tensor_scalar_mul(SCB[:, PRE_B + c * 64:PRE_B + (c + 1) * 64],
                                        SCB[:, PRE_B + c * 64:PRE_B + (c + 1) * 64],
                                        STA[:, S_DT + c:S_DT + c + 1])
        vPre = p.inc(lastv, "sV")

        p.w(t, "sV", vPre)
        jobs = [(base, dst, c) for c in range(8)
                for base, dst in ((PRE_A, mdA), (PRE_C, Cpk), (PRE_B, dBpk))]
        ev_hist = []
        for ji, (base, dst, c) in enumerate(jobs):
            bank = PS[ji % 2]
            if ji >= 2:
                p.w(t, "sV", ev_hist[ji - 2])
            off = base + c * 64
            t.matmul(bank[0:64, 0:128], SCB[:, off:off + 64], KE[:, :],
                     is_transpose=True, start=True, stop=False, skip_group_check=True)
            ins = t.matmul(bank[:, 128:256],
                           bass.AP(SCB, off - 64, [[2048, 128], [1, 128]]), KE[:, :],
                           is_transpose=True, start=True, stop=True, skip_group_check=True)
            tv = p.inc(ins, "sT")
            p.w(v, "sT", tv)
            v.tensor_copy(dst[0:64, c * 64:(c + 1) * 64],
                          bass.AP(bank, 0, [[512, 64], [2, 64]]))
            ev = p.inc(v.tensor_copy(dst[64:128, c * 64:(c + 1) * 64],
                                     bass.AP(bank, 64 * 512 + 128 + 1, [[512, 64], [2, 64]])),
                       "sV")
            ev_hist.append(ev)
            bankfree[ji % 2] = ("sV", ev)
        vPk = p.cnt["sV"]
        p.w(v, "sV", vPk)
        vq = p.inc(v.tensor_scalar_mul(dAQ[:, :], mdA[:, :], 64.0), "sV")
        p.w(s, "sV", vq)
        sdq = p.inc(s.activation(dAQ[:, :], dAQ[:, :], AF.Exp), "sS")

        p.w(t, "dC", cW_pre)
        bank_wait(2)
        for c in range(8):
            lastt = t.matmul(PS[2][:, c * 16:(c + 1) * 16], gsc[0:16, c * 128:(c + 1) * 128],
                             KE[0:16, 0:16], is_transpose=True, start=(c == 0), stop=(c == 7),
                             skip_group_check=True)
        tgw = p.inc(lastt, "sT")
        p.w(v, "sT", tgw)
        vgw = p.inc(v.tensor_copy(gwT[:, :], PS[2][:, 0:128]), "sV")
        bankfree[2] = ("sV", vgw)
        p.w(y, "sV", vgw)
        dGam = p.dma(y, gsc[:, 0:1024], gamb, "dM")
        tpk_done = p.cnt["sT"]

        def x_src(j):
            src = xo if j < NT else xg
            jj = j % NT
            return src[jj * 128:(jj + 1) * 128, :]

        sS0, sV0 = p.cnt["sS"], p.cnt["sV"]
        ubufs = [(SCA, 0), (SCA, 1024), (SCB, 0), (SCB, 1024)]
        xd = {}
        vcons = {}
        for g0 in range(0, 2 * NT, 2):
            for k in range(2):
                j = g0 + k
                ten, off = ubufs[j % 4]
                q = y if j % 2 == 0 else s
                if j < 4 and j % 4 >= 2:
                    p.w(q, "sT", tpk_done)
                if j >= 4:
                    p.w(q, "sV", vcons[j - 4])
                xd[j] = p.dma(q, ten[:, off:off + 1024], x_src(j),
                              "dX%d" % (j % 4))
            for k in range(2):
                j = g0 + k
                ten, off = ubufs[j % 4]
                buf = ten[:, off:off + 1024]
                p.w(s, "dX%d" % (j % 4), xd[j])
                p.inc(s.activation(R3[:, 0:1024], buf, AF.Square,
                                   accum_out=STA[:, S_SQ + j:S_SQ + j + 1]), "sS")
                p.inc(s.activation(R3[:, 0:1024], buf, AF.Copy,
                                   accum_out=STA[:, S_SUM + j:S_SUM + j + 1]), "sS")
            st4 = lambda c0_: STA[:, c0_ + g0:c0_ + g0 + 2]
            p.w(v, "sS", p.cnt["sS"])
            v.tensor_scalar_mul(st4(S_MU), st4(S_SUM), 1.0 / D)
            v.tensor_tensor(st4(S_V1), st4(S_MU), st4(S_MU), ALU.mult)
            v.tensor_scalar_mul(st4(S_T0), st4(S_SQ), 1.0 / D)
            v.tensor_tensor(st4(S_V1), st4(S_T0), st4(S_V1), ALU.subtract)
            v.tensor_scalar_add(st4(S_T0), st4(S_V1), EPS)
            vx = p.inc(v.reciprocal(st4(S_T1), st4(S_T0)), "sV")
            p.w(s, "sV", vx)
            sx = p.inc(s.activation(st4(S_R1), st4(S_T1), AF.Sqrt), "sS")
            p.w(v, "sS", sx)
            v.tensor_tensor(st4(S_T2), st4(S_R1), st4(S_R1), ALU.mult)
            v.tensor_tensor(st4(S_T2), st4(S_T0), st4(S_T2), ALU.mult)
            v.tensor_scalar(st4(S_T2), st4(S_T2), -0.5, 1.5, ALU.mult, ALU.add)
            v.tensor_tensor(st4(S_R1), st4(S_R1), st4(S_T2), ALU.mult)
            v.tensor_tensor(st4(S_T2), st4(S_V1), st4(S_T1), ALU.mult)
            v.tensor_scalar_add(st4(S_T2), st4(S_T2), EPS)
            vx = p.inc(v.reciprocal(st4(S_T1), st4(S_T2)), "sV")
            p.w(s, "sV", vx)
            sx = p.inc(s.activation(st4(S_T1), st4(S_T1), AF.Sqrt), "sS")
            p.w(v, "sS", sx)
            v.tensor_tensor(st4(S_T0), st4(S_T1), st4(S_T1), ALU.mult)
            v.tensor_tensor(st4(S_T0), st4(S_T2), st4(S_T0), ALU.mult)
            v.tensor_scalar(st4(S_T0), st4(S_T0), -0.5, 1.5, ALU.mult, ALU.add)
            v.tensor_tensor(st4(S_T1), st4(S_T1), st4(S_T0), ALU.mult)
            v.tensor_tensor(st4(S_S), st4(S_R1), st4(S_T1), ALU.mult)
            v.tensor_tensor(st4(S_MR), st4(S_MU), st4(S_R1), ALU.mult)
            p.inc(v.tensor_scalar_add(st4(S_Q1), st4(S_R1), 1.0), "sV")
            for k in range(2):
                j = g0 + k
                ten, off = ubufs[j % 4]
                buf = ten[:, off:off + 1024]
                dstR = R1 if j < NT else R2
                dc = (j % NT) * 1024
                p.w(v, "dM", dGam)
                v.tensor_scalar(dstR[:, dc:dc + 1024], buf,
                                STA[:, S_MU + j:S_MU + j + 1],
                                STA[:, S_S + j:S_S + j + 1],
                                ALU.subtract, ALU.mult)
                vv = p.inc(v.tensor_tensor(dstR[:, dc:dc + 1024],
                                           dstR[:, dc:dc + 1024],
                                           gsc[:, 0:1024], ALU.mult), "sV")
                vcons[j] = vv
                p.w(g, "sV", vv)
                p.w(g, "dC", cW_pre)
                p.inc(g.tensor_tensor(dstR[:, dc:dc + 1024], dstR[:, dc:dc + 1024],
                                      hsb[:, :], ALU.add), "sP")
        sP_u = p.cnt["sP"]

        if dbg:
            p.w(y, "sP", sP_u)
            p.dma(y, dbg_t["dbg_u"][:, 0:16384], R1[:, :], "dO")
            p.dma(y, dbg_t["dbg_u"][:, 16384:32768], R2[:, :], "dO")
        dbg_dO_u = p.cnt["dO"]

        if stop_after == "U":
            p.w(y, "sP", sP_u)
            p.dma(y, out[0:128, :], R1[:, 0:1024], "dO")
            p.w(y, "dO", p.cnt["dO"])
            p.finalize(frac)
            return nc, dbg_t
        p.w(y, "sP", sP_u)
        dDb = p.dma(y, hsb[:, :], Dbc, "dM")

        nbl = NB
        if stop_after and stop_after.startswith("T") and stop_after[1:].isdigit():
            nbl = int(stop_after[1:])
        tT_free = 0
        intra_done = [0, 0]
        vsc_prev = 0
        for b in range(nbl):
            c0, dp0 = b * DB, b * NDP
            eng, sem_e = (v, "sV") if b % 2 == 0 else (g, "sP")
            for hh in range(2):
                dph = dp0 + hh * 8
                p.w(eng, "sT", tT_free)
                if b % 2 == 1:
                    p.w(eng, "sV", vPk)
                p.w(eng, "dC", cW_pre)
                if hh == 1 and b % 2 == 1:
                    p.w(eng, "sV", p.cnt["sV"])
                in0 = bass.AP(mdA, dph, [[512, 128], [1, 8], [0, 192]])
                in1 = bass.AP(KJ, 0, [[192, 128], [0, 8], [1, 192]])
                ae = p.inc(eng.tensor_tensor(R3[:, TB_ARGE:TB_ARGE + 8 * 192], in0, in1,
                                             ALU.mult), sem_e)
                p.w(s, sem_e, ae)
                se = p.inc(s.activation(R3[:, TB_ARGE:TB_ARGE + 8 * 192],
                                        R3[:, TB_ARGE:TB_ARGE + 8 * 192], AF.Exp), "sS")
                p.w(v, "sS", max(se, sdq))
                p.w(v, "sT", tT_free)
                eap = lambda o: bass.AP(R3, TB_ARGE + o, [[R3_COLS, 128], [192, 8], [1, 64]])
                cb = bass.AP(Cpk, dph, [[512, 128], [1, 8], [0, 64]])
                db_ = bass.AP(dBpk, dph, [[512, 128], [1, 8], [0, 64]])
                ho = hh * 512
                v.tensor_tensor(R3[:, TB_CTIL + ho:TB_CTIL + ho + 512], eap(0), cb, ALU.mult)
                v.tensor_tensor(R3[:, TB_CTAU + ho:TB_CTAU + ho + 512], eap(64), cb, ALU.mult)
                vtb = p.inc(v.tensor_tensor(R3[:, TB_BTIL + ho:TB_BTIL + ho + 512],
                                            eap(128), db_, ALU.mult), "sV")
            if b == 0:
                v.memset(R3[:, TB_GRD0:TB_GRD0 + 64], 0.0)
                vtb = p.inc(v.memset(R3[:, TB_GRD1:TB_GRD1 + 64], 0.0), "sV")
            d0a = bass.AP(dAQ, dp0, [[512, 128], [1, NDP], [0, 32]])
            d0b = bass.AP(KM, 0, [[32, 128], [0, NDP], [1, 32]])
            p.w(g, "sV", vsc_prev)
            p.w(g, "sS", sdq)
            vd0 = p.inc(g.tensor_tensor(SCA[:, 1024:1536], d0a, d0b, ALU.mult), "sP")
            if b == 0:
                p.w(t, "sP", sP_u)
                p.w(t, "dM", dDb)

            if stop_after == "TA":
                p.w(y, "sV", p.cnt["sV"]); p.w(y, "sS", p.cnt["sS"])
                p.dma(y, out[0:128, :], R3[:, 0:1024], "dO")
                p.w(y, "dO", p.cnt["dO"])
                p.finalize(frac)
                return nc, dbg_t
            p.w(t, "sV", vtb)
            for bank_i, w0 in ((0, 0), (1, 4), (2, 8), (3, 12)):
                bank_wait(bank_i)
                for k in range(4):
                    wdx = w0 + k
                    src = bass.AP(R3, TB_BTIL + wdx * 64 - 64, [[R3_COLS, 128], [1, 128]])
                    ins = t.matmul(PS[bank_i][:, k * 128:(k + 1) * 128], src, KE[:, :],
                                   is_transpose=True, start=(k == 0), stop=(k == 3),
                                   skip_group_check=True)
                tw = p.inc(ins, "sT")
                p.w(s, "sT", tw)
                ev = p.inc(s.copy(R3[:, TB_BTT + w0 * 128:TB_BTT + (w0 + 4) * 128],
                                  PS[bank_i][:, :]), "sS")
                bankfree[bank_i] = ("sS", ev)
            bank_wait(0)
            src = bass.AP(R3, TB_BTIL + 16 * 64 - 64, [[R3_COLS, 128], [1, 128]])
            tw = p.inc(t.matmul(PS[0][:, 0:128], src, KE[:, :], is_transpose=True,
                                start=True, stop=True, skip_group_check=True), "sT")
            p.w(s, "sT", tw)
            ev = p.inc(s.copy(R3[:, TB_BTT + 16 * 128:TB_BTT + 17 * 128], PS[0][:, 0:128]),
                       "sS")
            bankfree[0] = ("sS", ev)
            sS_win = p.cnt["sS"]
            if stop_after == "TB":
                p.w(y, "sS", p.cnt["sS"])
                p.dma(y, out[0:128, :], R3[:, 0:1024], "dO")
                p.w(y, "dO", p.cnt["dO"])
                p.finalize(frac)
                return nc, dbg_t

            def mz_build(chh):
                for par in (0, 1):
                    for sub in range(2):
                        bank_i = par * 2 + sub
                        bank_wait(bank_i)
                        first = {0: True, 64: True}
                        for k in range(4):
                            kk = sub * 8 + k * 2 + par
                            ch = chh * 16 + kk
                            dpi = ch // 2
                            bt = R3[par * 64:par * 64 + 64,
                                    TB_BTIL + dpi * 64:TB_BTIL + dpi * 64 + 64]
                            ct = R3[par * 64:par * 64 + 64,
                                    TB_CTAU + dpi * 64:TB_CTAU + dpi * 64 + 64]
                            zz = KL[par * 64:par * 64 + 64,
                                    (1 - par) * 64:(1 - par) * 64 + 64]
                            for colg, half in ((0, 0), (64, 1)):
                                stf = first[colg]; first[colg] = False
                                dcol = k * 128 + half * 64
                                zcol = k * 128 + (1 - half) * 64
                                t.matmul(PS[bank_i][half * 64:half * 64 + 64, dcol:dcol + 64],
                                         bt, ct, start=stf, stop=False,
                                         tile_position=(par * 64, colg),
                                         skip_group_check=True)
                                lmz = t.matmul(
                                    PS[bank_i][half * 64:half * 64 + 64, zcol:zcol + 64],
                                    bt, zz, start=False, stop=(k == 3),
                                    tile_position=(par * 64, colg), skip_group_check=True)
                        tmz = p.inc(lmz, "sT")
                        p.w(v, "sT", tmz)
                        p.w(v, "sT", intra_done[chh])
                        ltzb = bass.AP(KL, 0, [[128, 128], [0, 4], [1, 128]])
                        if chh == 0:
                            dst = bass.AP(R3, TB_MZ + (sub * 8 + par) * 128,
                                          [[R3_COLS, 128], [256, 4], [1, 128]])
                        else:
                            dst = bass.AP(SCB, (sub * 8 + par) * 128,
                                          [[2048, 128], [256, 4], [1, 128]])
                        ev_ = p.inc(v.tensor_tensor(dst, PS[bank_i][:, :], ltzb, ALU.mult),
                                    "sV")
                        bankfree[bank_i] = ("sV", ev_)
                return p.cnt["sV"]

            sV_mz = mz_build(0)
            if stop_after == "TC":
                p.w(y, "sV", p.cnt["sV"])
                p.dma(y, out[0:128, :], R3[:, 0:1024], "dO")
                p.w(y, "dO", p.cnt["dO"])
                p.finalize(frac)
                return nc, dbg_t

            p.w(t, "sS", sS_win)
            bank_wait(6); bank_wait(7)
            started = {}
            last_ins = None
            for ch in range(DB):
                dpi, par = ch // 2, ch % 2
                for cpar in (0, 1):
                    bank_i = 6 + cpar
                    if cpar == 0:
                        stat = R3[0:64, TB_BTT + (dpi + 1) * 128 + par * 64:
                                  TB_BTT + (dpi + 1) * 128 + par * 64 + 64]
                    else:
                        stat = R3[64:128, TB_BTT + dpi * 128 + par * 64:
                                  TB_BTT + dpi * 128 + par * 64 + 64]
                    for half, uten in ((0, R2), (1, R1)):
                        ntile = NT - 1 if (half == 1 and cpar == 1) else NT
                        mov = bass.AP(uten, cpar * 64 * 16384 + c0 + ch,
                                      [[16384, 64], [1024, ntile]])
                        ocol = dpi * 32 + (0 if half == 0 else 16)
                        dst = bass.AP(PS[bank_i], par * 64 * 512 + ocol,
                                      [[512, 64], [1, ntile]])
                        key = (bank_i, par)
                        stf = key not in started
                        started[key] = True
                        last_ins = t.matmul(dst, stat, mov, start=stf, stop=False,
                                            tile_position=(cpar * 64, par * 64),
                                            skip_group_check=True)
            th = p.inc(last_ins, "sT")
            if stop_after == "TD0":
                p.w(y, "sT", th)
                p.dma(y, out[0:128, :], R3[:, 0:1024], "dO")
                p.w(y, "dO", p.cnt["dO"])
                p.finalize(frac)
                return nc, dbg_t
            p.w(s, "sT", th); p.w(v, "sT", th)
            s.copy(bass.AP(SCA, 0, [[2048, 128], [32, NDP], [2, 16]]),
                   bass.AP(PS[6], 0, [[512, 128], [32, NDP], [1, 16]]))
            ev6 = p.inc(s.copy(bass.AP(SCA, 512 + 1, [[2048, 128], [32, NDP], [2, 16]]),
                               bass.AP(PS[6], 16, [[512, 128], [32, NDP], [1, 16]])), "sS")
            v.tensor_copy(bass.AP(SCA, 1, [[2048, 128], [32, NDP], [2, 16]]),
                          bass.AP(PS[7], 0, [[512, 128], [32, NDP], [1, 16]]))
            ev = p.inc(v.tensor_copy(
                bass.AP(SCA, 512 + 2, [[2048, 128], [32, NDP], [2, 15]]),
                bass.AP(PS[7], 16, [[512, 128], [32, NDP], [1, 15]])), "sV")
            bankfree[6] = ("sS", ev6); bankfree[7] = ("sV", ev)
            if stop_after == "TDa":
                p.w(y, "sV", p.cnt["sV"]); p.w(y, "sS", p.cnt["sS"])
                p.dma(y, out[0:128, :], SCA[:, 0:1024], "dO")
                p.w(y, "dO", p.cnt["dO"])
                p.finalize(frac)
                return nc, dbg_t
            p.w(v, "sS", p.cnt["sS"])
            p.w(v, "sP", vd0)
            v.tensor_tensor_scan(SCA[:, 0:512], SCA[:, 1024:1536], SCA[:, 0:512],
                                 0.0, ALU.mult, ALU.add)
            v.tensor_scalar_mul(bass.AP(SCA, 512, [[2048, 128], [32, NDP]]),
                                bass.AP(SCA, 31, [[2048, 128], [32, NDP]]), KF[:, 0:1])
            vsc = p.inc(v.tensor_tensor_scan(SCA[:, 512:1024], SCA[:, 1024:1536],
                                             SCA[:, 512:1024], 0.0, ALU.mult, ALU.add),
                        "sV")
            vsc_prev = vsc
            if stop_after == "TDb":
                p.w(y, "sV", p.cnt["sV"])
                p.dma(y, out[0:128, :], SCA[:, 0:1024], "dO")
                p.w(y, "dO", p.cnt["dO"])
                p.finalize(frac)
                return nc, dbg_t

            if stop_after == "TD":
                p.w(y, "sV", p.cnt["sV"]); p.w(y, "sS", p.cnt["sS"])
                p.dma(y, out[0:128, :], SCA[:, 0:1024], "dO")
                p.w(y, "dO", p.cnt["dO"])
                p.finalize(frac)
                return nc, dbg_t
            started5 = {}

            def intra_half(chh, mzv):
                p.w(t, "sV", max(vsc, mzv))
                if chh == 0:
                    bank_wait(4); bank_wait(5)
                last = None
                for kk in range(16):
                    ch = chh * 16 + kk
                    dpi, par = ch // 2, ch % 2
                    mov_e = bass.AP(R1, c0 + ch, [[16384, 64], [1024, NT]])
                    mov_o = bass.AP(R1, 64 * 16384 + c0 + ch, [[16384, 64], [1024, NT]])
                    if chh == 0:
                        mzA = R3[0:64, TB_MZ + kk * 128:TB_MZ + kk * 128 + 64]
                        mzB = R3[64:128, TB_MZ + kk * 128 + 64:TB_MZ + kk * 128 + 128]
                    else:
                        mzA = SCB[0:64, kk * 128:kk * 128 + 64]
                        mzB = SCB[64:128, kk * 128 + 64:kk * 128 + 128]
                    stf = (4, 0) not in started5; started5[(4, 0)] = True
                    t.matmul(bass.AP(PS[4], ch * 16, [[512, 64], [1, NT]]),
                             mzA, mov_e, start=stf, stop=False,
                             tile_position=(0, 0), skip_group_check=True)
                    stf = (5, 1) not in started5; started5[(5, 1)] = True
                    t.matmul(bass.AP(PS[5], 64 * 512 + ch * 16, [[512, 64], [1, NT]]),
                             mzB, mov_o, start=stf, stop=False,
                             tile_position=(64, 64), skip_group_check=True)
                    if par == 0:
                        stf = (5, 0) not in started5; started5[(5, 0)] = True
                        t.matmul(bass.AP(PS[5], ch * 16, [[512, 64], [1, NT]]),
                                 KL[64:128, 64:128], KL[64:128, 0:NT],
                                 start=stf, stop=False, tile_position=(64, 0),
                                 skip_group_check=True)
                    else:
                        stf = (4, 1) not in started5; started5[(4, 1)] = True
                        t.matmul(bass.AP(PS[4], 64 * 512 + ch * 16, [[512, 64], [1, NT]]),
                                 KL[0:64, 0:64], KL[0:64, 64:64 + NT],
                                 start=stf, stop=False, tile_position=(0, 64),
                                 skip_group_check=True)
                    for cpar in (0, 1):
                        bank_i = 4 + par
                        ct = R3[par * 64:par * 64 + 64,
                                TB_CTIL + dpi * 64:TB_CTIL + dpi * 64 + 64]
                        mov_h = bass.AP(SCA, par * 64 * 2048 + 512 + dpi * 32 + cpar,
                                        [[2048, 64], [2, NT]])
                        key = (bank_i, cpar)
                        stf = key not in started5
                        started5[key] = True
                        dst = bass.AP(PS[bank_i], cpar * 64 * 512 + ch * 16,
                                      [[512, 64], [1, NT]])
                        last = t.matmul(dst, ct, mov_h, start=stf, stop=False,
                                        tile_position=(par * 64, cpar * 64),
                                        skip_group_check=True)
                return p.inc(last, "sT")

            sV_mz2 = mz_build(1)
            ti0 = intra_half(0, sV_mz)
            intra_done[0] = ti0
            ti = intra_half(1, sV_mz2)
            intra_done[1] = ti
            tT_free = ti
            p.w(g, "sT", ti)
            p.w(g, "dM", dDb)
            if b == 0:
                p.w(g, "dO", dbg_dO_u)
            ucols = bass.AP(R1, c0, [[16384, 128], [1024, NT], [1, DB]])
            dbb = bass.AP(hsb, c0, [[1024, 128], [0, NT], [1, DB]])
            g.tensor_tensor(ucols, ucols, dbb, ALU.mult)
            ytmp = SCA[:, 1536:2048]
            src4 = bass.AP(PS[4], 0, [[512, 128], [1, 16], [16, DB]])
            src5 = bass.AP(PS[5], 0, [[512, 128], [1, 16], [16, DB]])
            tmpd = bass.AP(SCA, 1536, [[2048, 128], [1, 16], [16, DB]])
            p.w(v, "sT", ti)
            p.w(v, "sP", p.cnt["sP"])
            v.scalar_tensor_tensor(tmpd, src4, 1.0, ucols, ALU.mult, ALU.add)
            ev = p.inc(v.scalar_tensor_tensor(ucols, src5, 1.0, tmpd, ALU.mult, ALU.add),
                       "sV")
            bankfree[4] = ("sV", ev)
            bankfree[5] = ("sV", ev)
            started5.clear()
        sV_y = p.cnt["sV"]
        sP_y = p.cnt["sP"]

        if dbg:
            p.w(y, "sV", sV_y)
            p.w(y, "sP", sP_y)
            p.dma(y, dbg_t["dbg_y"][:, :], R1[:, :], "dO")
        dbg_dO_y = p.cnt["dO"]
        if stop_after and stop_after.startswith("T") and stop_after[1:].isdigit():
            p.w(y, "sV", p.cnt["sV"])
            p.dma(y, out[0:128, :], R1[:, 0:1024], "dO")
            p.w(y, "dO", p.cnt["dO"])
            p.finalize(frac)
            return nc, dbg_t

        p.w(t, "sV", sV_y)
        p.w(t, "sP", sP_y)
        blocks = [(ti_, dcc) for dcc in range(8) for ti_ in range(NT)]
        for bi in range(0, len(blocks), 4):
            bank_i = (bi // 4) % 2
            bank_wait(bank_i)
            for k in range(4):
                ti_, dcc = blocks[bi + k]
                src = bass.AP(R1, ti_ * 1024 + dcc * 128, [[16384, 128], [1, 128]])
                ins = t.matmul(PS[bank_i][:, k * 128:(k + 1) * 128], src, KE[:, :],
                               is_transpose=True, start=(k == 0), stop=(k == 3),
                               skip_group_check=True)
            tw = p.inc(ins, "sT")
            p.w(v, "sT", tw)
            ti0, dc0 = blocks[bi]
            ev = p.inc(v.tensor_copy(
                R2[:, dc0 * 2048 + ti0 * 128:dc0 * 2048 + ti0 * 128 + 512],
                PS[bank_i][:, :]), "sV")
            bankfree[bank_i] = ("sV", ev)
        sV_yfm = p.cnt["sV"]

        last_attn_mm = 0
        for ec in range(8):
            if ec == 0:
                p.w(y, "sT", tT_free)
            else:
                p.w(y, "sT", wt_cons)
            if ec == 0:
                p.w(y, "sV", sV_y)
                p.w(y, "sP", sP_y)
            dwr = p.dma(y, SCB[:, 0:1024], Wout[ec * 128:(ec + 1) * 128, :], "dWo")
            p.w(t, "dWo", dwr)
            for hb in range(2):
                bank_i = 6 + hb
                bank_wait(bank_i)
                for k in range(4):
                    ins = t.matmul(PS[bank_i][:, k * 128:(k + 1) * 128],
                                   SCB[:, hb * 512 + k * 128:hb * 512 + (k + 1) * 128],
                                   KE[:, :], is_transpose=True, start=(k == 0),
                                   stop=(k == 3), skip_group_check=True)
                tw = p.inc(ins, "sT")
                wt_cons = tw
                p.w(v, "sT", tw)
                if ec > 0:
                    p.w(v, "sT", last_attn_mm)
                ev = p.inc(v.tensor_copy(hsb[:, hb * 512:hb * 512 + 512], PS[bank_i][:, :]),
                           "sV")
                bankfree[bank_i] = ("sV", ev)
            vwt = p.cnt["sV"]
            p.w(t, "sV", vwt)
            p.w(t, "sV", sV_yfm)
            for tt in range(4):
                bank_i = 2 + (tt % 2)
                bank_wait(bank_i)
                for dcc in range(8):
                    stat = hsb[:, dcc * 128:(dcc + 1) * 128]
                    mov = R2[:, dcc * 2048 + tt * 512:dcc * 2048 + tt * 512 + 512]
                    ins = t.matmul(PS[bank_i][:, :], stat, mov, start=(dcc == 0),
                                   stop=(dcc == 7), skip_group_check=True)
                ta = p.inc(ins, "sT")
                last_attn_mm = ta
                p.w(s, "sT", ta)
                p.w(s, "dO", dbg_dO_y)
                ev = p.inc(s.activation(
                    R1[:, ec * 2048 + tt * 512:ec * 2048 + tt * 512 + 512],
                    PS[bank_i][:, :], AF.Identity, bias=bcol[:, ec:ec + 1], scale=1.0),
                    "sS")
                bankfree[bank_i] = ("sS", ev)
        sS_attn = p.cnt["sS"]

        p.w(t, "sS", sS_attn)
        v_ev2 = []
        spill_h = []
        scb_ok = False
        sV_usw = sV_y
        for ti_ in range(NT):
            buf = SCA[:, (ti_ % 2) * 1024:(ti_ % 2) * 1024 + 1024]
            qx = y if ti_ % 2 == 0 else s
            if ti_ < 2:
                p.w(qx, "sV", sV_usw)
                p.w(qx, "sP", sP_y)
            if ti_ >= 2:
                p.w(qx, "sV", v_ev2[ti_ - 2])
            dj = p.dma(qx, buf, x_src(ti_), "dX%d" % (ti_ % 2))
            p.w(v, "dX%d" % (ti_ % 2), dj)
            vr = p.inc(v.tensor_scalar(buf, buf, STA[:, S_Q1 + ti_:S_Q1 + ti_ + 1],
                                       STA[:, S_MR + ti_:S_MR + ti_ + 1],
                                       ALU.mult, ALU.subtract), "sV")
            for hb in range(2):
                bank_i = 6 + hb
                bank_wait(bank_i)
                for k in range(4):
                    ec = hb * 4 + k
                    src = bass.AP(R1, ec * 2048 + ti_ * 128, [[16384, 128], [1, 128]])
                    ins = t.matmul(PS[bank_i][:, k * 128:(k + 1) * 128], src, KE[:, :],
                                   is_transpose=True, start=(k == 0), stop=(k == 3),
                                   skip_group_check=True)
            ta = p.inc(ins, "sT")
            p.w(v, "sT", ta)
            p.w(v, "sV", vr)
            if not scb_ok:
                p.w(v, "sT", wt_cons)
                scb_ok = True
            if ti_ >= 2:
                p.w(v, "dO", spill_h[ti_ - 1])
            v.tensor_tensor(bass.AP(SCB, (ti_ % 2) * 1024, [[2048, 128], [1, 512]]),
                            PS[6][:, :],
                            bass.AP(SCA, (ti_ % 2) * 1024, [[2048, 128], [1, 512]]),
                            ALU.add)
            ev = p.inc(v.tensor_tensor(
                bass.AP(SCB, (ti_ % 2) * 1024 + 512, [[2048, 128], [1, 512]]),
                PS[7][:, :],
                bass.AP(SCA, (ti_ % 2) * 1024 + 512, [[2048, 128], [1, 512]]),
                ALU.add), "sV")
            bankfree[6] = ("sV", ev); bankfree[7] = ("sV", ev)
            v_ev2.append(ev)
            p.w(y, "sV", ev)
            spill_h.append(p.dma(y, x1sp[ti_ * 128:(ti_ + 1) * 128, :],
                                 bass.AP(SCB, (ti_ % 2) * 1024, [[2048, 128], [1, 1024]]),
                                 "dO"))
        dO_x1 = p.cnt["dO"]

        if stop_after == "W":
            p.w(y, "dO", p.cnt["dO"])
            p.finalize(frac)
            return nc, dbg_t

        sS3_0, sV3_0 = p.cnt["sS"], p.cnt["sV"]
        x1d = []
        for j in range(NT):
            buf = SCA[:, (j % 2) * 1024:(j % 2) * 1024 + 1024]
            if j == 0:
                p.w(y, "dO", dO_x1)
            if j < 2:
                p.w(y, "sV", v_ev2[NT - 1])
            if j >= 2:
                p.w(y, "sS", sS3_0 + (j - 1))
                p.w(y, "sV", sV3_0 + (j - 1))
            x1d.append(p.dma(y, buf, x1sp[j * 128:(j + 1) * 128, :], "dX%d" % (j % 2)))
        p.w(s, "dO", dO_x1)
        for j in range(NT):
            buf = SCA[:, (j % 2) * 1024:(j % 2) * 1024 + 1024]
            p.w(s, "dX%d" % (j % 2), x1d[j])
            p.inc(s.activation(SCB[:, 0:1024], buf, AF.Square,
                               accum_out=STA[:, S_SUM3 + j:S_SUM3 + j + 1]), "sS")
            p.w(v, "dX%d" % (j % 2), x1d[j])
            p.inc(v.reduce_sum(STA[:, S_MU3 + j:S_MU3 + j + 1], buf, axis=AXX), "sV")
        st16 = lambda c0_: STA[:, c0_:c0_ + 16]
        p.w(v, "sS", p.cnt["sS"]); p.w(v, "sV", p.cnt["sV"])
        v.tensor_scalar_mul(st16(S_MU3), st16(S_MU3), 1.0 / D)
        v.tensor_tensor(st16(S_R33), st16(S_MU3), st16(S_MU3), ALU.mult)
        v.tensor_scalar_mul(st16(S_T0), st16(S_SUM3), 1.0 / D)
        v.tensor_tensor(st16(S_R33), st16(S_T0), st16(S_R33), ALU.subtract)
        v.tensor_scalar_add(st16(S_T0), st16(S_R33), EPS)
        vx = p.inc(v.reciprocal(st16(S_T1), st16(S_T0)), "sV")
        p.w(s, "sV", vx)
        sx = p.inc(s.activation(st16(S_R33), st16(S_T1), AF.Sqrt), "sS")
        p.w(v, "sS", sx)
        v.tensor_tensor(st16(S_T2), st16(S_R33), st16(S_R33), ALU.mult)
        v.tensor_tensor(st16(S_T2), st16(S_T0), st16(S_T2), ALU.mult)
        v.tensor_scalar(st16(S_T2), st16(S_T2), -0.5, 1.5, ALU.mult, ALU.add)
        vf3 = p.inc(v.tensor_tensor(st16(S_R33), st16(S_R33), st16(S_T2), ALU.mult), "sV")

        vts_h = []
        tx_h = []
        sV_f3, sS_f3 = p.cnt["sV"], p.cnt["sS"]
        for j in range(NT):
            bufA = SCA[:, (j % 2) * 1024:(j % 2) * 1024 + 1024]
            if j < 2:
                p.w(y, "sV", sV_f3)
                p.w(y, "sS", sS_f3)
            if j >= 2:
                p.w(y, "sV", vts_h[j - 2])
            dj = p.dma(y, bufA, x1sp[j * 128:(j + 1) * 128, :], "dX%d" % (j % 2))
            p.w(v, "dX%d" % (j % 2), dj)
            p.w(v, "sV", vf3)
            if j >= 2:
                p.w(v, "sT", tx_h[j - 2])
            bufB = SCB[:, (j % 2) * 1024:(j % 2) * 1024 + 1024]
            vts = p.inc(v.tensor_scalar(bufB, bufA,
                                        STA[:, S_MU3 + j:S_MU3 + j + 1],
                                        STA[:, S_R33 + j:S_R33 + j + 1],
                                        ALU.subtract, ALU.mult), "sV")
            vts_h.append(vts)
            p.w(t, "sV", vts)
            for hb in range(2):
                bank_i = 6 + hb
                bank_wait(bank_i)
                for k in range(4):
                    dcc = hb * 4 + k
                    src = bass.AP(SCB, (j % 2) * 1024 + dcc * 128, [[2048, 128], [1, 128]])
                    ins = t.matmul(PS[bank_i][:, k * 128:(k + 1) * 128], src, KE[:, :],
                                   is_transpose=True, start=(k == 0), stop=(k == 3),
                                   skip_group_check=True)
            tx = p.inc(ins, "sT")
            tx_h.append(tx)
            p.w(s, "sT", tx)
            for hb in range(2):
                dc0 = hb * 4
                ev = p.inc(s.copy(bass.AP(R1, dc0 * 2048 + j * 128,
                                          [[16384, 128], [2048, 4], [1, 128]]),
                                  PS[6 + hb][:, :]), "sS")
                bankfree[6 + hb] = ("sS", ev)
        sS_xf = p.cnt["sS"]
        if dbg:
            p.w(y, "sS", sS_xf)
            p.dma(y, dbg_t["dbg_xf"][:, :], R1[:, :], "dO")

        p.w(t, "sS", sS_xf)
        p.w(t, "sV", vgw)
        for tt in range(4):
            bank_i = tt % 2
            bank_wait(bank_i)
            for dcc in range(8):
                ins = t.matmul(PS[bank_i][0:16, 0:512], gwT[:, dcc * 16:(dcc + 1) * 16],
                               R1[:, dcc * 2048 + tt * 512:dcc * 2048 + tt * 512 + 512],
                               start=(dcc == 0), stop=(dcc == 7), skip_group_check=True)
            tg = p.inc(ins, "sT")
            p.w(v, "sT", tg)
            ev = p.inc(v.tensor_copy(
                bass.AP(R2, CFM_C + tt * 512, [[16384, 16], [1, 512]]),
                PS[bank_i][0:16, 0:512]), "sV")
            bankfree[bank_i] = ("sV", ev)
        sV_gs = p.cnt["sV"]
        p.w(t, "sV", sV_gs)
        bank_wait(2)
        for j in range(NT):
            ins = t.matmul(PS[2][:, j * 16:(j + 1) * 16],
                           bass.AP(R2, CFM_C + j * 128, [[16384, 16], [1, 128]]),
                           KE[0:16, 0:16], is_transpose=True, start=(j == 0),
                           stop=(j == NT - 1), skip_group_check=True)
        tg = p.inc(ins, "sT")
        p.w(v, "sT", tg)
        vtm = p.inc(v.tensor_copy(ctm[:, :], PS[2][:, 0:256]), "sV")
        bankfree[2] = ("sV", vtm)

        GS_MX, GS_SM, GS_OR, GS_SC, GS_GS, GS_PEN, GS_T = 0, 16, 32, 288, 544, 608, 672
        p.w(v, "sV", vtm)
        gA = lambda o, dims: bass.AP(gsc, o, [[1024, 128]] + dims)
        v.tensor_reduce(gA(GS_MX, [[1, 16]]), bass.AP(ctm, 0, [[256, 128], [16, 16], [1, 16]]),
                        axis=AXX, op=ALU.max)
        for j in range(NT):
            lastv = v.tensor_scalar(gsc[:, GS_OR + j * 16:GS_OR + (j + 1) * 16],
                                    ctm[:, j * 16:(j + 1) * 16],
                                    gsc[:, GS_MX + j:GS_MX + j + 1], None, ALU.subtract)
        vx = p.inc(lastv, "sV")
        p.w(s, "sV", vx)
        sx = p.inc(s.activation(gsc[:, GS_OR:GS_OR + 256], gsc[:, GS_OR:GS_OR + 256],
                                AF.Exp), "sS")
        p.w(v, "sS", sx)
        v.tensor_reduce(gA(GS_SM, [[1, 16]]), gA(GS_OR, [[16, 16], [1, 16]]),
                        axis=AXX, op=ALU.add)
        v.reciprocal(gA(GS_SM, [[1, 16]]), gA(GS_SM, [[1, 16]]))
        for j in range(NT):
            v.tensor_scalar_mul(gsc[:, GS_OR + j * 16:GS_OR + (j + 1) * 16],
                                gsc[:, GS_OR + j * 16:GS_OR + (j + 1) * 16],
                                gsc[:, GS_SM + j:GS_SM + j + 1])
        gbbc = bass.AP(KGB, 0, [[16, 128], [0, 16], [1, 16]])
        v.tensor_tensor(gsc[:, GS_SC:GS_SC + 256], gA(GS_OR, [[1, 256]]), gbbc, ALU.add)
        gsl = lambda i: bass.AP(gsc, GS_SC + i, [[1024, 128], [16, 16], [4, 4]])
        first = True
        for i in range(4):
            for jj in range(i + 1, 4):
                v.tensor_tensor(gsc[:, GS_T:GS_T + 64], gsl(i), gsl(jj), ALU.add)
                if first:
                    v.tensor_copy(gsc[:, GS_GS:GS_GS + 64], gsc[:, GS_T:GS_T + 64])
                    first = False
                else:
                    v.tensor_tensor(gsc[:, GS_GS:GS_GS + 64], gsc[:, GS_GS:GS_GS + 64],
                                    gsc[:, GS_T:GS_T + 64], ALU.max)
        gg = lambda k: bass.AP(gsc, GS_GS + k, [[1024, 128], [4, 16]])
        gp = lambda k: bass.AP(gsc, GS_PEN + k, [[1024, 128], [4, 16]])
        v.memset(gsc[:, GS_PEN:GS_PEN + 64], 0.0)
        for g1 in range(4):
            for g2 in range(4):
                if g1 == g2:
                    continue
                v.tensor_tensor(gsc[:, GS_T:GS_T + 16], gg(g2), gg(g1), ALU.is_gt)
                v.tensor_tensor(gp(g1), gp(g1), gA(GS_T, [[1, 16]]), ALU.add)
        v.tensor_scalar(gsc[:, GS_PEN:GS_PEN + 64], gsc[:, GS_PEN:GS_PEN + 64],
                        1.5, 1e30, ALU.is_ge, ALU.mult)
        pen_e = bass.AP(gsc, GS_PEN, [[1024, 128], [4, 16], [1, 4], [0, 4]])
        v.tensor_tensor(gsc[:, GS_SC:GS_SC + 256], gA(GS_SC, [[1, 256]]), pen_e, ALU.subtract)
        v.tensor_reduce(gA(GS_MX, [[1, 16]]), gA(GS_SC, [[16, 16], [1, 16]]),
                        axis=AXX, op=ALU.max)
        for j in range(NT):
            v.tensor_scalar(gsc[:, GS_T + j * 16:GS_T + (j + 1) * 16],
                            gsc[:, GS_SC + j * 16:GS_SC + (j + 1) * 16],
                            gsc[:, GS_MX + j:GS_MX + j + 1], None, ALU.is_ge)
        v.tensor_scalar_mul(gsc[:, GS_T:GS_T + 256], gsc[:, GS_T:GS_T + 256], 1e30)
        v.tensor_tensor(gsc[:, GS_T:GS_T + 256], gA(GS_SC, [[1, 256]]),
                        gA(GS_T, [[1, 256]]), ALU.subtract)
        v.tensor_reduce(gA(GS_SM, [[1, 16]]), gA(GS_T, [[16, 16], [1, 16]]),
                        axis=AXX, op=ALU.max)
        for j in range(NT):
            v.tensor_scalar(ctm[:, j * 16:(j + 1) * 16],
                            gsc[:, GS_SC + j * 16:GS_SC + (j + 1) * 16],
                            gsc[:, GS_SM + j:GS_SM + j + 1], None, ALU.is_ge)
        vcm = p.inc(v.tensor_tensor(ctm[:, :], ctm[:, :], gA(GS_OR, [[1, 256]]), ALU.mult),
                    "sV")
        if dbg:
            p.w(y, "sV", vcm)
            p.dma(y, dbg_t["dbg_cm"][:, :], ctm[:, :], "dO")
        if stop_after == "G":
            p.w(y, "sV", vcm)
            p.dma(y, out[0:128, 0:256], ctm[:, :], "dO")
            p.w(y, "dO", p.cnt["dO"])
            p.finalize(frac)
            return nc, dbg_t
        cfm_ap = lambda o, n_: bass.AP(R2, CFM_C + o, [[16384, 16], [1, n_]])
        p.w(t, "sV", vcm)
        for wv in range(4):
            bank_wait(2)
            for k in range(4):
                j = wv * 4 + k
                ins = t.matmul(PS[2][0:16, k * 128:(k + 1) * 128],
                               ctm[:, j * 16:(j + 1) * 16],
                               KE[:, :], is_transpose=True, start=(k == 0), stop=(k == 3),
                               skip_group_check=True)
            tg = p.inc(ins, "sT")
            p.w(v, "sT", tg)
            vcf = p.inc(v.tensor_copy(cfm_ap(wv * 512, 512),
                                      bass.AP(PS[2], 0, [[512, 16], [1, 512]])), "sV")
            bankfree[2] = ("sV", vcf)
        dbe2 = p.dma(y, bass.AP(R2, BE2_C, [[16384, 16], [1, 1024]]), be2, "dM")

        YA, CBC = 0, 8192
        KOH2 = 8192
        tcons_h, tcons_w2, prev_vh = 0, 0, 0
        we2_done = [0, 0]
        ostore_h = []
        dko_vals = {}
        for g_pre in range(2):
            dko_vals[g_pre] = p.dma(
                y, R3[0:16, KOH2 + g_pre * 128:KOH2 + (g_pre + 1) * 128],
                kohc[:, g_pre * 128:(g_pre + 1) * 128], "dK")
        pend = None

        def hmov(b, off, n_):
            if b == 0:
                return hsb[:, off:off + n_]
            return SCA[:, 1024 + off:1024 + off + n_]

        def emit_we2(pd):
            nonlocal tcons_w2
            p.w(t, "sV", pd["vh"])
            p.w(t, "dw2", pd["d2"])
            if pd["ei"] == 17:
                p.w(t, "dM", dbe2)
            bank_wait(4); bank_wait(5); bank_wait(6); bank_wait(7)
            done = [0, 0]
            for dcc in range(8):
                bk = PS[4 + dcc // 2]
                ocol = (dcc % 2) * 256
                for ic_ in range(4):
                    stat = R2[:, F_WE2 + ic_ * 1024 + dcc * 128:
                              F_WE2 + ic_ * 1024 + dcc * 128 + 128]
                    ins = t.matmul(bk[:, ocol:ocol + 256], stat,
                                   hmov(pd["buf"], ic_ * 256, 256),
                                   start=(ic_ == 0), stop=False,
                                   skip_group_check=True)
                if pd["ei"] == 17:
                    ins = t.matmul(bk[:, ocol:ocol + 256],
                                   bass.AP(R2, BE2_C + dcc * 128,
                                           [[16384, 16], [1, 128]]),
                                   cfm_ap(pd["tof"], 256),
                                   start=False, stop=True, skip_group_check=True)
                if dcc == 3:
                    done[0] = p.inc(ins, "sT")
            done[1] = p.inc(ins, "sT")
            tcons_w2 = done[1]
            we2_done[pd["buf"]] = done[1]
            pd["done"] = done

        def emit_evac(pd):
            vya = [0, 0]
            for hf in range(2):
                p.w(v, "sT", pd["done"][hf])
                for dcc in range(hf * 4, hf * 4 + 4):
                    bk = PS[4 + dcc // 2]
                    ocol = (dcc % 2) * 256
                    dsty = bass.AP(R2, YA + dcc * 1024 + pd["ts"] * 256,
                                   [[16384, 128], [1, 256]])
                    srcp = bass.AP(bk, ocol, [[512, 128], [1, 256]])
                    if pd["ei"] == 0:
                        ins2 = v.tensor_scalar(dsty, srcp,
                                               bcol[:, 8 + dcc:8 + dcc + 1],
                                               None, ALU.add)
                    else:
                        ins2 = v.tensor_tensor(dsty, dsty, srcp, ALU.add)
                vya[hf] = p.inc(ins2, "sV")
            bankfree[4] = ("sV", vya[0]); bankfree[5] = ("sV", vya[0])
            bankfree[6] = ("sV", vya[1]); bankfree[7] = ("sV", vya[1])

        for half in range(2):
            th0 = half * 1024
            for ei in range(18):
                is_sh = ei < 2
                e = ei - 2
                if is_sh:
                    shi = ei
                    w1s = lambda dcc: bass.AP(ws1.tensor, dcc * 128 * D + shi * IC,
                                              [[D, 128], [1, IC]])
                    w3s = lambda dcc: bass.AP(ws3.tensor, dcc * 128 * D + shi * IC,
                                              [[D, 128], [1, IC]])
                    w2s = lambda ic_: ws2[shi * IC + ic_ * 128:shi * IC + (ic_ + 1) * 128, :]
                else:
                    w1s = lambda dcc: We1[e][dcc * 128:(dcc + 1) * 128, :]
                    w3s = lambda dcc: We3[e][dcc * 128:(dcc + 1) * 128, :]
                    w2s = lambda ic_: We2[e][ic_ * 128:(ic_ + 1) * 128, :]
                if half == 0 and ei == 0:
                    p.w(y, "sT", tT_free)
                else:
                    p.w(y, "sT", tcons_h)
                d1 = [p.dma(y, R3[:, F_WE1 + dcc * 512:F_WE1 + (dcc + 1) * 512], w1s(dcc),
                            "dw1") for dcc in range(8)]
                d3 = [p.dma(y, R3[:, F_WE3 + dcc * 512:F_WE3 + (dcc + 1) * 512], w3s(dcc),
                            "dw3") for dcc in range(8)]
                if not is_sh:
                    g_idx = half * 16 + e
                    sl_ = g_idx % 2
                    p.w(t, "dK")
                    p.w(t, "sV", vcf)
                    bank_wait(4); bank_wait(5)
                    for k in range(2):
                        ins = t.matmul(PS[4 + k][:, :],
                                       R3[0:16, KOH2 + sl_ * 128:KOH2 + sl_ * 128 + 128],
                                       cfm_ap(th0 + k * 512, 512),
                                       start=True, stop=True, skip_group_check=True)
                    tcb = p.inc(ins, "sT")
                    p.w(s, "sT", tcb)
                    p.w(s, "sV", prev_vh)
                    s.copy(R2[:, CBC:CBC + 512], PS[4][:, :])
                    scb = p.inc(s.copy(R2[:, CBC + 512:CBC + 1024], PS[5][:, :]),
                                "sS")
                    bankfree[4] = ("sS", scb); bankfree[5] = ("sS", scb)
                    if g_idx + 2 < 32:
                        gn = g_idx + 2
                        p.w(y, "sT", tcb)
                        dko_vals[gn] = p.dma(
                            y, R3[0:16, KOH2 + (gn % 2) * 128:KOH2 + (gn % 2) * 128 + 128],
                            kohc[:, (gn % 16) * 128:((gn % 16) + 1) * 128], "dK")
                for ts_ in range(4):
                    tof = th0 + ts_ * 256
                    p.w(t, "dw1", d1[-1])
                    p.w(t, "dw3", d3[-1])
                    for ic2 in range(2):
                        b1, b3 = ic2, 2 + ic2
                        bank_wait(b1); bank_wait(b3)
                        for sub in range(2):
                            ich = ic2 * 2 + sub
                            for dcc in range(8):
                                st1 = R3[:, F_WE1 + dcc * 512 + ich * 128:
                                         F_WE1 + dcc * 512 + ich * 128 + 128]
                                st3 = R3[:, F_WE3 + dcc * 512 + ich * 128:
                                         F_WE3 + dcc * 512 + ich * 128 + 128]
                                mov = R1[:, dcc * 2048 + tof:dcc * 2048 + tof + 256]
                                t.matmul(PS[b1][:, sub * 256:(sub + 1) * 256], st1, mov,
                                         start=(dcc == 0), stop=(dcc == 7),
                                         skip_group_check=True)
                                ins = t.matmul(PS[b3][:, sub * 256:(sub + 1) * 256], st3,
                                               mov, start=(dcc == 0), stop=(dcc == 7),
                                               skip_group_check=True)
                    tcons_h = p.inc(ins, "sT")
                    if pend is not None:
                        emit_we2(pend)
                    if ts_ == 0:
                        if pend is None and half == 0 and ei == 0:
                            p.w(y, "sT", tT_free)
                        else:
                            p.w(y, "sT", tcons_w2)
                        d2 = [p.dma(y, R2[:, F_WE2 + ic_ * 1024:F_WE2 + (ic_ + 1) * 1024],
                                    w2s(ic_), "dw2") for ic_ in range(4)]
                    p.w(s, "sT", tcons_h)
                    if half == 0 and ei == 0 and ts_ == 0:
                        p.w(s, "sT", last_attn_mm)
                    p.w(s, "sV", prev_vh)
                    for ich in range(4):
                        col = (144 + shi * 4 + ich) if is_sh else (16 + e * 4 + ich)
                        p.inc(s.activation(SCA[:, ich * 256:(ich + 1) * 256],
                                           PS[ich // 2][:, (ich % 2) * 256:(ich % 2) * 256 + 256],
                                           AF.Silu, bias=bcol[:, col:col + 1], scale=1.0),
                              "sS")
                    sl = p.cnt["sS"]
                    buf = ts_ % 2
                    p.w(v, "sS", sl)
                    p.w(v, "sT", we2_done[buf])
                    for ich in range(4):
                        col3 = (152 + shi * 4 + ich) if is_sh else (80 + e * 4 + ich)
                        lastv = v.scalar_tensor_tensor(hmov(buf, ich * 256, 256),
                                                       PS[2 + ich // 2][:, (ich % 2) * 256:
                                                                        (ich % 2) * 256 + 256],
                                                       bcol[:, col3:col3 + 1],
                                                       SCA[:, ich * 256:(ich + 1) * 256],
                                                       ALU.add, ALU.mult)
                    if not is_sh:
                        p.w(v, "sS", scb)
                        cbap = bass.AP(R2, CBC + ts_ * 256, [[16384, 128], [0, 4], [1, 256]])
                        lastv = v.tensor_tensor(hmov(buf, 0, 1024), hmov(buf, 0, 1024),
                                                cbap, ALU.mult)
                    vh = p.inc(lastv, "sV")
                    prev_vh = vh
                    for bb in (0, 1, 2, 3):
                        bankfree[bb] = ("sV", vh)
                    if pend is not None:
                        emit_evac(pend)
                    pend = {"buf": buf, "vh": vh, "ts": ts_, "ei": ei,
                            "d2": d2[-1], "tof": tof}
            emit_we2(pend)
            emit_evac(pend)
            pend = None
            vout = []
            for tih in range(8):
                ti_ = half * 8 + tih
                bufA = SCA[:, (tih % 2) * 1024:(tih % 2) * 1024 + 1024]
                qx = y if tih % 2 == 0 else s
                if tih == 0:
                    p.w(qx, "sV", prev_vh)
                if tih == 1:
                    p.w(qx, "sT", tcons_w2)
                if tih >= 2:
                    p.w(qx, "sV", vout[tih - 2])
                dj = p.dma(qx, bufA, x1sp[ti_ * 128:(ti_ + 1) * 128, :],
                           "dX%d" % (tih % 2))
                p.w(t, "sV", p.cnt["sV"])
                for hb in range(2):
                    bank_i = 2 + hb
                    bank_wait(bank_i)
                    for k in range(4):
                        dcc = hb * 4 + k
                        src = bass.AP(R2, YA + dcc * 1024 + tih * 128,
                                      [[16384, 128], [1, 128]])
                        ins = t.matmul(PS[bank_i][:, k * 128:(k + 1) * 128], src, KE[:, :],
                                       is_transpose=True, start=(k == 0), stop=(k == 3),
                                       skip_group_check=True)
                to = p.inc(ins, "sT")
                p.w(v, "sT", to)
                p.w(v, "dX%d" % (tih % 2), dj)
                if ostore_h:
                    p.w(v, "dO", ostore_h[-1])
                v.tensor_tensor(bass.AP(SCB, (tih % 2) * 1024, [[2048, 128], [1, 512]]),
                                PS[2][:, :],
                                bass.AP(SCA, (tih % 2) * 1024, [[2048, 128], [1, 512]]),
                                ALU.add)
                ev = p.inc(v.tensor_tensor(
                    bass.AP(SCB, (tih % 2) * 1024 + 512, [[2048, 128], [1, 512]]),
                    PS[3][:, :],
                    bass.AP(SCA, (tih % 2) * 1024 + 512, [[2048, 128], [1, 512]]),
                    ALU.add), "sV")
                bankfree[2] = ("sV", ev); bankfree[3] = ("sV", ev)
                vout.append(ev)
                p.w(y, "sV", ev)
                ostore_h.append(p.dma(y, out[ti_ * 128:(ti_ + 1) * 128, :],
                                      bass.AP(SCB, (tih % 2) * 1024,
                                              [[2048, 128], [1, 1024]]), "dO"))
        p.w(y, "dO", p.cnt["dO"])
        p.finalize(frac)

    return nc, dbg_t


def _bcolh(i):
    o = np.zeros((128, 160), np.float32)
    o[:, 0:8] = i["L0_attn_b_out"].reshape(8, 128).T
    o[:, 8:16] = i["L0_ffn_bs2"].reshape(8, 128).T
    o[:, 16:80] = i["L0_ffn_be1"].reshape(16, 4, 128).transpose(2, 0, 1).reshape(128, 64)
    o[:, 80:144] = i["L0_ffn_be3"].reshape(16, 4, 128).transpose(2, 0, 1).reshape(128, 64)
    o[:, 144:152] = i["L0_ffn_bs1"].reshape(8, 128).T
    o[:, 152:160] = i["L0_ffn_bs3"].reshape(8, 128).T
    return o


def _kohc():
    a = np.zeros((16, 2048), np.float32)
    for e in range(16):
        a[e, e * 128:(e + 1) * 128] = 1.0
    return a


def _consts():
    tri = np.triu(np.ones((Q, Q), np.float32))
    LTZ = np.zeros((128, 128), np.float32)
    LTZ[0:64, 0:64] = tri
    LTZ[64:128, 64:128] = tri
    jt = np.concatenate([np.arange(1, 65), np.arange(-63, 1), np.arange(63, -1, -1)])
    JT = np.tile(jt.astype(np.float32), (128, 1))
    msk = np.ones((128, 32), np.float32)
    msk[:, 0] = 0.0
    return {"eye": np.eye(128, dtype=np.float32), "LTZ": LTZ, "JT": JT,
            "msk32": msk, "onesr": np.ones((1, 128), np.float32),
            "kohc": _kohc()}


_NC_CACHE = {}


def make_in_maps(i):
    cns = _consts()
    x = i["L0_attn_x"]
    base = {
        "logA": i["L0_attn_log_A"], "Bp": i["L0_attn_B_param"], "Cp": i["L0_attn_C_param"],
        "logdt": i["L0_attn_log_dt"],
        "Wout": i["L0_attn_W_out"], "bout": i["L0_attn_b_out"],
        "gw": i["L0_ffn_gate_w"],
        "We1": i["L0_ffn_We1"], "We2": i["L0_ffn_We2"], "We3": i["L0_ffn_We3"],
        "be1": i["L0_ffn_be1"], "be2": i["L0_ffn_be2"], "be3": i["L0_ffn_be3"],
        "ws1": i["L0_ffn_ws1"], "ws2": i["L0_ffn_ws2"], "ws3": i["L0_ffn_ws3"],
        "bs1": i["L0_ffn_bs1"], "bs2": i["L0_ffn_bs2"], "bs3": i["L0_ffn_bs3"],
        "gamb": np.tile(i["L0_attn_ln_gamma"][None, :], (128, 1)),
        "bcolh": _bcolh(i), "dtch": i["L0_attn_log_dt"].reshape(8, 128).T.copy(),
        "betb": np.tile(i["L0_attn_ln_beta"][None, :], (128, 1)),
        "Dbc": np.tile(i["L0_attn_D_param"][None, :], (128, 1)),
        "gbb": np.tile(i["L0_ffn_gate_b"][None, :], (128, 1)),
    }
    base.update(cns)
    in_maps = []
    for k in range(8):
        b, h = k // 2, k % 2
        m = dict(base)
        m["xo"] = np.ascontiguousarray(x[b, h * TOK:(h + 1) * TOK, :])
        m["xg"] = np.ascontiguousarray(x[b, (1 - h) * TOK:(2 - h) * TOK, :])
        m["flagc"] = np.full((128, 1), float(h), np.float32)
        in_maps.append(m)
    return in_maps


def kernel(**inputs):
    i = {k: np.ascontiguousarray(np.asarray(vv, np.float32)) for k, vv in inputs.items()}
    if "nc" not in _NC_CACHE:
        _NC_CACHE["nc"] = build_nc(dbg=False)
    nc, _ = _NC_CACHE["nc"]
    in_maps = make_in_maps(i)
    res = run_bass_kernel_spmd(nc, in_maps, core_ids=list(range(8)))
    outp = np.empty((4, 4096, D), np.float32)
    for k in range(8):
        b, h = k // 2, k % 2
        outp[b, h * TOK:(h + 1) * TOK, :] = res.results[k]["out"]
    return outp


# version 10
# speedup vs baseline: 1.1901x; 1.1791x over previous
import numpy as np
from contextlib import ExitStack
import concourse.bass as bass
import concourse.mybir as mybir
from concourse.bass_utils import run_bass_kernel_spmd

F32 = mybir.dt.float32
AF = mybir.ActivationFunctionType
ALU = mybir.AluOpType
AXX = mybir.AxisListType.X

D, N, Q, TOK = 1024, 64, 64, 2048
NT = TOK // 128
NCH = TOK // Q
DB = 32
NB = D // DB
NDP = DB // 2
EPS = 1e-5
NE, IC = 16, 512

TB_ARGE, TB_CTIL, TB_CTAU = 0, 1536, 2560
TB_GRD0, TB_BTIL, TB_GRD1 = 3584, 3648, 4672
TB_BTT, TB_MZ = 4736, 6912
R3_COLS = 8960
F_WE1, F_WE3 = 0, 4096
F_WE2 = 12288


class _RecInstr:
    __slots__ = ("post", "auto")

    def __init__(self, auto=None):
        self.post = []
        self.auto = auto

    def then_inc(self, sem, amt):
        self.post.append((sem, amt))
        return self


_OWN = {"v": "sV", "s": "sS", "g": "sP"}


class _RecEng:

    def __init__(self, prog, key):
        self._prog = prog
        self._key = key

    def __getattr__(self, attr):
        prog, key = self._prog, self._key
        own = _OWN.get(key)

        def call(*a, **kw):
            tape = prog.tape[key]
            if own is not None and attr not in ("wait_ge", "dma_start"):
                tape.append(("wait_ge", (prog.sem[own], prog.cnt[own]), {},
                             _RecInstr()))
                ri = _RecInstr(auto=own)
                ri.post.append((prog.sem[own], 1))
                prog.cnt[own] += 1
                tape.append((attr, a, kw, ri))
                return ri
            ri = _RecInstr()
            tape.append((attr, a, kw, ri))
            return ri
        return call


class P:
    def __init__(self, nc, block, stack):
        self.nc = nc
        self.block = block
        self.tape = {k: [] for k in ("t", "v", "s", "g", "y")}
        self.sem, self.cnt = {}, {}
        for name in ("dC", "dX0", "dX1", "dX2", "dX3", "dWo", "dw1", "dw3",
                     "dw2", "dM", "dK", "dO", "sT", "sV", "sS", "sP"):
            self.sem[name] = stack.enter_context(nc.semaphore(name))
            self.cnt[name] = 0
        self.t = _RecEng(self, "t")
        self.v = _RecEng(self, "v")
        self.s = _RecEng(self, "s")
        self.g = _RecEng(self, "g")
        self.y = _RecEng(self, "y")

    def inc(self, instr, name, amt=1):
        if instr.auto == name:
            return self.cnt[name]
        self.cnt[name] += amt
        instr.then_inc(self.sem[name], amt)
        return self.cnt[name]

    def w(self, eng, name, val=None):
        vv = self.cnt[name] if val is None else val
        if vv > 0:
            eng.wait_ge(self.sem[name], vv)

    def dma(self, q, o, i, name="dC", **kw):
        return self.inc(q.dma_start(out=o, in_=i, **kw), name, 16)

    def finalize(self, frac=1.0):
        secs = {"t": self.block.tensor, "v": self.block.vector,
                "s": self.block.scalar, "g": self.block.gpsimd,
                "y": self.block.sync}

        def replay(tape):
            tape = tape[:int(len(tape) * frac)]
            def body(e):
                for attr, a, kw, ri in tape:
                    ins = getattr(e, attr)(*a, **kw)
                    for sem, amt in ri.post:
                        ins.then_inc(sem, amt)
            return body

        for k, sec in secs.items():
            sec(replay(self.tape[k]))


def build_nc(dbg=False, frac=1.0, stop_after=None):
    nc = bass.Bass("TRN2", target_bir_lowering=False)
    ain = lambda n_, sh: nc.dram_tensor(n_, sh, F32, kind="ExternalInput").ap()
    xo = ain("xo", [TOK, D]); xg = ain("xg", [TOK, D])
    logA = ain("logA", [D, N]); Bp = ain("Bp", [D, N]); Cp = ain("Cp", [D, N])
    logdt = ain("logdt", [D])
    Wout = ain("Wout", [D, D]); bout = ain("bout", [D])
    gw = ain("gw", [NE, D])
    We1 = ain("We1", [NE, D, IC]); We2 = ain("We2", [NE, IC, D]); We3 = ain("We3", [NE, D, IC])
    be1 = ain("be1", [NE, IC]); be2 = ain("be2", [NE, D]); be3 = ain("be3", [NE, IC])
    ws1 = ain("ws1", [D, D]); ws2 = ain("ws2", [D, D]); ws3 = ain("ws3", [D, D])
    bs1 = ain("bs1", [D]); bs2 = ain("bs2", [D]); bs3 = ain("bs3", [D])
    eye = ain("eye", [128, 128]); LTZ = ain("LTZ", [128, 128])
    JT = ain("JT", [128, 192]); msk32 = ain("msk32", [128, 32])
    flagc = ain("flagc", [128, 1]); onesr = ain("onesr", [1, 128])
    kohc = ain("kohc", [16, 2048])
    gamb = ain("gamb", [128, D]); betb = ain("betb", [128, D])
    Dbc = ain("Dbc", [128, D]); gbb = ain("gbb", [128, NE])
    bcolh = ain("bcolh", [128, 160]); dtch = ain("dtch", [128, 8])
    out = nc.dram_tensor("out", [TOK, D], F32, kind="ExternalOutput").ap()
    x1sp = nc.dram_tensor("x1sp", [TOK, D], F32,
                          kind="ExternalOutput" if dbg else "Internal").ap()
    dbg_t = {}
    if dbg:
        for nm, sh in (("dbg_u", [128, 32768]), ("dbg_y", [128, 16384]),
                       ("dbg_cm", [128, NT * NE]), ("dbg_xf", [128, 16384])):
            dbg_t[nm] = nc.dram_tensor(nm, sh, F32, kind="ExternalOutput").ap()

    stack = ExitStack()
    with stack:
        sb = lambda nm, cols, rows=128: stack.enter_context(
            nc.sbuf_tensor(nm, [rows, cols], F32))
        R1 = sb("R1", 16384)
        R2 = sb("R2", 16384)
        R3 = sb("R3", R3_COLS)
        SCA = sb("SCA", 2048)
        SCB = sb("SCB", 2048)
        KE = sb("KE", 128); KL = sb("KL", 128); KJ = sb("KJ", 192)
        KM = sb("KM", 32); KF = sb("KF", 1)
        KGB = sb("KGB", NE)
        mdA = sb("mdA", 512); Cpk = sb("Cpk", 512); dBpk = sb("dBpk", 512)
        dAQ = sb("dAQ", 512)
        gwT = sb("gwT", 8 * NE)
        STA = sb("STA", 448)
        bcol = sb("bcol", 160)
        ctm = sb("ctm", NT * NE)
        gsc = sb("gsc", 1024)
        hsb = sb("hsb", 1024)
        CFM_C, BE2_C = 9216, 11264
        PS = [stack.enter_context(nc.psum_tensor("ps%d" % i, [128, 512], F32))
              for i in range(8)]
        block = stack.enter_context(nc.Block())
        p = P(nc, block, stack)
        t, v, s, g, y = p.t, p.v, p.s, p.g, p.y

        bankfree = [("sT", 0)] * 8

        def bank_wait(i):
            sem, val = bankfree[i]
            p.w(t, sem, val)

        S_SUM, S_SQ, S_MU, S_V1, S_R1, S_S, S_Q1, S_MR = 0, 32, 64, 96, 128, 160, 192, 224
        S_SUM3, S_MU3, S_R33 = 256, 272, 288
        S_T0, S_T1, S_T2, S_DT = 320, 352, 384, 416

        p.dma(y, KE[:, :], eye); p.dma(y, KL[:, :], LTZ); p.dma(y, KJ[:, :], JT)
        p.dma(y, KM[:, :], msk32); p.dma(y, KF[:, :], flagc)
        p.dma(y, hsb[:, :], betb)
        p.dma(y, KGB[:, :], gbb)
        p.dma(y, bcol[:, :], bcolh)
        PRE_G, PRE_A, PRE_C, PRE_B = 0, 64, 576, 1088
        for c in range(8):
            p.dma(y, SCB[:, PRE_A + c * 64:PRE_A + c * 64 + 64], logA[c * 128:(c + 1) * 128, :])
            p.dma(y, SCB[:, PRE_C + c * 64:PRE_C + c * 64 + 64], Cp[c * 128:(c + 1) * 128, :])
            p.dma(y, SCB[:, PRE_B + c * 64:PRE_B + c * 64 + 64], Bp[c * 128:(c + 1) * 128, :])
        p.dma(y, STA[:, S_DT:S_DT + 8], dtch)
        p.dma(y, gsc[0:16, :], gw)
        cW_pre = p.cnt["dC"]

        p.w(v, "dC", cW_pre)
        v.memset(SCB[:, PRE_G:PRE_G + 64], 0.0)
        p.w(s, "dC", cW_pre)
        s.activation(STA[:, S_DT:S_DT + 8], STA[:, S_DT:S_DT + 8], AF.Exp)
        sx = p.inc(s.activation(SCB[:, PRE_A:PRE_A + 512], SCB[:, PRE_A:PRE_A + 512],
                                AF.Exp), "sS")
        p.w(v, "sS", sx)
        for c in range(8):
            v.tensor_scalar(SCB[:, PRE_A + c * 64:PRE_A + (c + 1) * 64],
                            SCB[:, PRE_A + c * 64:PRE_A + (c + 1) * 64],
                            STA[:, S_DT + c:S_DT + c + 1], -1.0, ALU.mult, ALU.mult)
            lastv = v.tensor_scalar_mul(SCB[:, PRE_B + c * 64:PRE_B + (c + 1) * 64],
                                        SCB[:, PRE_B + c * 64:PRE_B + (c + 1) * 64],
                                        STA[:, S_DT + c:S_DT + c + 1])
        vPre = p.inc(lastv, "sV")

        p.w(t, "sV", vPre)
        jobs = [(base, dst, c) for c in range(8)
                for base, dst in ((PRE_A, mdA), (PRE_C, Cpk), (PRE_B, dBpk))]
        ev_hist = []
        for ji, (base, dst, c) in enumerate(jobs):
            bank = PS[ji % 2]
            if ji >= 2:
                p.w(t, "sV", ev_hist[ji - 2])
            off = base + c * 64
            t.matmul(bank[0:64, 0:128], SCB[:, off:off + 64], KE[:, :],
                     is_transpose=True, start=True, stop=False, skip_group_check=True)
            ins = t.matmul(bank[:, 128:256],
                           bass.AP(SCB, off - 64, [[2048, 128], [1, 128]]), KE[:, :],
                           is_transpose=True, start=True, stop=True, skip_group_check=True)
            tv = p.inc(ins, "sT")
            p.w(v, "sT", tv)
            v.tensor_copy(dst[0:64, c * 64:(c + 1) * 64],
                          bass.AP(bank, 0, [[512, 64], [2, 64]]))
            ev = p.inc(v.tensor_copy(dst[64:128, c * 64:(c + 1) * 64],
                                     bass.AP(bank, 64 * 512 + 128 + 1, [[512, 64], [2, 64]])),
                       "sV")
            ev_hist.append(ev)
            bankfree[ji % 2] = ("sV", ev)
        vPk = p.cnt["sV"]
        p.w(v, "sV", vPk)
        vq = p.inc(v.tensor_scalar_mul(dAQ[:, :], mdA[:, :], 64.0), "sV")
        p.w(s, "sV", vq)
        sdq = p.inc(s.activation(dAQ[:, :], dAQ[:, :], AF.Exp), "sS")

        p.w(t, "dC", cW_pre)
        bank_wait(2)
        for c in range(8):
            lastt = t.matmul(PS[2][:, c * 16:(c + 1) * 16], gsc[0:16, c * 128:(c + 1) * 128],
                             KE[0:16, 0:16], is_transpose=True, start=(c == 0), stop=(c == 7),
                             skip_group_check=True)
        tgw = p.inc(lastt, "sT")
        p.w(v, "sT", tgw)
        vgw = p.inc(v.tensor_copy(gwT[:, :], PS[2][:, 0:128]), "sV")
        bankfree[2] = ("sV", vgw)
        p.w(y, "sV", vgw)
        dGam = p.dma(y, gsc[:, 0:1024], gamb, "dM")
        tpk_done = p.cnt["sT"]

        def x_src(j):
            src = xo if j < NT else xg
            jj = j % NT
            return src[jj * 128:(jj + 1) * 128, :]

        sS0, sV0 = p.cnt["sS"], p.cnt["sV"]
        ubufs = [(SCA, 0), (SCA, 1024), (SCB, 0), (SCB, 1024)]
        xd = {}
        vcons = {}
        for g0 in range(0, 2 * NT, 2):
            for k in range(2):
                j = g0 + k
                ten, off = ubufs[j % 4]
                q = y if j % 2 == 0 else s
                if j < 4 and j % 4 >= 2:
                    p.w(q, "sT", tpk_done)
                if j >= 4:
                    p.w(q, "sV", vcons[j - 4])
                xd[j] = p.dma(q, ten[:, off:off + 1024], x_src(j),
                              "dX%d" % (j % 4))
            for k in range(2):
                j = g0 + k
                ten, off = ubufs[j % 4]
                buf = ten[:, off:off + 1024]
                p.w(s, "dX%d" % (j % 4), xd[j])
                p.inc(s.activation(R3[:, 0:1024], buf, AF.Square,
                                   accum_out=STA[:, S_SQ + j:S_SQ + j + 1]), "sS")
                p.inc(s.activation(R3[:, 0:1024], buf, AF.Copy,
                                   accum_out=STA[:, S_SUM + j:S_SUM + j + 1]), "sS")
            st4 = lambda c0_: STA[:, c0_ + g0:c0_ + g0 + 2]
            p.w(v, "sS", p.cnt["sS"])
            v.tensor_scalar_mul(st4(S_MU), st4(S_SUM), 1.0 / D)
            v.tensor_tensor(st4(S_V1), st4(S_MU), st4(S_MU), ALU.mult)
            v.tensor_scalar_mul(st4(S_T0), st4(S_SQ), 1.0 / D)
            v.tensor_tensor(st4(S_V1), st4(S_T0), st4(S_V1), ALU.subtract)
            v.tensor_scalar_add(st4(S_T0), st4(S_V1), EPS)
            vx = p.inc(v.reciprocal(st4(S_T1), st4(S_T0)), "sV")
            p.w(s, "sV", vx)
            sx = p.inc(s.activation(st4(S_R1), st4(S_T1), AF.Sqrt), "sS")
            p.w(v, "sS", sx)
            v.tensor_tensor(st4(S_T2), st4(S_R1), st4(S_R1), ALU.mult)
            v.tensor_tensor(st4(S_T2), st4(S_T0), st4(S_T2), ALU.mult)
            v.tensor_scalar(st4(S_T2), st4(S_T2), -0.5, 1.5, ALU.mult, ALU.add)
            v.tensor_tensor(st4(S_R1), st4(S_R1), st4(S_T2), ALU.mult)
            v.tensor_tensor(st4(S_T2), st4(S_V1), st4(S_T1), ALU.mult)
            v.tensor_scalar_add(st4(S_T2), st4(S_T2), EPS)
            vx = p.inc(v.reciprocal(st4(S_T1), st4(S_T2)), "sV")
            p.w(s, "sV", vx)
            sx = p.inc(s.activation(st4(S_T1), st4(S_T1), AF.Sqrt), "sS")
            p.w(v, "sS", sx)
            v.tensor_tensor(st4(S_T0), st4(S_T1), st4(S_T1), ALU.mult)
            v.tensor_tensor(st4(S_T0), st4(S_T2), st4(S_T0), ALU.mult)
            v.tensor_scalar(st4(S_T0), st4(S_T0), -0.5, 1.5, ALU.mult, ALU.add)
            v.tensor_tensor(st4(S_T1), st4(S_T1), st4(S_T0), ALU.mult)
            v.tensor_tensor(st4(S_S), st4(S_R1), st4(S_T1), ALU.mult)
            v.tensor_tensor(st4(S_MR), st4(S_MU), st4(S_R1), ALU.mult)
            p.inc(v.tensor_scalar_add(st4(S_Q1), st4(S_R1), 1.0), "sV")
            for k in range(2):
                j = g0 + k
                ten, off = ubufs[j % 4]
                buf = ten[:, off:off + 1024]
                dstR = R1 if j < NT else R2
                dc = (j % NT) * 1024
                p.w(v, "dM", dGam)
                v.tensor_scalar(dstR[:, dc:dc + 1024], buf,
                                STA[:, S_MU + j:S_MU + j + 1],
                                STA[:, S_S + j:S_S + j + 1],
                                ALU.subtract, ALU.mult)
                vv = p.inc(v.tensor_tensor(dstR[:, dc:dc + 1024],
                                           dstR[:, dc:dc + 1024],
                                           gsc[:, 0:1024], ALU.mult), "sV")
                vcons[j] = vv
                p.w(g, "sV", vv)
                p.w(g, "dC", cW_pre)
                p.inc(g.tensor_tensor(dstR[:, dc:dc + 1024], dstR[:, dc:dc + 1024],
                                      hsb[:, :], ALU.add), "sP")
        sP_u = p.cnt["sP"]

        if dbg:
            p.w(y, "sP", sP_u)
            p.dma(y, dbg_t["dbg_u"][:, 0:16384], R1[:, :], "dO")
            p.dma(y, dbg_t["dbg_u"][:, 16384:32768], R2[:, :], "dO")
        dbg_dO_u = p.cnt["dO"]

        if stop_after == "U":
            p.w(y, "sP", sP_u)
            p.dma(y, out[0:128, :], R1[:, 0:1024], "dO")
            p.w(y, "dO", p.cnt["dO"])
            p.finalize(frac)
            return nc, dbg_t
        p.w(y, "sP", sP_u)
        dDb = p.dma(y, hsb[:, :], Dbc, "dM")

        nbl = NB
        if stop_after and stop_after.startswith("T") and stop_after[1:].isdigit():
            nbl = int(stop_after[1:])
        tT_free = 0
        intra_done = [0, 0]
        vsc_prev = 0
        se_last, tmz_last = 0, 0
        gtb_blk = {}
        ARGE_W = 8 * 192

        def tables_p1(bb, prolog=False):
            nonlocal se_last
            for hh in range(2):
                dph = bb * NDP + hh * 8
                p.w(g, "sS", se_last)
                if prolog and hh == 0:
                    p.w(g, "sV", vPk)
                    p.w(g, "dC", cW_pre)
                if not prolog and hh == 0:
                    p.w(g, "sT", tmz_last)
                in0 = bass.AP(mdA, dph, [[512, 128], [1, 8], [0, 192]])
                in1 = bass.AP(KJ, 0, [[192, 128], [0, 8], [1, 192]])
                ae = p.inc(g.tensor_tensor(R3[:, TB_ARGE:TB_ARGE + ARGE_W], in0, in1,
                                           ALU.mult), "sP")
                p.w(s, "sP", ae)
                se_last = p.inc(s.activation(R3[:, TB_ARGE:TB_ARGE + ARGE_W],
                                             R3[:, TB_ARGE:TB_ARGE + ARGE_W], AF.Exp),
                                "sS")
                p.w(g, "sS", se_last)
                eap = lambda o: bass.AP(R3, TB_ARGE + o, [[R3_COLS, 128], [192, 8], [1, 64]])
                cb = bass.AP(Cpk, dph, [[512, 128], [1, 8], [0, 64]])
                db_ = bass.AP(dBpk, dph, [[512, 128], [1, 8], [0, 64]])
                ho = hh * 512
                g.tensor_tensor(R3[:, TB_CTAU + ho:TB_CTAU + ho + 512], eap(64), cb,
                                ALU.mult)
                gg = p.inc(g.tensor_tensor(R3[:, TB_BTIL + ho:TB_BTIL + ho + 512],
                                           eap(128), db_, ALU.mult), "sP")
                if hh == 0:
                    p.w(g, "sT", intra_done[0])
                    p.inc(g.tensor_tensor(R3[:, TB_CTIL:TB_CTIL + 512], eap(0), cb,
                                          ALU.mult), "sP")
            gtb_blk[bb] = gg

        ctil1_blk = {}

        def tables_ctil1(bb):
            dph = bb * NDP + 8
            eap = lambda o: bass.AP(R3, TB_ARGE + o, [[R3_COLS, 128], [192, 8], [1, 64]])
            cb = bass.AP(Cpk, dph, [[512, 128], [1, 8], [0, 64]])
            p.w(g, "sT", tT_free)
            ctil1_blk[bb] = p.inc(g.tensor_tensor(
                R3[:, TB_CTIL + 512:TB_CTIL + 1024], eap(0), cb, ALU.mult), "sP")

        g.memset(R3[:, TB_GRD0:TB_GRD0 + 64], 0.0)
        p.inc(g.memset(R3[:, TB_GRD1:TB_GRD1 + 64], 0.0), "sP")
        tables_p1(0, prolog=True)
        tables_ctil1(0)
        for b in range(nbl):
            c0, dp0 = b * DB, b * NDP
            d0a = bass.AP(dAQ, dp0, [[512, 128], [1, NDP], [0, 32]])
            d0b = bass.AP(KM, 0, [[32, 128], [0, NDP], [1, 32]])
            p.w(g, "sV", vsc_prev)
            p.w(g, "sS", sdq)
            vd0 = p.inc(g.tensor_tensor(SCA[:, 1024:1536], d0a, d0b, ALU.mult), "sP")
            if b == 0:
                p.w(t, "sP", sP_u)
                p.w(t, "dM", dDb)

            if stop_after == "TA":
                p.w(y, "sV", p.cnt["sV"]); p.w(y, "sS", p.cnt["sS"])
                p.dma(y, out[0:128, :], R3[:, 0:1024], "dO")
                p.w(y, "dO", p.cnt["dO"])
                p.finalize(frac)
                return nc, dbg_t
            p.w(t, "sP", gtb_blk[b])
            for bank_i, w0 in ((0, 0), (1, 4), (2, 8), (3, 12)):
                bank_wait(bank_i)
                for k in range(4):
                    wdx = w0 + k
                    src = bass.AP(R3, TB_BTIL + wdx * 64 - 64, [[R3_COLS, 128], [1, 128]])
                    ins = t.matmul(PS[bank_i][:, k * 128:(k + 1) * 128], src, KE[:, :],
                                   is_transpose=True, start=(k == 0), stop=(k == 3),
                                   skip_group_check=True)
                tw = p.inc(ins, "sT")
                p.w(s, "sT", tw)
                ev = p.inc(s.copy(R3[:, TB_BTT + w0 * 128:TB_BTT + (w0 + 4) * 128],
                                  PS[bank_i][:, :]), "sS")
                bankfree[bank_i] = ("sS", ev)
            bank_wait(0)
            src = bass.AP(R3, TB_BTIL + 16 * 64 - 64, [[R3_COLS, 128], [1, 128]])
            tw = p.inc(t.matmul(PS[0][:, 0:128], src, KE[:, :], is_transpose=True,
                                start=True, stop=True, skip_group_check=True), "sT")
            p.w(s, "sT", tw)
            ev = p.inc(s.copy(R3[:, TB_BTT + 16 * 128:TB_BTT + 17 * 128], PS[0][:, 0:128]),
                       "sS")
            bankfree[0] = ("sS", ev)
            sS_win = p.cnt["sS"]
            if stop_after == "TB":
                p.w(y, "sS", p.cnt["sS"])
                p.dma(y, out[0:128, :], R3[:, 0:1024], "dO")
                p.w(y, "dO", p.cnt["dO"])
                p.finalize(frac)
                return nc, dbg_t

            def mz_build(chh):
                for par in (0, 1):
                    for sub in range(2):
                        bank_i = par * 2 + sub
                        bank_wait(bank_i)
                        first = {0: True, 64: True}
                        for k in range(4):
                            kk = sub * 8 + k * 2 + par
                            ch = chh * 16 + kk
                            dpi = ch // 2
                            bt = R3[par * 64:par * 64 + 64,
                                    TB_BTIL + dpi * 64:TB_BTIL + dpi * 64 + 64]
                            ct = R3[par * 64:par * 64 + 64,
                                    TB_CTAU + dpi * 64:TB_CTAU + dpi * 64 + 64]
                            zz = KL[par * 64:par * 64 + 64,
                                    (1 - par) * 64:(1 - par) * 64 + 64]
                            for colg, half in ((0, 0), (64, 1)):
                                stf = first[colg]; first[colg] = False
                                dcol = k * 128 + half * 64
                                zcol = k * 128 + (1 - half) * 64
                                t.matmul(PS[bank_i][half * 64:half * 64 + 64, dcol:dcol + 64],
                                         bt, ct, start=stf, stop=False,
                                         tile_position=(par * 64, colg),
                                         skip_group_check=True)
                                lmz = t.matmul(
                                    PS[bank_i][half * 64:half * 64 + 64, zcol:zcol + 64],
                                    bt, zz, start=False, stop=(k == 3),
                                    tile_position=(par * 64, colg), skip_group_check=True)
                        tmz = p.inc(lmz, "sT")
                        p.w(v, "sT", tmz)
                        p.w(v, "sT", intra_done[chh])
                        ltzb = bass.AP(KL, 0, [[128, 128], [0, 4], [1, 128]])
                        if chh == 0:
                            dst = bass.AP(R3, TB_MZ + (sub * 8 + par) * 128,
                                          [[R3_COLS, 128], [256, 4], [1, 128]])
                        else:
                            dst = bass.AP(SCB, (sub * 8 + par) * 128,
                                          [[2048, 128], [256, 4], [1, 128]])
                        ev_ = p.inc(v.tensor_tensor(dst, PS[bank_i][:, :], ltzb, ALU.mult),
                                    "sV")
                        bankfree[bank_i] = ("sV", ev_)
                return p.cnt["sV"]

            sV_mz = mz_build(0)
            if stop_after == "TC":
                p.w(y, "sV", p.cnt["sV"])
                p.dma(y, out[0:128, :], R3[:, 0:1024], "dO")
                p.w(y, "dO", p.cnt["dO"])
                p.finalize(frac)
                return nc, dbg_t

            p.w(t, "sS", sS_win)
            bank_wait(6); bank_wait(7)
            started = {}
            last_ins = None
            for ch in range(DB):
                dpi, par = ch // 2, ch % 2
                for cpar in (0, 1):
                    bank_i = 6 + cpar
                    if cpar == 0:
                        stat = R3[0:64, TB_BTT + (dpi + 1) * 128 + par * 64:
                                  TB_BTT + (dpi + 1) * 128 + par * 64 + 64]
                    else:
                        stat = R3[64:128, TB_BTT + dpi * 128 + par * 64:
                                  TB_BTT + dpi * 128 + par * 64 + 64]
                    for half, uten in ((0, R2), (1, R1)):
                        ntile = NT - 1 if (half == 1 and cpar == 1) else NT
                        mov = bass.AP(uten, cpar * 64 * 16384 + c0 + ch,
                                      [[16384, 64], [1024, ntile]])
                        ocol = dpi * 32 + (0 if half == 0 else 16)
                        dst = bass.AP(PS[bank_i], par * 64 * 512 + ocol,
                                      [[512, 64], [1, ntile]])
                        key = (bank_i, par)
                        stf = key not in started
                        started[key] = True
                        last_ins = t.matmul(dst, stat, mov, start=stf, stop=False,
                                            tile_position=(cpar * 64, par * 64),
                                            skip_group_check=True)
            th = p.inc(last_ins, "sT")
            if stop_after == "TD0":
                p.w(y, "sT", th)
                p.dma(y, out[0:128, :], R3[:, 0:1024], "dO")
                p.w(y, "dO", p.cnt["dO"])
                p.finalize(frac)
                return nc, dbg_t
            p.w(s, "sT", th); p.w(v, "sT", th)
            s.copy(bass.AP(SCA, 0, [[2048, 128], [32, NDP], [2, 16]]),
                   bass.AP(PS[6], 0, [[512, 128], [32, NDP], [1, 16]]))
            ev6 = p.inc(s.copy(bass.AP(SCA, 512 + 1, [[2048, 128], [32, NDP], [2, 16]]),
                               bass.AP(PS[6], 16, [[512, 128], [32, NDP], [1, 16]])), "sS")
            v.tensor_copy(bass.AP(SCA, 1, [[2048, 128], [32, NDP], [2, 16]]),
                          bass.AP(PS[7], 0, [[512, 128], [32, NDP], [1, 16]]))
            ev = p.inc(v.tensor_copy(
                bass.AP(SCA, 512 + 2, [[2048, 128], [32, NDP], [2, 15]]),
                bass.AP(PS[7], 16, [[512, 128], [32, NDP], [1, 15]])), "sV")
            bankfree[6] = ("sS", ev6); bankfree[7] = ("sV", ev)
            if stop_after == "TDa":
                p.w(y, "sV", p.cnt["sV"]); p.w(y, "sS", p.cnt["sS"])
                p.dma(y, out[0:128, :], SCA[:, 0:1024], "dO")
                p.w(y, "dO", p.cnt["dO"])
                p.finalize(frac)
                return nc, dbg_t
            p.w(v, "sS", p.cnt["sS"])
            p.w(v, "sP", vd0)
            v.tensor_tensor_scan(SCA[:, 0:512], SCA[:, 1024:1536], SCA[:, 0:512],
                                 0.0, ALU.mult, ALU.add)
            v.tensor_scalar_mul(bass.AP(SCA, 512, [[2048, 128], [32, NDP]]),
                                bass.AP(SCA, 31, [[2048, 128], [32, NDP]]), KF[:, 0:1])
            vsc = p.inc(v.tensor_tensor_scan(SCA[:, 512:1024], SCA[:, 1024:1536],
                                             SCA[:, 512:1024], 0.0, ALU.mult, ALU.add),
                        "sV")
            vsc_prev = vsc
            if stop_after == "TDb":
                p.w(y, "sV", p.cnt["sV"])
                p.dma(y, out[0:128, :], SCA[:, 0:1024], "dO")
                p.w(y, "dO", p.cnt["dO"])
                p.finalize(frac)
                return nc, dbg_t

            if stop_after == "TD":
                p.w(y, "sV", p.cnt["sV"]); p.w(y, "sS", p.cnt["sS"])
                p.dma(y, out[0:128, :], SCA[:, 0:1024], "dO")
                p.w(y, "dO", p.cnt["dO"])
                p.finalize(frac)
                return nc, dbg_t
            started5 = {}

            def intra_half(chh, mzv):
                p.w(t, "sV", max(vsc, mzv))
                if chh == 1:
                    p.w(t, "sP", ctil1_blk[b])
                if chh == 0:
                    bank_wait(4); bank_wait(5)
                last = None
                for kk in range(16):
                    ch = chh * 16 + kk
                    dpi, par = ch // 2, ch % 2
                    mov_e = bass.AP(R1, c0 + ch, [[16384, 64], [1024, NT]])
                    mov_o = bass.AP(R1, 64 * 16384 + c0 + ch, [[16384, 64], [1024, NT]])
                    if chh == 0:
                        mzA = R3[0:64, TB_MZ + kk * 128:TB_MZ + kk * 128 + 64]
                        mzB = R3[64:128, TB_MZ + kk * 128 + 64:TB_MZ + kk * 128 + 128]
                    else:
                        mzA = SCB[0:64, kk * 128:kk * 128 + 64]
                        mzB = SCB[64:128, kk * 128 + 64:kk * 128 + 128]
                    stf = (4, 0) not in started5; started5[(4, 0)] = True
                    t.matmul(bass.AP(PS[4], ch * 16, [[512, 64], [1, NT]]),
                             mzA, mov_e, start=stf, stop=False,
                             tile_position=(0, 0), skip_group_check=True)
                    stf = (5, 1) not in started5; started5[(5, 1)] = True
                    t.matmul(bass.AP(PS[5], 64 * 512 + ch * 16, [[512, 64], [1, NT]]),
                             mzB, mov_o, start=stf, stop=False,
                             tile_position=(64, 64), skip_group_check=True)
                    if par == 0:
                        stf = (5, 0) not in started5; started5[(5, 0)] = True
                        t.matmul(bass.AP(PS[5], ch * 16, [[512, 64], [1, NT]]),
                                 KL[64:128, 64:128], KL[64:128, 0:NT],
                                 start=stf, stop=False, tile_position=(64, 0),
                                 skip_group_check=True)
                    else:
                        stf = (4, 1) not in started5; started5[(4, 1)] = True
                        t.matmul(bass.AP(PS[4], 64 * 512 + ch * 16, [[512, 64], [1, NT]]),
                                 KL[0:64, 0:64], KL[0:64, 64:64 + NT],
                                 start=stf, stop=False, tile_position=(0, 64),
                                 skip_group_check=True)
                    for cpar in (0, 1):
                        bank_i = 4 + par
                        ct = R3[par * 64:par * 64 + 64,
                                TB_CTIL + dpi * 64:TB_CTIL + dpi * 64 + 64]
                        mov_h = bass.AP(SCA, par * 64 * 2048 + 512 + dpi * 32 + cpar,
                                        [[2048, 64], [2, NT]])
                        key = (bank_i, cpar)
                        stf = key not in started5
                        started5[key] = True
                        dst = bass.AP(PS[bank_i], cpar * 64 * 512 + ch * 16,
                                      [[512, 64], [1, NT]])
                        last = t.matmul(dst, ct, mov_h, start=stf, stop=False,
                                        tile_position=(par * 64, cpar * 64),
                                        skip_group_check=True)
                return p.inc(last, "sT")

            sV_mz2 = mz_build(1)
            tmz_last = p.cnt["sT"]
            ti0 = intra_half(0, sV_mz)
            intra_done[0] = ti0
            if b + 1 < nbl:
                tables_p1(b + 1)
            ti = intra_half(1, sV_mz2)
            intra_done[1] = ti
            tT_free = ti
            if b + 1 < nbl:
                tables_ctil1(b + 1)
            p.w(g, "sT", ti)
            p.w(g, "dM", dDb)
            if b == 0:
                p.w(g, "dO", dbg_dO_u)
            ucols = bass.AP(R1, c0, [[16384, 128], [1024, NT], [1, DB]])
            dbb = bass.AP(hsb, c0, [[1024, 128], [0, NT], [1, DB]])
            g.tensor_tensor(ucols, ucols, dbb, ALU.mult)
            ytmp = SCA[:, 1536:2048]
            src4 = bass.AP(PS[4], 0, [[512, 128], [1, 16], [16, DB]])
            src5 = bass.AP(PS[5], 0, [[512, 128], [1, 16], [16, DB]])
            tmpd = bass.AP(SCA, 1536, [[2048, 128], [1, 16], [16, DB]])
            p.w(v, "sT", ti)
            p.w(v, "sP", p.cnt["sP"])
            v.scalar_tensor_tensor(tmpd, src4, 1.0, ucols, ALU.mult, ALU.add)
            ev = p.inc(v.scalar_tensor_tensor(ucols, src5, 1.0, tmpd, ALU.mult, ALU.add),
                       "sV")
            bankfree[4] = ("sV", ev)
            bankfree[5] = ("sV", ev)
            started5.clear()
        sV_y = p.cnt["sV"]
        sP_y = p.cnt["sP"]

        if dbg:
            p.w(y, "sV", sV_y)
            p.w(y, "sP", sP_y)
            p.dma(y, dbg_t["dbg_y"][:, :], R1[:, :], "dO")
        dbg_dO_y = p.cnt["dO"]
        if stop_after and stop_after.startswith("T") and stop_after[1:].isdigit():
            p.w(y, "sV", p.cnt["sV"])
            p.dma(y, out[0:128, :], R1[:, 0:1024], "dO")
            p.w(y, "dO", p.cnt["dO"])
            p.finalize(frac)
            return nc, dbg_t

        p.w(t, "sV", sV_y)
        p.w(t, "sP", sP_y)
        blocks = [(ti_, dcc) for dcc in range(8) for ti_ in range(NT)]
        for bi in range(0, len(blocks), 4):
            bank_i = (bi // 4) % 2
            bank_wait(bank_i)
            for k in range(4):
                ti_, dcc = blocks[bi + k]
                src = bass.AP(R1, ti_ * 1024 + dcc * 128, [[16384, 128], [1, 128]])
                ins = t.matmul(PS[bank_i][:, k * 128:(k + 1) * 128], src, KE[:, :],
                               is_transpose=True, start=(k == 0), stop=(k == 3),
                               skip_group_check=True)
            tw = p.inc(ins, "sT")
            p.w(v, "sT", tw)
            ti0, dc0 = blocks[bi]
            ev = p.inc(v.tensor_copy(
                R2[:, dc0 * 2048 + ti0 * 128:dc0 * 2048 + ti0 * 128 + 512],
                PS[bank_i][:, :]), "sV")
            bankfree[bank_i] = ("sV", ev)
        sV_yfm = p.cnt["sV"]

        last_attn_mm = 0
        for ec in range(8):
            if ec == 0:
                p.w(y, "sT", tT_free)
            else:
                p.w(y, "sT", wt_cons)
            if ec == 0:
                p.w(y, "sV", sV_y)
                p.w(y, "sP", sP_y)
            dwr = p.dma(y, SCB[:, 0:1024], Wout[ec * 128:(ec + 1) * 128, :], "dWo")
            p.w(t, "dWo", dwr)
            for hb in range(2):
                bank_i = 6 + hb
                bank_wait(bank_i)
                for k in range(4):
                    ins = t.matmul(PS[bank_i][:, k * 128:(k + 1) * 128],
                                   SCB[:, hb * 512 + k * 128:hb * 512 + (k + 1) * 128],
                                   KE[:, :], is_transpose=True, start=(k == 0),
                                   stop=(k == 3), skip_group_check=True)
                tw = p.inc(ins, "sT")
                wt_cons = tw
                p.w(v, "sT", tw)
                if ec > 0:
                    p.w(v, "sT", last_attn_mm)
                ev = p.inc(v.tensor_copy(hsb[:, hb * 512:hb * 512 + 512], PS[bank_i][:, :]),
                           "sV")
                bankfree[bank_i] = ("sV", ev)
            vwt = p.cnt["sV"]
            p.w(t, "sV", vwt)
            p.w(t, "sV", sV_yfm)
            for tt in range(4):
                bank_i = 2 + (tt % 2)
                bank_wait(bank_i)
                for dcc in range(8):
                    stat = hsb[:, dcc * 128:(dcc + 1) * 128]
                    mov = R2[:, dcc * 2048 + tt * 512:dcc * 2048 + tt * 512 + 512]
                    ins = t.matmul(PS[bank_i][:, :], stat, mov, start=(dcc == 0),
                                   stop=(dcc == 7), skip_group_check=True)
                ta = p.inc(ins, "sT")
                last_attn_mm = ta
                p.w(s, "sT", ta)
                p.w(s, "dO", dbg_dO_y)
                ev = p.inc(s.activation(
                    R1[:, ec * 2048 + tt * 512:ec * 2048 + tt * 512 + 512],
                    PS[bank_i][:, :], AF.Identity, bias=bcol[:, ec:ec + 1], scale=1.0),
                    "sS")
                bankfree[bank_i] = ("sS", ev)
        sS_attn = p.cnt["sS"]

        p.w(t, "sS", sS_attn)
        v_ev2 = []
        spill_h = []
        scb_ok = False
        sV_usw = sV_y
        for ti_ in range(NT):
            buf = SCA[:, (ti_ % 2) * 1024:(ti_ % 2) * 1024 + 1024]
            qx = y if ti_ % 2 == 0 else s
            if ti_ < 2:
                p.w(qx, "sV", sV_usw)
                p.w(qx, "sP", sP_y)
            if ti_ >= 2:
                p.w(qx, "sV", v_ev2[ti_ - 2])
            dj = p.dma(qx, buf, x_src(ti_), "dX%d" % (ti_ % 2))
            p.w(v, "dX%d" % (ti_ % 2), dj)
            vr = p.inc(v.tensor_scalar(buf, buf, STA[:, S_Q1 + ti_:S_Q1 + ti_ + 1],
                                       STA[:, S_MR + ti_:S_MR + ti_ + 1],
                                       ALU.mult, ALU.subtract), "sV")
            for hb in range(2):
                bank_i = 6 + hb
                bank_wait(bank_i)
                for k in range(4):
                    ec = hb * 4 + k
                    src = bass.AP(R1, ec * 2048 + ti_ * 128, [[16384, 128], [1, 128]])
                    ins = t.matmul(PS[bank_i][:, k * 128:(k + 1) * 128], src, KE[:, :],
                                   is_transpose=True, start=(k == 0), stop=(k == 3),
                                   skip_group_check=True)
            ta = p.inc(ins, "sT")
            p.w(v, "sT", ta)
            p.w(v, "sV", vr)
            if not scb_ok:
                p.w(v, "sT", wt_cons)
                scb_ok = True
            if ti_ >= 2:
                p.w(v, "dO", spill_h[ti_ - 1])
            v.tensor_tensor(bass.AP(SCB, (ti_ % 2) * 1024, [[2048, 128], [1, 512]]),
                            PS[6][:, :],
                            bass.AP(SCA, (ti_ % 2) * 1024, [[2048, 128], [1, 512]]),
                            ALU.add)
            ev = p.inc(v.tensor_tensor(
                bass.AP(SCB, (ti_ % 2) * 1024 + 512, [[2048, 128], [1, 512]]),
                PS[7][:, :],
                bass.AP(SCA, (ti_ % 2) * 1024 + 512, [[2048, 128], [1, 512]]),
                ALU.add), "sV")
            bankfree[6] = ("sV", ev); bankfree[7] = ("sV", ev)
            v_ev2.append(ev)
            p.w(y, "sV", ev)
            spill_h.append(p.dma(y, x1sp[ti_ * 128:(ti_ + 1) * 128, :],
                                 bass.AP(SCB, (ti_ % 2) * 1024, [[2048, 128], [1, 1024]]),
                                 "dO"))
        dO_x1 = p.cnt["dO"]

        if stop_after == "W":
            p.w(y, "dO", p.cnt["dO"])
            p.finalize(frac)
            return nc, dbg_t

        sS3_0, sV3_0 = p.cnt["sS"], p.cnt["sV"]
        x1d = []
        for j in range(NT):
            buf = SCA[:, (j % 2) * 1024:(j % 2) * 1024 + 1024]
            if j == 0:
                p.w(y, "dO", dO_x1)
            if j < 2:
                p.w(y, "sV", v_ev2[NT - 1])
            if j >= 2:
                p.w(y, "sS", sS3_0 + (j - 1))
                p.w(y, "sV", sV3_0 + (j - 1))
            x1d.append(p.dma(y, buf, x1sp[j * 128:(j + 1) * 128, :], "dX%d" % (j % 2)))
        p.w(s, "dO", dO_x1)
        for j in range(NT):
            buf = SCA[:, (j % 2) * 1024:(j % 2) * 1024 + 1024]
            p.w(s, "dX%d" % (j % 2), x1d[j])
            p.inc(s.activation(SCB[:, 0:1024], buf, AF.Square,
                               accum_out=STA[:, S_SUM3 + j:S_SUM3 + j + 1]), "sS")
            p.w(v, "dX%d" % (j % 2), x1d[j])
            p.inc(v.reduce_sum(STA[:, S_MU3 + j:S_MU3 + j + 1], buf, axis=AXX), "sV")
        st16 = lambda c0_: STA[:, c0_:c0_ + 16]
        p.w(v, "sS", p.cnt["sS"]); p.w(v, "sV", p.cnt["sV"])
        v.tensor_scalar_mul(st16(S_MU3), st16(S_MU3), 1.0 / D)
        v.tensor_tensor(st16(S_R33), st16(S_MU3), st16(S_MU3), ALU.mult)
        v.tensor_scalar_mul(st16(S_T0), st16(S_SUM3), 1.0 / D)
        v.tensor_tensor(st16(S_R33), st16(S_T0), st16(S_R33), ALU.subtract)
        v.tensor_scalar_add(st16(S_T0), st16(S_R33), EPS)
        vx = p.inc(v.reciprocal(st16(S_T1), st16(S_T0)), "sV")
        p.w(s, "sV", vx)
        sx = p.inc(s.activation(st16(S_R33), st16(S_T1), AF.Sqrt), "sS")
        p.w(v, "sS", sx)
        v.tensor_tensor(st16(S_T2), st16(S_R33), st16(S_R33), ALU.mult)
        v.tensor_tensor(st16(S_T2), st16(S_T0), st16(S_T2), ALU.mult)
        v.tensor_scalar(st16(S_T2), st16(S_T2), -0.5, 1.5, ALU.mult, ALU.add)
        vf3 = p.inc(v.tensor_tensor(st16(S_R33), st16(S_R33), st16(S_T2), ALU.mult), "sV")

        vts_h = []
        tx_h = []
        sV_f3, sS_f3 = p.cnt["sV"], p.cnt["sS"]
        for j in range(NT):
            bufA = SCA[:, (j % 2) * 1024:(j % 2) * 1024 + 1024]
            if j < 2:
                p.w(y, "sV", sV_f3)
                p.w(y, "sS", sS_f3)
            if j >= 2:
                p.w(y, "sV", vts_h[j - 2])
            dj = p.dma(y, bufA, x1sp[j * 128:(j + 1) * 128, :], "dX%d" % (j % 2))
            p.w(v, "dX%d" % (j % 2), dj)
            p.w(v, "sV", vf3)
            if j >= 2:
                p.w(v, "sT", tx_h[j - 2])
            bufB = SCB[:, (j % 2) * 1024:(j % 2) * 1024 + 1024]
            vts = p.inc(v.tensor_scalar(bufB, bufA,
                                        STA[:, S_MU3 + j:S_MU3 + j + 1],
                                        STA[:, S_R33 + j:S_R33 + j + 1],
                                        ALU.subtract, ALU.mult), "sV")
            vts_h.append(vts)
            p.w(t, "sV", vts)
            for hb in range(2):
                bank_i = 6 + hb
                bank_wait(bank_i)
                for k in range(4):
                    dcc = hb * 4 + k
                    src = bass.AP(SCB, (j % 2) * 1024 + dcc * 128, [[2048, 128], [1, 128]])
                    ins = t.matmul(PS[bank_i][:, k * 128:(k + 1) * 128], src, KE[:, :],
                                   is_transpose=True, start=(k == 0), stop=(k == 3),
                                   skip_group_check=True)
            tx = p.inc(ins, "sT")
            tx_h.append(tx)
            p.w(s, "sT", tx)
            for hb in range(2):
                dc0 = hb * 4
                ev = p.inc(s.copy(bass.AP(R1, dc0 * 2048 + j * 128,
                                          [[16384, 128], [2048, 4], [1, 128]]),
                                  PS[6 + hb][:, :]), "sS")
                bankfree[6 + hb] = ("sS", ev)
        sS_xf = p.cnt["sS"]
        if dbg:
            p.w(y, "sS", sS_xf)
            p.dma(y, dbg_t["dbg_xf"][:, :], R1[:, :], "dO")

        p.w(t, "sS", sS_xf)
        p.w(t, "sV", vgw)
        for tt in range(4):
            bank_i = tt % 2
            bank_wait(bank_i)
            for dcc in range(8):
                ins = t.matmul(PS[bank_i][0:16, 0:512], gwT[:, dcc * 16:(dcc + 1) * 16],
                               R1[:, dcc * 2048 + tt * 512:dcc * 2048 + tt * 512 + 512],
                               start=(dcc == 0), stop=(dcc == 7), skip_group_check=True)
            tg = p.inc(ins, "sT")
            p.w(v, "sT", tg)
            ev = p.inc(v.tensor_copy(
                bass.AP(R2, CFM_C + tt * 512, [[16384, 16], [1, 512]]),
                PS[bank_i][0:16, 0:512]), "sV")
            bankfree[bank_i] = ("sV", ev)
        sV_gs = p.cnt["sV"]
        p.w(t, "sV", sV_gs)
        bank_wait(2)
        for j in range(NT):
            ins = t.matmul(PS[2][:, j * 16:(j + 1) * 16],
                           bass.AP(R2, CFM_C + j * 128, [[16384, 16], [1, 128]]),
                           KE[0:16, 0:16], is_transpose=True, start=(j == 0),
                           stop=(j == NT - 1), skip_group_check=True)
        tg = p.inc(ins, "sT")
        p.w(v, "sT", tg)
        vtm = p.inc(v.tensor_copy(ctm[:, :], PS[2][:, 0:256]), "sV")
        bankfree[2] = ("sV", vtm)

        GS_MX, GS_SM, GS_OR, GS_SC, GS_GS, GS_PEN, GS_T = 0, 16, 32, 288, 544, 608, 672
        p.w(v, "sV", vtm)
        gA = lambda o, dims: bass.AP(gsc, o, [[1024, 128]] + dims)
        v.tensor_reduce(gA(GS_MX, [[1, 16]]), bass.AP(ctm, 0, [[256, 128], [16, 16], [1, 16]]),
                        axis=AXX, op=ALU.max)
        for j in range(NT):
            lastv = v.tensor_scalar(gsc[:, GS_OR + j * 16:GS_OR + (j + 1) * 16],
                                    ctm[:, j * 16:(j + 1) * 16],
                                    gsc[:, GS_MX + j:GS_MX + j + 1], None, ALU.subtract)
        vx = p.inc(lastv, "sV")
        p.w(s, "sV", vx)
        sx = p.inc(s.activation(gsc[:, GS_OR:GS_OR + 256], gsc[:, GS_OR:GS_OR + 256],
                                AF.Exp), "sS")
        p.w(v, "sS", sx)
        v.tensor_reduce(gA(GS_SM, [[1, 16]]), gA(GS_OR, [[16, 16], [1, 16]]),
                        axis=AXX, op=ALU.add)
        v.reciprocal(gA(GS_SM, [[1, 16]]), gA(GS_SM, [[1, 16]]))
        for j in range(NT):
            v.tensor_scalar_mul(gsc[:, GS_OR + j * 16:GS_OR + (j + 1) * 16],
                                gsc[:, GS_OR + j * 16:GS_OR + (j + 1) * 16],
                                gsc[:, GS_SM + j:GS_SM + j + 1])
        gbbc = bass.AP(KGB, 0, [[16, 128], [0, 16], [1, 16]])
        v.tensor_tensor(gsc[:, GS_SC:GS_SC + 256], gA(GS_OR, [[1, 256]]), gbbc, ALU.add)
        gsl = lambda i: bass.AP(gsc, GS_SC + i, [[1024, 128], [16, 16], [4, 4]])
        first = True
        for i in range(4):
            for jj in range(i + 1, 4):
                v.tensor_tensor(gsc[:, GS_T:GS_T + 64], gsl(i), gsl(jj), ALU.add)
                if first:
                    v.tensor_copy(gsc[:, GS_GS:GS_GS + 64], gsc[:, GS_T:GS_T + 64])
                    first = False
                else:
                    v.tensor_tensor(gsc[:, GS_GS:GS_GS + 64], gsc[:, GS_GS:GS_GS + 64],
                                    gsc[:, GS_T:GS_T + 64], ALU.max)
        gg = lambda k: bass.AP(gsc, GS_GS + k, [[1024, 128], [4, 16]])
        gp = lambda k: bass.AP(gsc, GS_PEN + k, [[1024, 128], [4, 16]])
        v.memset(gsc[:, GS_PEN:GS_PEN + 64], 0.0)
        for g1 in range(4):
            for g2 in range(4):
                if g1 == g2:
                    continue
                v.tensor_tensor(gsc[:, GS_T:GS_T + 16], gg(g2), gg(g1), ALU.is_gt)
                v.tensor_tensor(gp(g1), gp(g1), gA(GS_T, [[1, 16]]), ALU.add)
        v.tensor_scalar(gsc[:, GS_PEN:GS_PEN + 64], gsc[:, GS_PEN:GS_PEN + 64],
                        1.5, 1e30, ALU.is_ge, ALU.mult)
        pen_e = bass.AP(gsc, GS_PEN, [[1024, 128], [4, 16], [1, 4], [0, 4]])
        v.tensor_tensor(gsc[:, GS_SC:GS_SC + 256], gA(GS_SC, [[1, 256]]), pen_e, ALU.subtract)
        v.tensor_reduce(gA(GS_MX, [[1, 16]]), gA(GS_SC, [[16, 16], [1, 16]]),
                        axis=AXX, op=ALU.max)
        for j in range(NT):
            v.tensor_scalar(gsc[:, GS_T + j * 16:GS_T + (j + 1) * 16],
                            gsc[:, GS_SC + j * 16:GS_SC + (j + 1) * 16],
                            gsc[:, GS_MX + j:GS_MX + j + 1], None, ALU.is_ge)
        v.tensor_scalar_mul(gsc[:, GS_T:GS_T + 256], gsc[:, GS_T:GS_T + 256], 1e30)
        v.tensor_tensor(gsc[:, GS_T:GS_T + 256], gA(GS_SC, [[1, 256]]),
                        gA(GS_T, [[1, 256]]), ALU.subtract)
        v.tensor_reduce(gA(GS_SM, [[1, 16]]), gA(GS_T, [[16, 16], [1, 16]]),
                        axis=AXX, op=ALU.max)
        for j in range(NT):
            v.tensor_scalar(ctm[:, j * 16:(j + 1) * 16],
                            gsc[:, GS_SC + j * 16:GS_SC + (j + 1) * 16],
                            gsc[:, GS_SM + j:GS_SM + j + 1], None, ALU.is_ge)
        vcm = p.inc(v.tensor_tensor(ctm[:, :], ctm[:, :], gA(GS_OR, [[1, 256]]), ALU.mult),
                    "sV")
        if dbg:
            p.w(y, "sV", vcm)
            p.dma(y, dbg_t["dbg_cm"][:, :], ctm[:, :], "dO")
        if stop_after == "G":
            p.w(y, "sV", vcm)
            p.dma(y, out[0:128, 0:256], ctm[:, :], "dO")
            p.w(y, "dO", p.cnt["dO"])
            p.finalize(frac)
            return nc, dbg_t
        cfm_ap = lambda o, n_: bass.AP(R2, CFM_C + o, [[16384, 16], [1, n_]])
        p.w(t, "sV", vcm)
        for wv in range(4):
            bank_wait(2)
            for k in range(4):
                j = wv * 4 + k
                ins = t.matmul(PS[2][0:16, k * 128:(k + 1) * 128],
                               ctm[:, j * 16:(j + 1) * 16],
                               KE[:, :], is_transpose=True, start=(k == 0), stop=(k == 3),
                               skip_group_check=True)
            tg = p.inc(ins, "sT")
            p.w(v, "sT", tg)
            vcf = p.inc(v.tensor_copy(cfm_ap(wv * 512, 512),
                                      bass.AP(PS[2], 0, [[512, 16], [1, 512]])), "sV")
            bankfree[2] = ("sV", vcf)
        dbe2 = p.dma(y, bass.AP(R2, BE2_C, [[16384, 16], [1, 1024]]), be2, "dM")

        YA, CBC = 0, 8192
        KOH2 = 8192
        tcons_h, tcons_w2, prev_vh = 0, 0, 0
        we2_done = [0, 0]
        ostore_h = []
        dko_vals = {}
        for g_pre in range(2):
            dko_vals[g_pre] = p.dma(
                y, R3[0:16, KOH2 + g_pre * 128:KOH2 + (g_pre + 1) * 128],
                kohc[:, g_pre * 128:(g_pre + 1) * 128], "dK")
        pend = None

        def hmov(b, off, n_):
            if b == 0:
                return hsb[:, off:off + n_]
            return SCA[:, 1024 + off:1024 + off + n_]

        def emit_we2(pd):
            nonlocal tcons_w2
            p.w(t, "sV", pd["vh"])
            p.w(t, "dw2", pd["d2"])
            if pd["ei"] == 17:
                p.w(t, "dM", dbe2)
            bank_wait(4); bank_wait(5); bank_wait(6); bank_wait(7)
            done = [0, 0]
            for dcc in range(8):
                bk = PS[4 + dcc // 2]
                ocol = (dcc % 2) * 256
                for ic_ in range(4):
                    stat = R2[:, F_WE2 + ic_ * 1024 + dcc * 128:
                              F_WE2 + ic_ * 1024 + dcc * 128 + 128]
                    ins = t.matmul(bk[:, ocol:ocol + 256], stat,
                                   hmov(pd["buf"], ic_ * 256, 256),
                                   start=(ic_ == 0), stop=False,
                                   skip_group_check=True)
                if pd["ei"] == 17:
                    ins = t.matmul(bk[:, ocol:ocol + 256],
                                   bass.AP(R2, BE2_C + dcc * 128,
                                           [[16384, 16], [1, 128]]),
                                   cfm_ap(pd["tof"], 256),
                                   start=False, stop=True, skip_group_check=True)
                if dcc == 3:
                    done[0] = p.inc(ins, "sT")
            done[1] = p.inc(ins, "sT")
            tcons_w2 = done[1]
            we2_done[pd["buf"]] = done[1]
            pd["done"] = done

        def emit_evac(pd):
            vya = [0, 0]
            for hf in range(2):
                p.w(v, "sT", pd["done"][hf])
                for dcc in range(hf * 4, hf * 4 + 4):
                    bk = PS[4 + dcc // 2]
                    ocol = (dcc % 2) * 256
                    dsty = bass.AP(R2, YA + dcc * 1024 + pd["ts"] * 256,
                                   [[16384, 128], [1, 256]])
                    srcp = bass.AP(bk, ocol, [[512, 128], [1, 256]])
                    if pd["ei"] == 0:
                        ins2 = v.tensor_scalar(dsty, srcp,
                                               bcol[:, 8 + dcc:8 + dcc + 1],
                                               None, ALU.add)
                    else:
                        ins2 = v.tensor_tensor(dsty, dsty, srcp, ALU.add)
                vya[hf] = p.inc(ins2, "sV")
            bankfree[4] = ("sV", vya[0]); bankfree[5] = ("sV", vya[0])
            bankfree[6] = ("sV", vya[1]); bankfree[7] = ("sV", vya[1])

        for half in range(2):
            th0 = half * 1024
            for ei in range(18):
                is_sh = ei < 2
                e = ei - 2
                if is_sh:
                    shi = ei
                    w1s = lambda dcc: bass.AP(ws1.tensor, dcc * 128 * D + shi * IC,
                                              [[D, 128], [1, IC]])
                    w3s = lambda dcc: bass.AP(ws3.tensor, dcc * 128 * D + shi * IC,
                                              [[D, 128], [1, IC]])
                    w2s = lambda ic_: ws2[shi * IC + ic_ * 128:shi * IC + (ic_ + 1) * 128, :]
                else:
                    w1s = lambda dcc: We1[e][dcc * 128:(dcc + 1) * 128, :]
                    w3s = lambda dcc: We3[e][dcc * 128:(dcc + 1) * 128, :]
                    w2s = lambda ic_: We2[e][ic_ * 128:(ic_ + 1) * 128, :]
                if half == 0 and ei == 0:
                    p.w(y, "sT", tT_free)
                else:
                    p.w(y, "sT", tcons_h)
                d1 = [p.dma(y, R3[:, F_WE1 + dcc * 512:F_WE1 + (dcc + 1) * 512], w1s(dcc),
                            "dw1") for dcc in range(8)]
                d3 = [p.dma(y, R3[:, F_WE3 + dcc * 512:F_WE3 + (dcc + 1) * 512], w3s(dcc),
                            "dw3") for dcc in range(8)]
                if not is_sh:
                    g_idx = half * 16 + e
                    sl_ = g_idx % 2
                    p.w(t, "dK")
                    p.w(t, "sV", vcf)
                    bank_wait(4); bank_wait(5)
                    for k in range(2):
                        ins = t.matmul(PS[4 + k][:, :],
                                       R3[0:16, KOH2 + sl_ * 128:KOH2 + sl_ * 128 + 128],
                                       cfm_ap(th0 + k * 512, 512),
                                       start=True, stop=True, skip_group_check=True)
                    tcb = p.inc(ins, "sT")
                    p.w(s, "sT", tcb)
                    p.w(s, "sV", prev_vh)
                    s.copy(R2[:, CBC:CBC + 512], PS[4][:, :])
                    scb = p.inc(s.copy(R2[:, CBC + 512:CBC + 1024], PS[5][:, :]),
                                "sS")
                    bankfree[4] = ("sS", scb); bankfree[5] = ("sS", scb)
                    if g_idx + 2 < 32:
                        gn = g_idx + 2
                        p.w(y, "sT", tcb)
                        dko_vals[gn] = p.dma(
                            y, R3[0:16, KOH2 + (gn % 2) * 128:KOH2 + (gn % 2) * 128 + 128],
                            kohc[:, (gn % 16) * 128:((gn % 16) + 1) * 128], "dK")
                for ts_ in range(4):
                    tof = th0 + ts_ * 256
                    p.w(t, "dw1", d1[-1])
                    p.w(t, "dw3", d3[-1])
                    for ic2 in range(2):
                        b1, b3 = ic2, 2 + ic2
                        bank_wait(b1); bank_wait(b3)
                        for sub in range(2):
                            ich = ic2 * 2 + sub
                            for dcc in range(8):
                                st1 = R3[:, F_WE1 + dcc * 512 + ich * 128:
                                         F_WE1 + dcc * 512 + ich * 128 + 128]
                                st3 = R3[:, F_WE3 + dcc * 512 + ich * 128:
                                         F_WE3 + dcc * 512 + ich * 128 + 128]
                                mov = R1[:, dcc * 2048 + tof:dcc * 2048 + tof + 256]
                                t.matmul(PS[b1][:, sub * 256:(sub + 1) * 256], st1, mov,
                                         start=(dcc == 0), stop=(dcc == 7),
                                         skip_group_check=True)
                                ins = t.matmul(PS[b3][:, sub * 256:(sub + 1) * 256], st3,
                                               mov, start=(dcc == 0), stop=(dcc == 7),
                                               skip_group_check=True)
                    tcons_h = p.inc(ins, "sT")
                    if pend is not None:
                        emit_we2(pend)
                    if ts_ == 0:
                        if pend is None and half == 0 and ei == 0:
                            p.w(y, "sT", tT_free)
                        else:
                            p.w(y, "sT", tcons_w2)
                        d2 = [p.dma(y, R2[:, F_WE2 + ic_ * 1024:F_WE2 + (ic_ + 1) * 1024],
                                    w2s(ic_), "dw2") for ic_ in range(4)]
                    p.w(s, "sT", tcons_h)
                    if half == 0 and ei == 0 and ts_ == 0:
                        p.w(s, "sT", last_attn_mm)
                    p.w(s, "sV", prev_vh)
                    for ich in range(4):
                        col = (144 + shi * 4 + ich) if is_sh else (16 + e * 4 + ich)
                        p.inc(s.activation(SCA[:, ich * 256:(ich + 1) * 256],
                                           PS[ich // 2][:, (ich % 2) * 256:(ich % 2) * 256 + 256],
                                           AF.Silu, bias=bcol[:, col:col + 1], scale=1.0),
                              "sS")
                    sl = p.cnt["sS"]
                    buf = ts_ % 2
                    p.w(v, "sS", sl)
                    p.w(v, "sT", we2_done[buf])
                    for ich in range(4):
                        col3 = (152 + shi * 4 + ich) if is_sh else (80 + e * 4 + ich)
                        lastv = v.scalar_tensor_tensor(hmov(buf, ich * 256, 256),
                                                       PS[2 + ich // 2][:, (ich % 2) * 256:
                                                                        (ich % 2) * 256 + 256],
                                                       bcol[:, col3:col3 + 1],
                                                       SCA[:, ich * 256:(ich + 1) * 256],
                                                       ALU.add, ALU.mult)
                    if not is_sh:
                        p.w(v, "sS", scb)
                        cbap = bass.AP(R2, CBC + ts_ * 256, [[16384, 128], [0, 4], [1, 256]])
                        lastv = v.tensor_tensor(hmov(buf, 0, 1024), hmov(buf, 0, 1024),
                                                cbap, ALU.mult)
                    vh = p.inc(lastv, "sV")
                    prev_vh = vh
                    for bb in (0, 1, 2, 3):
                        bankfree[bb] = ("sV", vh)
                    if pend is not None:
                        emit_evac(pend)
                    pend = {"buf": buf, "vh": vh, "ts": ts_, "ei": ei,
                            "d2": d2[-1], "tof": tof}
            emit_we2(pend)
            emit_evac(pend)
            pend = None
            vout = []
            for tih in range(8):
                ti_ = half * 8 + tih
                bufA = SCA[:, (tih % 2) * 1024:(tih % 2) * 1024 + 1024]
                qx = y if tih % 2 == 0 else s
                if tih == 0:
                    p.w(qx, "sV", prev_vh)
                if tih == 1:
                    p.w(qx, "sT", tcons_w2)
                if tih >= 2:
                    p.w(qx, "sV", vout[tih - 2])
                dj = p.dma(qx, bufA, x1sp[ti_ * 128:(ti_ + 1) * 128, :],
                           "dX%d" % (tih % 2))
                p.w(t, "sV", p.cnt["sV"])
                for hb in range(2):
                    bank_i = 2 + hb
                    bank_wait(bank_i)
                    for k in range(4):
                        dcc = hb * 4 + k
                        src = bass.AP(R2, YA + dcc * 1024 + tih * 128,
                                      [[16384, 128], [1, 128]])
                        ins = t.matmul(PS[bank_i][:, k * 128:(k + 1) * 128], src, KE[:, :],
                                       is_transpose=True, start=(k == 0), stop=(k == 3),
                                       skip_group_check=True)
                to = p.inc(ins, "sT")
                p.w(v, "sT", to)
                p.w(v, "dX%d" % (tih % 2), dj)
                if ostore_h:
                    p.w(v, "dO", ostore_h[-1])
                v.tensor_tensor(bass.AP(SCB, (tih % 2) * 1024, [[2048, 128], [1, 512]]),
                                PS[2][:, :],
                                bass.AP(SCA, (tih % 2) * 1024, [[2048, 128], [1, 512]]),
                                ALU.add)
                ev = p.inc(v.tensor_tensor(
                    bass.AP(SCB, (tih % 2) * 1024 + 512, [[2048, 128], [1, 512]]),
                    PS[3][:, :],
                    bass.AP(SCA, (tih % 2) * 1024 + 512, [[2048, 128], [1, 512]]),
                    ALU.add), "sV")
                bankfree[2] = ("sV", ev); bankfree[3] = ("sV", ev)
                vout.append(ev)
                p.w(y, "sV", ev)
                ostore_h.append(p.dma(y, out[ti_ * 128:(ti_ + 1) * 128, :],
                                      bass.AP(SCB, (tih % 2) * 1024,
                                              [[2048, 128], [1, 1024]]), "dO"))
        p.w(y, "dO", p.cnt["dO"])
        p.finalize(frac)

    return nc, dbg_t


def _bcolh(i):
    o = np.zeros((128, 160), np.float32)
    o[:, 0:8] = i["L0_attn_b_out"].reshape(8, 128).T
    o[:, 8:16] = i["L0_ffn_bs2"].reshape(8, 128).T
    o[:, 16:80] = i["L0_ffn_be1"].reshape(16, 4, 128).transpose(2, 0, 1).reshape(128, 64)
    o[:, 80:144] = i["L0_ffn_be3"].reshape(16, 4, 128).transpose(2, 0, 1).reshape(128, 64)
    o[:, 144:152] = i["L0_ffn_bs1"].reshape(8, 128).T
    o[:, 152:160] = i["L0_ffn_bs3"].reshape(8, 128).T
    return o


def _kohc():
    a = np.zeros((16, 2048), np.float32)
    for e in range(16):
        a[e, e * 128:(e + 1) * 128] = 1.0
    return a


def _consts():
    tri = np.triu(np.ones((Q, Q), np.float32))
    LTZ = np.zeros((128, 128), np.float32)
    LTZ[0:64, 0:64] = tri
    LTZ[64:128, 64:128] = tri
    jt = np.concatenate([np.arange(1, 65), np.arange(-63, 1), np.arange(63, -1, -1)])
    JT = np.tile(jt.astype(np.float32), (128, 1))
    msk = np.ones((128, 32), np.float32)
    msk[:, 0] = 0.0
    return {"eye": np.eye(128, dtype=np.float32), "LTZ": LTZ, "JT": JT,
            "msk32": msk, "onesr": np.ones((1, 128), np.float32),
            "kohc": _kohc()}


_NC_CACHE = {}


def make_in_maps(i):
    cns = _consts()
    x = i["L0_attn_x"]
    base = {
        "logA": i["L0_attn_log_A"], "Bp": i["L0_attn_B_param"], "Cp": i["L0_attn_C_param"],
        "logdt": i["L0_attn_log_dt"],
        "Wout": i["L0_attn_W_out"], "bout": i["L0_attn_b_out"],
        "gw": i["L0_ffn_gate_w"],
        "We1": i["L0_ffn_We1"], "We2": i["L0_ffn_We2"], "We3": i["L0_ffn_We3"],
        "be1": i["L0_ffn_be1"], "be2": i["L0_ffn_be2"], "be3": i["L0_ffn_be3"],
        "ws1": i["L0_ffn_ws1"], "ws2": i["L0_ffn_ws2"], "ws3": i["L0_ffn_ws3"],
        "bs1": i["L0_ffn_bs1"], "bs2": i["L0_ffn_bs2"], "bs3": i["L0_ffn_bs3"],
        "gamb": np.tile(i["L0_attn_ln_gamma"][None, :], (128, 1)),
        "bcolh": _bcolh(i), "dtch": i["L0_attn_log_dt"].reshape(8, 128).T.copy(),
        "betb": np.tile(i["L0_attn_ln_beta"][None, :], (128, 1)),
        "Dbc": np.tile(i["L0_attn_D_param"][None, :], (128, 1)),
        "gbb": np.tile(i["L0_ffn_gate_b"][None, :], (128, 1)),
    }
    base.update(cns)
    in_maps = []
    for k in range(8):
        b, h = k // 2, k % 2
        m = dict(base)
        m["xo"] = np.ascontiguousarray(x[b, h * TOK:(h + 1) * TOK, :])
        m["xg"] = np.ascontiguousarray(x[b, (1 - h) * TOK:(2 - h) * TOK, :])
        m["flagc"] = np.full((128, 1), float(h), np.float32)
        in_maps.append(m)
    return in_maps


def kernel(**inputs):
    i = {k: np.ascontiguousarray(np.asarray(vv, np.float32)) for k, vv in inputs.items()}
    if "nc" not in _NC_CACHE:
        _NC_CACHE["nc"] = build_nc(dbg=False)
    nc, _ = _NC_CACHE["nc"]
    in_maps = make_in_maps(i)
    res = run_bass_kernel_spmd(nc, in_maps, core_ids=list(range(8)))
    outp = np.empty((4, 4096, D), np.float32)
    for k in range(8):
        b, h = k // 2, k % 2
        outp[b, h * TOK:(h + 1) * TOK, :] = res.results[k]["out"]
    return outp
